# version 17
# baseline (speedup 1.0000x reference)
import math
from contextlib import ExitStack

import numpy as np
import ml_dtypes

import concourse.bass as bass
import concourse.mybir as mybir
from concourse.bass_utils import run_bass_kernel_spmd

F32 = mybir.dt.float32
BF16 = mybir.dt.bfloat16
AF = mybir.ActivationFunctionType
ALU = mybir.AluOpType
AX = mybir.AxisListType
bfloat16 = ml_dtypes.bfloat16

NCORES = 8
B, T, D = 4, 4096, 1024
NH, HD = 16, 64
SCALE = 0.125

ACTORS = ("tensor", "vector", "scalar", "gpsimd", "sync")

PHASE = [""]
STATS = None

_NAME_N = [0]


def _un(n):
    _NAME_N[0] += 1
    return "%s_u%d" % (n, _NAME_N[0])


class Tracker:

    NSLOT = 8

    def __init__(self, nc, ctx):
        self.nc = nc
        self.sems = {}
        self.counts = {}
        self.queues = {a: [] for a in ACTORS}
        for name in ACTORS:
            self.sems[name] = ctx.enter_context(nc.semaphore("trk_" + name))
            self.counts[name] = 0
        self.dma_sems = {}
        self.dma_counts = {}
        self.dma_rr = {}
        for name in ("sync", "gpsimd", "scalar"):
            self.dma_sems[name] = [ctx.enter_context(nc.semaphore("dma_%s%d" % (name, i)))
                                   for i in range(self.NSLOT)]
            self.dma_counts[name] = [0] * self.NSLOT
            self.dma_rr[name] = 0
        self.last_w = {}
        self.readers = {}
        self.fence_tokens = []
        self.seen_since_fence = set()
        self.waited = {a: {} for a in ACTORS}

    def fence(self):
        toks = {}
        for a, c in self.counts.items():
            if c > 0:
                toks[id(self.sems[a])] = (a, self.sems[a], c)
        for a, slots in self.dma_counts.items():
            for s, c in enumerate(slots):
                if c > 0:
                    toks[id(self.dma_sems[a][s])] = (a + "_dma", self.dma_sems[a][s], 16 * c)
        self.fence_tokens = list(toks.values())
        self.seen_since_fence = set()

    def _deps_for(self, reads, writes):
        deps = {}

        def add(tok):
            if tok is None:
                return
            a, sem, v = tok
            k = id(sem)
            cur = deps.get(k)
            if cur is None or v > cur[2]:
                deps[k] = (a, sem, v)

        for b_ in reads:
            add(self.last_w.get(b_))
        fresh = False
        for b_ in writes:
            if b_ not in self.seen_since_fence:
                fresh = True
            add(self.last_w.get(b_))
            for tok in self.readers.get(b_, ()):
                add(tok)
        if fresh:
            for tok in self.fence_tokens:
                add(tok)
        return list(deps.values())

    def emit(self, actor, fn, reads=(), writes=(), dma=False, trust_same=False,
             meta=None):
        deps = self._deps_for(reads, writes)
        waits = []
        for a, sem, v in deps:
            if a == actor and not dma and (
                    actor == "tensor" or trust_same):
                continue
            waits.append((sem, v))
        if STATS is not None:
            STATS.append((PHASE[0], actor, meta, len(waits), dma))
        if dma:
            s = self.dma_rr[actor]
            self.dma_rr[actor] = (s + 1) % self.NSLOT
            sem = self.dma_sems[actor][s]
            if self.dma_counts[actor][s] > 0:
                waits.append((sem, 16 * self.dma_counts[actor][s]))
            self.dma_counts[actor][s] += 1
            val = 16 * self.dma_counts[actor][s]
            inc = 16
        else:
            pass
        wd = self.waited[actor]
        fw = []
        for sem_, v_ in waits:
            if wd.get(id(sem_), 0) < v_:
                wd[id(sem_)] = v_
                fw.append((sem_, v_))
        waits = fw
        if dma:
            pass
        else:
            self.counts[actor] += 1
            sem = self.sems[actor]
            val = self.counts[actor]
            inc = 1

        def play(eng, waits=waits, fn=fn, sem=sem, inc=inc):
            for s_, c_ in waits:
                eng.wait_ge(s_, c_)
            fn(eng).then_inc(sem, inc)

        self.queues[actor].append(play)
        tok = (actor + "_dma" if dma else actor, sem, val)
        for b_ in writes:
            self.seen_since_fence.add(b_)
            self.last_w[b_] = tok
            self.readers[b_] = []
        for b_ in reads:
            self.readers.setdefault(b_, []).append(tok)
        return tok

    def reuse(self, newkey, *oldkeys):
        toks = []
        for ok in oldkeys:
            if ok in self.last_w:
                toks.append(self.last_w[ok])
            toks.extend(self.readers.get(ok, ()))
        self.readers.setdefault(newkey, []).extend(toks)

    def flush(self, block):
        final = []
        for a, c in self.counts.items():
            if c > 0:
                final.append((self.sems[a], c))
        for a, slots in self.dma_counts.items():
            for s, c in enumerate(slots):
                if c > 0:
                    final.append((self.dma_sems[a][s], 16 * c))
        for actor in ACTORS:
            q = self.queues[actor]
            if not q and actor != "sync":
                continue

            def body(eng, q=q, actor=actor):
                for f in q:
                    f(eng)
                if actor == "sync":
                    for s_, c_ in final:
                        eng.wait_ge(s_, c_)

            getattr(block, actor)(body)

    def mm(self, out_ap, lhsT, rhs, reads, writes, start=True, stop=True,
           tile_position=None):
        return self.emit(
            "tensor",
            lambda e: e.matmul(out_ap, lhsT, rhs, start=start, stop=stop,
                               tile_position=tile_position),
            reads=reads, writes=writes,
            meta=("mm", rhs.shape[-1], lhsT.shape[-1], str(getattr(rhs, "dtype", ""))))

    def trans(self, out_ap, in_ap, ident, reads, writes):
        return self.emit("tensor", lambda e: e.transpose(out_ap, in_ap, ident),
                         reads=reads, writes=writes,
                         meta=("trans", in_ap.shape[-1], in_ap.shape[0], ""))

    def act(self, out_ap, in_ap, func, reads, writes, bias=0.0, scale=1.0,
            accum_out=None):
        return self.emit(
            "scalar",
            lambda e: e.activation(out_ap, in_ap, func, bias=bias, scale=scale,
                                   accum_out=accum_out),
            reads=reads, writes=writes,
            meta=("act", out_ap.shape[-1], out_ap.shape[0], str(getattr(out_ap, "dtype", ""))))

    def dma(self, actor, out_ap, in_ap, reads, writes):
        import numpy as _np
        nb = 1
        for s in out_ap.shape:
            nb *= s
        return self.emit(actor, lambda e: e.dma_start(out=out_ap, in_=in_ap),
                         reads=reads, writes=writes, dma=True,
                         meta=("dma", nb, 0, str(getattr(out_ap, "dtype", ""))))

    def tt(self, out_ap, a, b_, op, reads, writes):
        return self.emit("vector",
                         lambda e: e.tensor_tensor(out=out_ap, in0=a, in1=b_, op=op),
                         reads=reads, writes=writes,
                         meta=("tt", out_ap.shape[-1], out_ap.shape[0], str(getattr(out_ap, "dtype", ""))))

    def ts(self, out_ap, a, s1, op0, reads, writes, s2=None, op1=None):
        kw = {"scalar2": s2}
        if op1 is not None:
            kw["op1"] = op1
        return self.emit(
            "vector",
            lambda e: e.tensor_scalar(out=out_ap, in0=a, scalar1=s1, op0=op0, **kw),
            reads=reads, writes=writes,
            meta=("ts", out_ap.shape[-1], out_ap.shape[0], str(getattr(out_ap, "dtype", ""))))

    def stt(self, out_ap, in0, scalar, in1, op0, op1, reads, writes):
        return self.emit(
            "vector",
            lambda e: e.scalar_tensor_tensor(out=out_ap, in0=in0, scalar=scalar,
                                             in1=in1, op0=op0, op1=op1),
            reads=reads, writes=writes,
            meta=("stt", out_ap.shape[-1], out_ap.shape[0], str(getattr(out_ap, "dtype", ""))))

    def copy(self, out_ap, in_ap, reads, writes):
        return self.emit("vector", lambda e: e.tensor_copy(out_ap, in_ap),
                         reads=reads, writes=writes,
                         meta=("copy", out_ap.shape[-1], out_ap.shape[0], str(getattr(out_ap, "dtype", ""))))

    def memset(self, ap, val, reads, writes, actor="vector"):
        return self.emit(actor, lambda e: e.memset(ap, val),
                         reads=reads, writes=writes,
                         meta=("memset", ap.shape[-1], ap.shape[0], str(getattr(ap, "dtype", ""))))

    def recip(self, out_ap, in_ap, reads, writes):
        return self.emit("vector", lambda e: e.reciprocal(out_ap, in_ap),
                         reads=reads, writes=writes,
                         meta=("recip", out_ap.shape[-1], out_ap.shape[0], str(getattr(out_ap, "dtype", ""))))

    def reduce(self, out_ap, in_ap, op, reads, writes, negate=None):
        return self.emit(
            "vector",
            lambda e: e.tensor_reduce(out_ap, in_ap, AX.X, op, negate=negate),
            reads=reads, writes=writes,
            meta=("reduce", in_ap.shape[-1], in_ap.shape[0], str(getattr(in_ap, "dtype", ""))))

    def affine_sel(self, out_ap, in_ap, pattern, compare_op, fill, base, cm,
                   reads, writes):
        return self.emit(
            "gpsimd",
            lambda e: e.affine_select(out_ap, in_ap, pattern=pattern,
                                      compare_op=compare_op, fill=fill, base=base,
                                      channel_multiplier=cm),
            reads=reads, writes=writes,
            meta=("asel", out_ap.shape[-1], out_ap.shape[0], str(getattr(out_ap, "dtype", ""))))


def build_identity(tk, ident_ap, key):
    tk.memset(ident_ap, 0.0, reads=(), writes=(key,), actor="gpsimd")
    tk.affine_sel(ident_ap, ident_ap, [[-1, ident_ap.shape[1]]], ALU.not_equal,
                  1.0, 0, 1, reads=(key,), writes=(key,))


def emit_ln(tk, nchunks, W, xsl, ones_cf, ones_row, sq0, sq1, rows,
            eps_ap, psS0, psS1, rb, mb, kx, ones_b, kb0="b0", kb1="b1"):
    psA = psS0[0:1, 0:W]
    psB = psS1[0:1, 0:W]
    for kc in range(nchunks):
        xap = xsl(kc)
        tk.mm(psA, ones_b, xap, start=(kc == 0), stop=(kc == nchunks - 1),
              reads=(kx, "ones_b"), writes=(kb0,))
        sq = (sq0, sq1)[kc % 2][:, 0:W]
        ksq = "sq" + str(kc % 2)
        tk.act(sq, xap, AF.Square, reads=(kx,), writes=(ksq,))
        tk.mm(psB, ones_cf, sq, start=(kc == 0), stop=(kc == nchunks - 1),
              reads=(ksq, "ones_cf"), writes=(kb1,))
    mu = rows[:, 0 * W:1 * W]
    ex2 = rows[:, 1 * W:2 * W]
    var = rows[:, 2 * W:3 * W]
    rstd = rows[:, 3 * W:4 * W]
    mur = rows[:, 4 * W:5 * W]
    n = nchunks * 128
    tk.ts(mu, psA, 1.0 / n, ALU.mult, reads=(kb0,), writes=("mu",))
    tk.ts(ex2, psB, 1.0 / n, ALU.mult, reads=(kb1,), writes=("ex2",))
    tk.tt(var, mu, mu, ALU.mult, reads=("mu",), writes=("var",))
    tk.tt(var, ex2, var, ALU.subtract, reads=("ex2", "var"), writes=("var",))
    tk.act(var, var, AF.Ln, bias=eps_ap, reads=("var", "eps"), writes=("var",))
    tk.act(rstd, var, AF.Exp, scale=-0.5, reads=("var",), writes=("rstd",))
    tk.stt(mur, mu, -1.0, rstd, ALU.mult, ALU.mult,
           reads=("mu", "rstd"), writes=("mur",))
    tk.mm(psS0[:, 0:W], ones_row, rstd, reads=("rstd", "ones_row"), writes=(kb0,))
    tk.mm(psS1[:, 0:W], ones_row, mur, reads=("mur", "ones_row"), writes=(kb1,))
    tk.copy(rb[:, 0:W], psS0[:, 0:W], reads=(kb0,), writes=("rb",))
    tk.copy(mb[:, 0:W], psS1[:, 0:W], reads=(kb1,), writes=("mb",))
    for kc in range(nchunks):
        xap = xsl(kc)
        tk.tt(xap, xap, rb[:, 0:W], ALU.mult, reads=(kx, "rb"), writes=(kx,))
        tk.tt(xap, xap, mb[:, 0:W], ALU.add, reads=(kx, "mb"), writes=(kx,))



def emit_l0_attn(nc, tk, xT, wqk, wk2, bqk, bk2, epsc, y_out,
                 dbg=None):
    debug = dbg is not None
    if debug:
        dbg_q, dbg_k, dbg_ka, dbg_xn = dbg

    NT = T // 512
    CL = T


    with ExitStack() as ctx:

        def sbL(name, shape, dt=BF16):
            return ctx.enter_context(nc.sbuf_tensor(_un(name), shape, dt, side="left"))

        def psum(name, shape, dt=F32):
            return ctx.enter_context(nc.psum_tensor(_un(name), shape, dt))

        q_sb = sbL("q_sb", [128, 4 * T])
        k_sb = sbL("k_sb", [128, 4 * T])
        ka_sb = sbL("ka_sb", [128, 4 * T])
        xn_lo = sbL("xn_lo", [128, 4 * CL + 128])
        bqk_sb = sbL("bqk_sb", [128, 8], F32)
        bk2s_sb = sbL("bk2s_sb", [128, 4], F32)
        ident = sbL("ident", [128, 128])
        identf = sbL("identf", [128, 128], F32)
        ones_b = sbL("ones_b", [128, 1])
        ones_cf = sbL("ones_cf", [128, 1], F32)
        ones_row = sbL("ones_row", [1, 128], F32)
        eps_sb = sbL("eps_sb", [1, 1], F32)

        psS0 = psum("psS0", [128, 512])
        psS1 = psum("psS1", [128, 512])
        psYa = psum("psYa", [65, 512])
        psYb = psum("psYb", [65, 512])
        psYM = psum("psYM", [64, 512])
        psMa = psum("psMa", [128, 128])
        psT = psum("psT", [128, 128], BF16)
        psState = psum("psState", [128, 512])

        if True:
            tk.dma("sync", bqk_sb[:], bqk, reads=(), writes=("bqk",))
            tk.dma("sync", bk2s_sb[:], bk2, reads=(), writes=("bk2s",))
            tk.dma("sync", eps_sb[:], epsc, reads=(), writes=("eps",))
            tk.ts(bk2s_sb[:], bk2s_sb[:], SCALE * 0.02, ALU.mult,
                  reads=("bk2s",), writes=("bk2s",))
            build_identity(tk, ident[:], "id")
            build_identity(tk, identf[:], "idf32")
            tk.memset(ones_b[:], 1.0, reads=(), writes=("ones_b",))
            tk.memset(ones_cf[:], 1.0, reads=(), writes=("ones_cf",))
            tk.memset(ones_row[:], 1.0, reads=(), writes=("ones_row",))

            with ExitStack() as phase12:
                def sbR(name, shape, dt=BF16):
                    return phase12.enter_context(
                        nc.sbuf_tensor(_un(name), shape, dt, side="right"))

                wqk_sb = sbR("wqk_sb", [128, 8 * 1024])
                wk2_sb = sbR("wk2_sb", [128, 8 * 512])
                xn_hi = sbR("xn_hi", [128, 4 * CL])
                rowsc = sbR("rowsc", [1, 5 * 512], F32)
                sq0 = sbR("sq0", [128, 512], F32)
                sq1 = sbR("sq1", [128, 512], F32)
                rb_b = sbR("rb_b", [128, 512])
                mb_b = sbR("mb_b", [128, 512])

                tk.dma("gpsimd", wqk_sb[:].rearrange("p (c n) -> p c n", c=8),
                       wqk.rearrange("c p n -> p c n"),
                       reads=(), writes=("wqk",))
                tk.dma("gpsimd", wk2_sb[:].rearrange("p (c n) -> p c n", c=8),
                       wk2.rearrange("c p n -> p c n"),
                       reads=(), writes=("wk2",))

                def xn_sl(kc, c0, c1):
                    if kc < 4:
                        return xn_lo[:, kc * CL + c0: kc * CL + c1]
                    return xn_hi[:, (kc - 4) * CL + c0: (kc - 4) * CL + c1]

                for kc in range(8):
                    tk.dma("gpsimd", xn_sl(kc, 0, T),
                           xT[kc * 128:(kc + 1) * 128, :], reads=(), writes=("xn",))
                tk.memset(xn_lo[:, 4 * CL:], 0.0, reads=(), writes=("xn",))
                for t in range(NT):
                    c0, c1 = t * 512, (t + 1) * 512
                    emit_ln(tk, 8, 512, lambda kc: xn_sl(kc, c0, c1),
                            ones_cf[:], ones_row[:], sq0[:], sq1[:], rowsc[:],
                            eps_sb[0:1, 0:1], psS0, psS1, rb_b[:], mb_b[:],
                            "xn", ones_b[:])

                projs = (
                    (q_sb, "qo", wqk_sb, "wqk", 1024, bqk_sb, "bqk", None, 1.0, 0),
                    (k_sb, "ko", wqk_sb, "wqk", 1024, bqk_sb, "bqk", None, 1.0, 4),
                    (ka_sb, "kao", wk2_sb, "wk2", 512, bk2s_sb, "bk2s", AF.Sigmoid,
                     SCALE * 0.02, 0),
                )
                pcount = 0
                for dst, dkey, wsb, wkey, wstride, bias_sb, bias_key, func, scl, boc in projs:
                    for oc in range(4):
                        oc_g = boc + oc
                        for t in range(NT):
                            c0, c1 = t * 512, (t + 1) * 512
                            ps = (psS0, psS1)[pcount % 2]
                            bank = "b" + str(pcount % 2)
                            pcount += 1
                            for kc in range(8):
                                lhsT = wsb[:, kc * wstride + oc_g * 128:
                                           kc * wstride + oc_g * 128 + 128]
                                tk.mm(ps[:, :], lhsT, xn_sl(kc, c0, c1),
                                      start=(kc == 0), stop=(kc == 7),
                                      reads=(wkey, "xn"), writes=(bank,))
                            if func is None:
                                tk.ts(dst[:, oc * T + c0: oc * T + c1], ps[:, :],
                                      bias_sb[:, oc_g:oc_g + 1], ALU.add,
                                      reads=(bank, bias_key), writes=(dkey,))
                            else:
                                tk.act(dst[:, oc * T + c0: oc * T + c1], ps[:, :], func,
                                       bias=bias_sb[:, oc_g:oc_g + 1], scale=scl,
                                       reads=(bank, bias_key), writes=(dkey,))

            with ExitStack() as phase34:
                def sbR2(name, shape, dt=BF16):
                    return phase34.enter_context(
                        nc.sbuf_tensor(_un(name), shape, dt, side="right"))

                v_ext = sbR2("v_ext", [128, 32 * 520])
                maskS = sbR2("maskS", [128, 4 * 512])
                maskM = sbR2("maskM", [128, 128], F32)
                S_sb = sbR2("S_sb", [128, 8 * 64])
                Sf_sb = sbR2("Sf_sb", [128, 8 * 64], F32)
                qa8 = sbR2("qa8", [128, 4 * 512])
                qa_t = sbR2("qa_t", [128, 512])
                etm8 = sbR2("etm8", [128, 8 * 64])
                katm8 = sbR2("katm8", [128, 4 * 128])
                vtm8 = sbR2("vtm8", [128, 4 * 128])
                mascr8 = sbR2("mascr8", [128, 8 * 128])
                ya8 = sbR2("ya8", [64, 8 * 512])
                ymj8 = sbR2("ymj8", [64, 8 * 128])
                rb64 = sbR2("rb64", [64, 512], F32)
                rd_sb = sbR2("rd_sb", [1, 512], F32)
                pt_sb = sbR2("pt_sb", [128, 4 * 512])
                names = ["v_ext", "maskS", "maskM", "qa_t", "rb64", "rd",
                         "pt0", "pt1", "pt2", "pt3"]
                for h in range(8):
                    names += ["S_sb%d" % h, "Sf%d" % h, "qa%d" % (h // 2), "etm%d" % h,
                              "katm%d" % (h // 2), "vtm%d" % (h // 2),
                              "mascr%d" % h, "ya%d" % h, "ymj%d" % h]
                for nm in names:
                    tk.reuse(nm, "wqk", "wk2", "xn", "sq0", "sq1", "rb", "mb",
                             "mu", "ex2", "var", "rstd", "mur")
                tk.memset(maskS[:], 0.0, reads=(), writes=("maskS",), actor="gpsimd")
                for r in range(4):
                    tk.affine_sel(maskS[:, r * 512:(r + 1) * 512],
                                  maskS[:, r * 512:(r + 1) * 512],
                                  [[1, 512]], ALU.is_ge, -1.0e5, -128 * r, -1,
                                  reads=("maskS",), writes=("maskS",))
                tk.memset(maskM[:], 1.0, reads=(), writes=("maskM",), actor="gpsimd")
                tk.affine_sel(maskM[:], maskM[:], [[1, 128]], ALU.is_ge, 0.0, -1, -1,
                              reads=("maskM",), writes=("maskM",))

                for tb in range(32):
                    base = tb * 520
                    tk.memset(v_ext[:, base + 64: base + 520: 65], 1.0,
                              reads=(), writes=("v_ext",))
                    for dc in range(4):
                        vsrc = xn_lo[:, dc * CL + tb * 128: dc * CL + (tb + 1) * 128]
                        if dc % 2 == 0:
                            psx, bankx = psT, "bT"
                            tk.trans(psx[:, :], vsrc, ident[:],
                                     reads=("xn", "id"), writes=(bankx,))
                        else:
                            psx, bankx = psMa, "bMa"
                            tk.mm(psx[:, :], vsrc, ident[:],
                                  reads=("xn", "id"), writes=(bankx,))
                        tk.copy(v_ext[:, base + (2 * dc) * 65: base + (2 * dc) * 65 + 64],
                                psx[:, 0:64], reads=(bankx,), writes=("v_ext",))
                        tk.copy(v_ext[:, base + (2 * dc + 1) * 65:
                                      base + (2 * dc + 1) * 65 + 64],
                                psx[:, 64:128], reads=(bankx,), writes=("v_ext",))

                if debug:
                    tk.dma("sync", dbg_q, q_sb[:], reads=("qo",), writes=())
                    tk.dma("sync", dbg_k, k_sb[:], reads=("ko",), writes=())
                    tk.dma("sync", dbg_ka, ka_sb[:], reads=("kao",), writes=())
                    tk.dma("sync", dbg_xn, xn_lo[:], reads=("xn",), writes=())
                for qt in range(NT):
                    q0, q1 = qt * 512, (qt + 1) * 512
                    nkb = 4 * qt + 4
                    pend = [None]
                    pend_n = [None]

                    def flushpv():
                        if pend[0] is not None:
                            pend[0]()
                            pend[0] = None

                    def flushnorm():
                        if pend_n[0] is not None:
                            pend_n[0]()
                            pend_n[0] = None

                    for h in range(8):
                        r0 = (h % 2) * 64
                        cb = (h // 2) * T
                        qk = "qa%d" % (h // 2)
                        psY, yk_b = ((psYa, "bYa"), (psYb, "bYb"))[h % 2]
                        if h % 2 == 0:
                            tk.ts(qa_t[:], q_sb[:, cb + q0:cb + q1], SCALE * 0.02,
                                  ALU.mult, reads=("qo",), writes=("qa_t",))
                            tk.stt(qa8[:, (h // 2) * 512:(h // 2 + 1) * 512],
                                   q_sb[:, cb + q0:cb + q1], SCALE,
                                   qa_t[:], ALU.mult, ALU.min,
                                   reads=("qo", "qa_t"), writes=(qk,))
                        for kb in range(nkb):
                            ps = (psS0, psS1)[kb % 2]
                            bank = "b" + str(kb % 2)
                            slot = (h % 2) * 2 + kb % 2
                            ptsl = pt_sb[:, slot * 512:(slot + 1) * 512]
                            ptk = "pt%d" % slot
                            r = kb - (nkb - 4)
                            tk.mm(ps[:, :],
                                  k_sb[r0:r0 + 64, cb + kb * 128:cb + (kb + 1) * 128],
                                  q_sb[r0:r0 + 64, cb + q0:cb + q1],
                                  stop=(r < 0),
                                  reads=("ko", "qo"), writes=(bank,))
                            if r >= 0:
                                tk.mm(ps[:, :], ident[:],
                                      maskS[:, r * 512:(r + 1) * 512],
                                      start=False, stop=True,
                                      reads=("id", "maskS"), writes=(bank,))
                            tk.act(ptsl, ps[:, :], AF.Exp, scale=SCALE,
                                   reads=(bank,), writes=(ptk,))
                            flushpv()
                            if kb == 1:
                                flushnorm()

                            def pv(h=h, kb=kb, ptsl=ptsl, ptk=ptk, nkb=nkb,
                                   psY=psY, yk_b=yk_b):
                                tk.mm(psY[:, :],
                                      v_ext[:, kb * 520 + h * 65: kb * 520 + h * 65 + 65],
                                      ptsl, start=(kb == 0), stop=(kb == nkb - 1),
                                      reads=("v_ext", ptk), writes=(yk_b,))
                            pend[0] = pv
                        flushpv()
                        tk.copy(rd_sb[:], psY[64:65, :], reads=(yk_b,), writes=("rd",))

                        def norm(h=h, psY=psY, yk_b=yk_b):
                            tk.mm(psYM[0:64, :], ones_row[0:1, 0:64], rd_sb[:],
                                  reads=("rd", "ones_row"), writes=("bYM",))
                            tk.recip(rb64[:], psYM[0:64, :],
                                     reads=("bYM",), writes=("rb64",))
                            tk.stt(ya8[:, h * 512:(h + 1) * 512], psY[0:64, :], 1.0,
                                   rb64[:], ALU.mult, ALU.mult,
                                   reads=(yk_b, "rb64"), writes=("ya%d" % h,))
                        pend_n[0] = norm
                    flushnorm()
                    for j in range(4):
                        i = 4 * qt + j
                        for pr in range(4):
                            sh0 = pr * CL + i * 128 + 1
                            psx, bankx = ((psMa, "bMa"), (psT, "bT"))[pr % 2]
                            if pr % 2 == 0:
                                tk.mm(psx[:, :], xn_lo[:, sh0: sh0 + 128], ident[:],
                                      reads=("xn", "id"), writes=(bankx,))
                            else:
                                tk.trans(psx[:, :], xn_lo[:, sh0: sh0 + 128],
                                         ident[:], reads=("xn", "id"), writes=(bankx,))
                            tk.copy(vtm8[:, pr * 128:(pr + 1) * 128], psx[:, 0:128],
                                    reads=(bankx,), writes=("vtm%d" % pr,))
                            if i < 31:
                                ksrc = ka_sb[:, pr * T + i * 128:pr * T + (i + 1) * 128]
                                if pr % 2 == 0:
                                    tk.mm(psx[:, :], ksrc, ident[:],
                                          reads=("kao", "id"), writes=(bankx,))
                                else:
                                    tk.trans(psx[:, :], ksrc, ident[:],
                                             reads=("kao", "id"), writes=(bankx,))
                                tk.copy(katm8[:, pr * 128:(pr + 1) * 128],
                                        psx[:, 0:128],
                                        reads=(bankx,), writes=("katm%d" % pr,))
                        for h in range(8):
                            r0 = (h % 2) * 64
                            pr = h // 2
                            qk = "qa%d" % pr
                            vk, kk = "vtm%d" % pr, "katm%d" % pr
                            ek, mk = "etm%d" % h, "mascr%d" % h
                            sk, yk, jk = "S_sb%d" % h, "ya%d" % h, "ymj%d" % h
                            vsl = vtm8[:, pr * 128 + r0: pr * 128 + r0 + 64]
                            ksl = katm8[:, pr * 128 + r0: pr * 128 + r0 + 64]
                            esl = etm8[:, h * 64:(h + 1) * 64]
                            msl = mascr8[:, h * 128:(h + 1) * 128]
                            tk.trans(psT[:, 0:64], ya8[:, h * 512 + j * 128:
                                                       h * 512 + (j + 1) * 128],
                                     ident[0:64, 0:64], reads=(yk, "id"), writes=("bT",))
                            tk.tt(esl, vsl, psT[:, 0:64], ALU.subtract,
                                  reads=(vk, "bT"), writes=(ek,))
                            tk.mm(psMa[:, :],
                                  ka_sb[r0:r0 + 64, pr * T + i * 128:
                                        pr * T + (i + 1) * 128],
                                  qa8[r0:r0 + 64, pr * 512 + j * 128:
                                      pr * 512 + (j + 1) * 128],
                                  reads=("kao", qk), writes=("bMa",))
                            tk.stt(msl, psMa[:, :], 1.0, maskM[:], ALU.mult,
                                   ALU.mult, reads=("bMa", "maskM"), writes=(mk,))
                            have_state = i > 0
                            if have_state:
                                tk.mm(psYM[0:64, 0:128], S_sb[r0:r0 + 64, h * 64:(h + 1) * 64],
                                      qa8[r0:r0 + 64, pr * 512 + j * 128:
                                          pr * 512 + (j + 1) * 128],
                                      start=True, stop=False,
                                      reads=(sk, qk), writes=("bYM",))
                            tk.mm(psYM[0:64, 0:128], esl, msl,
                                  start=(not have_state), stop=True,
                                  reads=(ek, mk), writes=("bYM",))
                            tk.copy(ymj8[:, h * 128:(h + 1) * 128], psYM[0:64, 0:128],
                                    reads=("bYM",), writes=(jk,))
                            tk.tt(ya8[:, h * 512 + j * 128: h * 512 + (j + 1) * 128],
                                  ya8[:, h * 512 + j * 128: h * 512 + (j + 1) * 128],
                                  ymj8[:, h * 128:(h + 1) * 128], ALU.add,
                                  reads=(yk, jk), writes=(yk,))
                            if i < 31:
                                sfk = "Sf%d" % h
                                if have_state:
                                    tk.mm(psState[r0:r0 + 64, h * 64:(h + 1) * 64],
                                          identf[r0:r0 + 64, r0:r0 + 64],
                                          Sf_sb[r0:r0 + 64, h * 64:(h + 1) * 64],
                                          start=True, stop=False,
                                          tile_position=(r0, r0),
                                          reads=("idf32", sfk), writes=("bSt",))
                                tk.mm(psState[r0:r0 + 64, h * 64:(h + 1) * 64],
                                      ksl, esl, start=(not have_state), stop=True,
                                      tile_position=(0, r0),
                                      reads=(kk, ek), writes=("bSt",))
                                tk.copy(Sf_sb[r0:r0 + 64, h * 64:(h + 1) * 64],
                                        psState[r0:r0 + 64, h * 64:(h + 1) * 64],
                                        reads=("bSt",), writes=(sfk,))
                                tk.copy(S_sb[r0:r0 + 64, h * 64:(h + 1) * 64],
                                        Sf_sb[r0:r0 + 64, h * 64:(h + 1) * 64],
                                        reads=(sfk,), writes=(sk,))
                    for h in range(8):
                        tk.dma("sync", y_out[h * 64:(h + 1) * 64, q0:q1],
                               ya8[:, h * 512:(h + 1) * 512],
                               reads=("ya%d" % h,), writes=())

    return nc





def build_l0_attn(debug=False):
    nc = bass.Bass("TRN2", target_bir_lowering=False)
    xT = nc.dram_tensor("xT", [D, T], F32, kind="ExternalInput").ap()
    wqk = nc.dram_tensor("wqk", [8, 128, 1024], F32, kind="ExternalInput").ap()
    wk2 = nc.dram_tensor("wk2", [8, 128, 512], F32, kind="ExternalInput").ap()
    bqk = nc.dram_tensor("bqk", [128, 8], F32, kind="ExternalInput").ap()
    bk2 = nc.dram_tensor("bk2", [128, 4], F32, kind="ExternalInput").ap()
    epsc = nc.dram_tensor("epsc", [1, 1], F32, kind="ExternalInput").ap()
    y_out = nc.dram_tensor("y", [512, T], BF16, kind="ExternalOutput").ap()
    dbg = None
    if debug:
        dbg = (nc.dram_tensor("dbg_q", [128, 4 * T], BF16, kind="ExternalOutput").ap(),
               nc.dram_tensor("dbg_k", [128, 4 * T], BF16, kind="ExternalOutput").ap(),
               nc.dram_tensor("dbg_ka", [128, 4 * T], BF16, kind="ExternalOutput").ap(),
               nc.dram_tensor("dbg_xn", [128, 4 * T + 128], BF16, kind="ExternalOutput").ap())
    with ExitStack() as ctx0:
        tk = Tracker(nc, ctx0)
        with nc.Block() as block:
            emit_l0_attn(nc, tk, xT, wqk, wk2, bqk, bk2, epsc, y_out, dbg=dbg)
            tk.flush(block)
    return nc

def emit_l0_ffn(nc, tk, xT, yb, wproj, bproj, w1, w2, h1, h2, w3,
                s1, s2, sh1, sh2, s3, epsc, x1_out, dbg=None, yload=None,
                dbg_wpx=None):
    debug = dbg is not None
    if debug:
        dbg_attn, dbg_ffn, dbg_wp, dbg_yt, dbg_raw = dbg
    TO = 2048

    NBT = 2
    BT = 1024


    with ExitStack() as ctx:

        def sb(name, shape, dt=BF16):
            return ctx.enter_context(nc.sbuf_tensor(_un(name), shape, dt))

        def psum(name, shape, dt=F32):
            return ctx.enter_context(nc.psum_tensor(_un(name), shape, dt))

        m_sb = sb("m_sb", [128, 32 * BT])
        g_sb = sb("g_sb", [128, 32 * BT])
        xn_t = sb("xn_t", [128, 8 * BT])
        wproj_sb = sb("wproj_sb", [128, 8 * 1024])
        wA = sb("wA", [128, 32 * 128])
        wB = sb("wB", [128, 32 * 128])
        s1_sb = sb("s1_sb", [128, 32], F32)
        s2_sb = sb("s2_sb", [128, 32], F32)
        sh1_sb = sb("sh1_sb", [128, 32], F32)
        sh2_sb = sb("sh2_sb", [128, 32], F32)
        s3_sb = sb("s3_sb", [128, 8], F32)
        bproj_sb = sb("bproj_sb", [128, 8], F32)
        ones_b = sb("ones_b", [128, 1])
        ones_cf = sb("ones_cf", [128, 1], F32)
        ones_row = sb("ones_row", [1, 128], F32)
        eps_sb = sb("eps_sb", [1, 1], F32)
        rowsc = sb("rowsc", [1, 5 * 512], F32)
        sq0 = sb("sq0", [128, 512], F32)
        sq1 = sb("sq1", [128, 512], F32)
        rb_b = sb("rb_b", [128, 512])
        mb_b = sb("mb_b", [128, 512])
        sl_scr = sb("sl_scr", [128, 2 * 512])
        b2_scr = sb("b2_scr", [128, 2 * 512])
        fo_scr = sb("fo_scr", [128, 512], F32)
        po_scr = sb("po_scr", [128, 512], F32)
        xf_scr = sb("xf_scr", [128, 512], F32)
        x1_scr = sb("x1_scr", [128, 512], F32)

        psb = [psum("psb%d" % i, [128, 512]) for i in range(4)]

        if True:
            tk.dma("sync", s1_sb[:], s1, reads=(), writes=("s1",))
            tk.dma("sync", s2_sb[:], s2, reads=(), writes=("s2",))
            tk.dma("sync", sh1_sb[:], sh1, reads=(), writes=("sh1",))
            tk.dma("sync", sh2_sb[:], sh2, reads=(), writes=("sh2",))
            tk.dma("sync", s3_sb[:], s3, reads=(), writes=("s3",))
            tk.dma("sync", bproj_sb[:], bproj, reads=(), writes=("bp",))
            tk.dma("sync", eps_sb[:], epsc, reads=(), writes=("eps",))
            tk.ts(bproj_sb[:], bproj_sb[:], 2.0, ALU.mult, reads=("bp",), writes=("bp",))
            tk.memset(ones_b[:], 1.0, reads=(), writes=("ones_b",))
            tk.memset(ones_cf[:], 1.0, reads=(), writes=("ones_cf",))
            tk.memset(ones_row[:], 1.0, reads=(), writes=("ones_row",))
            tk.dma("gpsimd", wproj_sb[:].rearrange("p (c n) -> p c n", c=8),
                   wproj.rearrange("c p n -> p c n"), reads=(), writes=("wproj",))
            if dbg_wpx is not None:
                tk.dma("sync", dbg_wpx[0], wproj_sb[:], reads=("wproj",), writes=())

            wslots = [(wA, "wA"), (wB, "wB")]
            y_t = xn_t

            for bt in range(NBT):
                t0 = bt * BT
                tk.reuse("xn", "yt")
                for kc in range(8):
                    tk.dma("gpsimd", xn_t[:, kc * BT:(kc + 1) * BT],
                           xT[kc * 128:(kc + 1) * 128, t0:t0 + BT],
                           reads=(), writes=("xn",))
                for st in range(2):
                    c0 = st * 512
                    emit_ln(tk, 8, 512,
                            lambda kc: xn_t[:, kc * BT + c0: kc * BT + c0 + 512],
                            ones_cf[:], ones_row[:], sq0[:], sq1[:], rowsc[:],
                            eps_sb[0:1, 0:1], psb[0], psb[1], rb_b[:], mb_b[:],
                            "xn", ones_b[:], kb0="pb0", kb1="pb1")
                for hc in range(32):
                    sa, ka_ = wslots[0]
                    sb_, kb_ = wslots[1]
                    tk.dma("gpsimd", sa[:, 0:8 * 128], w1[hc], reads=(), writes=(ka_,))
                    tk.dma("gpsimd", sb_[:, 0:8 * 128], w2[hc], reads=(), writes=(kb_,))
                    for st in range(2):
                        c0 = st * 512
                        pA = psb[2 * st]
                        pB = psb[2 * st + 1]
                        bkA, bkB = "pb%d" % (2 * st), "pb%d" % (2 * st + 1)
                        for kc in range(8):
                            tk.mm(pA[:, :], sa[:, kc * 128:(kc + 1) * 128],
                                  xn_t[:, kc * BT + c0: kc * BT + c0 + 512],
                                  start=(kc == 0), stop=(kc == 7),
                                  reads=(ka_, "xn"), writes=(bkA,))
                        for kc in range(8):
                            tk.mm(pB[:, :], sb_[:, kc * 128:(kc + 1) * 128],
                                  xn_t[:, kc * BT + c0: kc * BT + c0 + 512],
                                  start=(kc == 0), stop=(kc == 7),
                                  reads=(kb_, "xn"), writes=(bkB,))
                        tk.act(sl_scr[:, c0:c0 + 512], pA[:, :], AF.Silu,
                               scale=s1_sb[:, hc:hc + 1],
                               reads=(bkA, "s1"), writes=("sl",))
                        tk.ts(b2_scr[:, c0:c0 + 512], pB[:, :], s2_sb[:, hc:hc + 1],
                              ALU.mult, reads=(bkB, "s2"), writes=("b2",))
                        tk.tt(m_sb[:, hc * BT + c0: hc * BT + c0 + 512],
                              sl_scr[:, c0:c0 + 512], b2_scr[:, c0:c0 + 512],
                              ALU.mult, reads=("sl", "b2"), writes=("m",))

                if dbg_wpx is not None and bt == 0:
                    tk.dma("sync", dbg_wpx[1], wproj_sb[:], reads=("wproj", "m"), writes=())
                for oc in range(32):
                    sa, ka_ = wslots[oc % 2]
                    tk.dma("gpsimd", sa[:], h1[oc], reads=(), writes=(ka_,))
                    for st in range(2):
                        c0 = st * 512
                        pA = psb[(oc + st) % 4]
                        bkA = "pb%d" % ((oc + st) % 4)
                        for kc in range(32):
                            tk.mm(pA[:, :], sa[:, kc * 128:(kc + 1) * 128],
                                  m_sb[:, kc * BT + c0: kc * BT + c0 + 512],
                                  start=(kc == 0), stop=(kc == 31),
                                  reads=(ka_, "m"), writes=(bkA,))
                        tk.act(g_sb[:, oc * BT + c0: oc * BT + c0 + 512], pA[:, :],
                               AF.Gelu, scale=sh1_sb[:, oc:oc + 1],
                               reads=(bkA, "sh1"), writes=("g",))

                if dbg_wpx is not None and bt == 0:
                    tk.dma("sync", dbg_wpx[2], wproj_sb[:], reads=("wproj", "g"), writes=())
                for oc in range(32):
                    sa, ka_ = wslots[oc % 2]
                    tk.dma("gpsimd", sa[:], h2[oc], reads=(), writes=(ka_,))
                    for st in range(2):
                        c0 = st * 512
                        pA = psb[(oc + st) % 4]
                        bkA = "pb%d" % ((oc + st) % 4)
                        for kc in range(32):
                            tk.mm(pA[:, :], sa[:, kc * 128:(kc + 1) * 128],
                                  g_sb[:, kc * BT + c0: kc * BT + c0 + 512],
                                  start=(kc == 0), stop=(kc == 31),
                                  reads=(ka_, "g"), writes=(bkA,))
                        tk.act(m_sb[:, oc * BT + c0: oc * BT + c0 + 512], pA[:, :],
                               AF.Gelu, scale=sh2_sb[:, oc:oc + 1],
                               reads=(bkA, "sh2"), writes=("m",))

                tk.reuse("yt", "xn")
                for kc in range(8):
                    if yload is not None:
                        yload(tk, y_t[:, kc * BT:(kc + 1) * BT], kc, t0, BT)
                    else:
                        tk.dma("sync", y_t[:, kc * BT:(kc + 1) * BT],
                               yb[kc * 128:(kc + 1) * 128, t0:t0 + BT],
                               reads=(), writes=("yt",))
                for oc in range(8):
                    sa, ka_ = wslots[oc % 2]
                    tk.dma("gpsimd", sa[:], w3[oc], reads=(), writes=(ka_,))
                    for st in range(2):
                        c0 = st * 512
                        pA = psb[(2 * oc + st) % 3]
                        bkA = "pb%d" % ((2 * oc + st) % 3)
                        pB = psb[3]
                        bkB = "pb3"
                        for kc in range(32):
                            tk.mm(pA[:, :], sa[:, kc * 128:(kc + 1) * 128],
                                  m_sb[:, kc * BT + c0: kc * BT + c0 + 512],
                                  start=(kc == 0), stop=(kc == 31),
                                  reads=(ka_, "m"), writes=(bkA,))
                        for kc in range(8):
                            tk.mm(pB[:, :], wproj_sb[:, kc * 1024 + oc * 128:
                                                     kc * 1024 + oc * 128 + 128],
                                  y_t[:, kc * BT + c0: kc * BT + c0 + 512],
                                  start=(kc == 0), stop=(kc == 7),
                                  reads=("wproj", "yt"), writes=(bkB,))
                        tk.ts(fo_scr[:], pA[:, :], s3_sb[:, oc:oc + 1], ALU.mult,
                              reads=(bkA, "s3"), writes=("fo",))
                        if debug and bt == 0 and oc == 0 and st == 0:
                            tk.copy(x1_scr[:], pB[:, :], reads=(bkB,), writes=("x1s",))
                            tk.dma("sync", dbg_raw, x1_scr[:], reads=("x1s",), writes=())
                            tk.dma("sync", dbg_wp, wproj_sb[:], reads=("wproj",), writes=())
                            tk.dma("sync", dbg_yt, y_t[:], reads=("yt",), writes=())
                        tk.ts(po_scr[:], pB[:, :], bproj_sb[:, oc:oc + 1], ALU.add,
                              reads=(bkB, "bp"), writes=("po",))
                        tk.dma("sync", xf_scr[:, 0:512],
                               xT[oc * 128:(oc + 1) * 128, t0 + c0: t0 + c0 + 512],
                               reads=(), writes=("xf",))
                        if debug:
                            tk.dma("sync", dbg_attn[oc * 128:(oc + 1) * 128,
                                                    t0 + c0: t0 + c0 + 512],
                                   po_scr[:], reads=("po",), writes=())
                            tk.dma("sync", dbg_ffn[oc * 128:(oc + 1) * 128,
                                                   t0 + c0: t0 + c0 + 512],
                                   fo_scr[:], reads=("fo",), writes=())
                        tk.tt(fo_scr[:], fo_scr[:], po_scr[:], ALU.add,
                              reads=("fo", "po"), writes=("fo",))
                        tk.tt(x1_scr[:], fo_scr[:], xf_scr[:, 0:512], ALU.add,
                              reads=("fo", "xf"), writes=("x1s",))
                        tk.dma("sync", x1_out[oc * 128:(oc + 1) * 128, t0 + c0: t0 + c0 + 512],
                               x1_scr[:], reads=("x1s",), writes=())

    return nc




def build_l0_ffn(debug=False):
    nc = bass.Bass("TRN2", target_bir_lowering=False)
    TO = 2048
    xT = nc.dram_tensor("xT", [D, TO], F32, kind="ExternalInput").ap()
    yb = nc.dram_tensor("yb", [D, TO], BF16, kind="ExternalInput").ap()
    wproj = nc.dram_tensor("wproj", [8, 128, 1024], F32, kind="ExternalInput").ap()
    bproj = nc.dram_tensor("bproj", [128, 8], F32, kind="ExternalInput").ap()
    w1 = nc.dram_tensor("w1", [32, 128, 8, 128], F32, kind="ExternalInput").ap()
    w2 = nc.dram_tensor("w2", [32, 128, 8, 128], F32, kind="ExternalInput").ap()
    h1 = nc.dram_tensor("h1", [32, 128, 32, 128], F32, kind="ExternalInput").ap()
    h2 = nc.dram_tensor("h2", [32, 128, 32, 128], F32, kind="ExternalInput").ap()
    w3 = nc.dram_tensor("w3", [8, 128, 32, 128], F32, kind="ExternalInput").ap()
    s1 = nc.dram_tensor("s1", [128, 32], F32, kind="ExternalInput").ap()
    s2 = nc.dram_tensor("s2", [128, 32], F32, kind="ExternalInput").ap()
    sh1 = nc.dram_tensor("sh1", [128, 32], F32, kind="ExternalInput").ap()
    sh2 = nc.dram_tensor("sh2", [128, 32], F32, kind="ExternalInput").ap()
    s3 = nc.dram_tensor("s3", [128, 8], F32, kind="ExternalInput").ap()
    epsc = nc.dram_tensor("epsc", [1, 1], F32, kind="ExternalInput").ap()
    x1_out = nc.dram_tensor("x1", [D, TO], F32, kind="ExternalOutput").ap()
    dbg = None
    if debug:
        dbg = (nc.dram_tensor("dbg_attn", [D, TO], F32, kind="ExternalOutput").ap(),
               nc.dram_tensor("dbg_ffn", [D, TO], F32, kind="ExternalOutput").ap(),
               nc.dram_tensor("dbg_wp", [128, 8192], BF16, kind="ExternalOutput").ap(),
               nc.dram_tensor("dbg_yt", [128, 8192], BF16, kind="ExternalOutput").ap(),
               nc.dram_tensor("dbg_raw", [128, 512], F32, kind="ExternalOutput").ap())
    with ExitStack() as ctx0:
        tk = Tracker(nc, ctx0)
        with nc.Block() as block:
            PHASE[0] = "l0f"
            emit_l0_ffn(nc, tk, xT, yb, wproj, bproj, w1, w2, h1, h2, w3,
                        s1, s2, sh1, sh2, s3, epsc, x1_out, dbg=dbg)
            tk.flush(block)
    return nc

def stage_l0_ffn(inputs, y_by_core):
    x = inputs["L0_attn_x"]
    wproj_h = _f32c(inputs["L0_attn_w_proj"].T.reshape(8, 128, 1024))
    bproj_h = _f32c(inputs["L0_attn_b_proj"].reshape(8, 128).T)
    w1_h = _f32c(inputs["L0_ffn_w1_w"].T.reshape(8, 128, 32, 128).transpose(2, 1, 0, 3))
    w2_h = _f32c(inputs["L0_ffn_w2_w"].T.reshape(8, 128, 32, 128).transpose(2, 1, 0, 3))
    h1_h = _f32c(inputs["L0_ffn_h1_w"].T.reshape(32, 128, 32, 128).transpose(2, 1, 0, 3))
    h2_h = _f32c(inputs["L0_ffn_h2_w"].T.reshape(32, 128, 32, 128).transpose(2, 1, 0, 3))
    w3_h = _f32c(inputs["L0_ffn_w3_w"].T.reshape(32, 128, 8, 128).transpose(2, 1, 0, 3))
    s1_h = _f32c(inputs["L0_ffn_w1_s"].reshape(32, 128).T)
    s2_h = _f32c(inputs["L0_ffn_w2_s"].reshape(32, 128).T)
    sh1_h = _f32c(inputs["L0_ffn_h1_s"].reshape(32, 128).T)
    sh2_h = _f32c(inputs["L0_ffn_h2_s"].reshape(32, 128).T)
    s3_h = _f32c(inputs["L0_ffn_w3_s"].reshape(8, 128).T)
    in_maps = []
    for c in range(NCORES):
        b, half = c // 2, c % 2
        sl = slice(half * 2048, half * 2048 + 2048)
        y_full = np.concatenate([y_by_core[2 * b], y_by_core[2 * b + 1]], 0)
        in_maps.append({
            "xT": _f32c(x[b].T[:, sl]),
            "yb": np.ascontiguousarray(y_full[:, sl]),
            "wproj": wproj_h, "bproj": bproj_h,
            "w1": w1_h, "w2": w2_h, "h1": h1_h, "h2": h2_h, "w3": w3_h,
            "s1": s1_h, "s2": s2_h, "sh1": sh1_h, "sh2": sh2_h, "s3": s3_h,
            "epsc": np.full((1, 1), 1e-5, np.float32),
        })
    return in_maps


def run_l0_ffn(inputs, y_by_core, core_ids=None):
    nc = build_l0_ffn()
    in_maps = stage_l0_ffn(inputs, y_by_core)
    ids = core_ids if core_ids is not None else list(range(NCORES))
    res = run_bass_kernel_spmd(nc, [in_maps[i] for i in ids], core_ids=ids)
    return [r["x1"] for r in res.results]



LN16 = math.log(16.0)


def emit_l1_attn(nc, tk, x1e, ck, cb, rw, rb, pw, pb, omg, lmt, sint, cost,
                 rotm, epsc, qr_out, kvp_out, dbg=None):
    debug = dbg is not None
    if debug:
        dbg_xn, dbg_h, dbg_ft, dbg_q = dbg
    TO = 2048
    TE = TO + 2

    G2 = [math.sqrt(2 * g) for g in (0.5, 1.0, 2.0)]
    SF = math.sqrt(2.0 / 256.0)
    CDK = 1.5 * 2 ** 23


    with ExitStack() as ctx:

        def sb(name, shape, dt=BF16):
            return ctx.enter_context(nc.sbuf_tensor(_un(name), shape, dt))

        def psum(name, shape, dt=F32):
            return ctx.enter_context(nc.psum_tensor(_un(name), shape, dt))

        q_sb = sb("q_sb", [128, 8 * TO])
        k_sb = sb("k_sb", [128, 8 * TO])
        v_sb = sb("v_sb", [128, 8 * TO])
        wsh = sb("wsh", [128, 6144])
        cb_sb = sb("cb_sb", [128, 24], F32)
        rb_sb = sb("rb_sb", [128, 18], F32)
        pb_sb = sb("pb_sb", [128, 24], F32)
        omg_sb = sb("omg_sb", [128, 256])
        lmt_sb = sb("lmt_sb", [128, 64])
        ones_b = sb("ones_b", [128, 1])
        ones_cf = sb("ones_cf", [128, 1], F32)
        ones_row = sb("ones_row", [1, 128], F32)
        eps_sb = sb("eps_sb", [1, 1], F32)
        idf = sb("idf", [1, 1], F32)
        hpi_col = sb("hpi_col", [128, 1], F32)
        ident_b = sb("ident_b", [128, 128])
        rowsc = sb("rowsc", [1, 5 * 512], F32)
        sq0 = sb("sq0", [128, 512], F32)
        sq1 = sb("sq1", [128, 512], F32)
        rb_b = sb("rb_b", [128, 512])
        mb_b = sb("mb_b", [128, 512])
        u_scr = sb("u_scr", [128, 2 * 512], F32)
        t_scr = sb("t_scr", [128, 2 * 512], F32)
        rot_scr = sb("rot_scr", [128, 2 * 512])
        t1_scr = sb("t1_scr", [128, 2 * 512])
        ll_col = sb("ll_col", [128, 1], F32)
        llr = sb("llr", [1, 64], F32)
        kvs_scr = sb("kvs_scr", [128, 130], F32)

        psA = psum("psA", [128, 512])
        psB = psum("psB", [128, 512])
        psC = psum("psC", [128, 512])
        psD = psum("psD", [128, 512])
        psT = psum("psT", [128, 128], BF16)
        psTf = psum("psTf", [128, 128])
        psKV = psum("psKV", [128, 130])
        psKP = psum("psKP", [128, 256])

        if True:
            for ap_, sbuf_, key in ((cb, cb_sb, "cb"), (rb, rb_sb, "rbv"),
                                    (pb, pb_sb, "pbv"), (epsc, eps_sb, "eps")):
                tk.dma("sync", sbuf_[:], ap_, reads=(), writes=(key,))
            tk.dma("gpsimd", omg_sb[:], omg, reads=(), writes=("omg",))
            tk.dma("gpsimd", lmt_sb[:], lmt, reads=(), writes=("lmt",))
            tk.memset(ones_b[:], 1.0, reads=(), writes=("ones_b",))
            tk.memset(ones_cf[:], 1.0, reads=(), writes=("ones_cf",))
            tk.memset(ones_row[:], 1.0, reads=(), writes=("ones_row",))
            tk.memset(idf[:], 1.0, reads=(), writes=("idf",))
            tk.memset(hpi_col[:], math.pi / 2, reads=(), writes=("hpi",))
            build_identity(tk, ident_b[:], "idb")

            with ExitStack() as scopeA:
                def sbA(name, shape, dt=BF16):
                    return scopeA.enter_context(nc.sbuf_tensor(_un(name), shape, dt))

                TT2 = 1024
                TE2 = TT2 + 2
                h_sb = sbA("h_sb", [128, 8 * TT2])
                ft_sb = sbA("ft_sb", [128, 6 * TT2])
                xn1 = sbA("xn1", [128, 8 * TE2])

                for bt in range(2):
                    t0 = bt * TT2
                    tk.reuse("xn", "hv", "ft")
                    for kc in range(8):
                        tk.dma("gpsimd", xn1[:, kc * TE2:(kc + 1) * TE2],
                               x1e[kc * 128:(kc + 1) * 128, t0:t0 + TE2],
                               reads=(), writes=("xn",))
                    for t in range(2):
                        c0 = 1 + t * 512
                        emit_ln(tk, 8, 512,
                                lambda kc: xn1[:, kc * TE2 + c0: kc * TE2 + c0 + 512],
                                ones_cf[:], ones_row[:], sq0[:], sq1[:], rowsc[:],
                                eps_sb[0:1, 0:1], psA, psB, rb_b[:], mb_b[:],
                                "xn", ones_b[:], kb0="pA", kb1="pB")

                    def halo_sl(kc):
                        base = kc * TE2
                        return xn1[:, base: base + TE2: TE2 - 1]
                    emit_ln(tk, 8, 2, halo_sl, ones_cf[:], ones_row[:], sq0[:], sq1[:],
                            rowsc[:], eps_sb[0:1, 0:1], psA, psB, rb_b[:], mb_b[:],
                            "xn", ones_b[:], kb0="pA", kb1="pB")

                    for i, (dst, dkey) in enumerate(((q_sb, "qv"), (k_sb, "kv_"),
                                                     (v_sb, "vv"))):
                        tk.dma("gpsimd",
                               wsh[:].rearrange("p (w g c n) -> p w g c n",
                                                w=3, g=4, c=2),
                               ck[i].rearrange("w g c p n -> p w g c n"),
                               reads=(), writes=("wsh",))
                        for oc in range(8):
                            g = oc // 2
                            oc2 = oc % 2
                            for t in range(2):
                                c0 = t * 512
                                ps = (psA, psB)[t % 2]
                                bank = ("pA", "pB")[t % 2]
                                first = True
                                for w in range(3):
                                    for c2 in range(2):
                                        kc = 2 * g + c2
                                        lhsT = wsh[:, ((w * 4 + g) * 2 + c2) * 256 +
                                                   oc2 * 128:
                                                   ((w * 4 + g) * 2 + c2) * 256 +
                                                   oc2 * 128 + 128]
                                        tk.mm(ps[:, :], lhsT,
                                              xn1[:, kc * TE2 + w + c0:
                                                  kc * TE2 + w + c0 + 512],
                                              start=first, stop=(w == 2 and c2 == 1),
                                              reads=("wsh", "xn"), writes=(bank,))
                                        first = False
                                tk.ts(h_sb[:, oc * TT2 + c0: oc * TT2 + c0 + 512],
                                      ps[:, :],
                                      cb_sb[:, i * 8 + oc: i * 8 + oc + 1], ALU.add,
                                      reads=(bank, "cb"), writes=("hv",))
                        if debug and bt == 0 and i == 0:
                            tk.dma("sync", dbg_xn, xn1[:], reads=("xn",), writes=())
                            tk.dma("sync", dbg_h, h_sb[:], reads=("hv",), writes=())
                        for g in range(3):
                            tk.dma("gpsimd",
                                   wsh[:, 0:2048].rearrange("p (c n) -> p c n", c=8),
                                   rw[i, g], reads=(), writes=("wsh",))
                            for mc in range(2):
                                for t in range(2):
                                    c0 = t * 512
                                    ps = (psC, psD)[t % 2]
                                    bank = ("pC", "pD")[t % 2]
                                    for kc in range(8):
                                        tk.mm(ps[:, :],
                                              wsh[:, kc * 256 + mc * 128:
                                                  kc * 256 + mc * 128 + 128],
                                              h_sb[:, kc * TT2 + c0:
                                                   kc * TT2 + c0 + 512],
                                              start=(kc == 0), stop=(kc == 7),
                                              reads=("wsh", "hv"), writes=(bank,))
                                    gm = g * 2 + mc
                                    sl = (mc * 2 + t) % 2
                                    usl = u_scr[:, sl * 512:(sl + 1) * 512]
                                    tsl = t_scr[:, sl * 512:(sl + 1) * 512]
                                    uk, tkk = "u%d" % sl, "t%d" % sl
                                    tk.ts(usl, ps[:, :], G2[g], ALU.mult,
                                          s2=rb_sb[:, i * 6 + gm: i * 6 + gm + 1],
                                          op1=ALU.add, reads=(bank, "rbv"),
                                          writes=(uk,))
                                    tk.ts(tsl, usl, 1.0 / (2 * math.pi),
                                          ALU.mult, s2=CDK, op1=ALU.add,
                                          reads=(uk,), writes=(tkk,))
                                    tk.ts(tsl, tsl, CDK, ALU.subtract,
                                          reads=(tkk,), writes=(tkk,))
                                    tk.stt(tsl, tsl, -2 * math.pi, usl,
                                           ALU.mult, ALU.add, reads=(tkk, uk),
                                           writes=(tkk,))
                                    tk.act(tsl, tsl, AF.Abs, reads=(tkk,),
                                           writes=(tkk,))
                                    tk.act(ft_sb[:, gm * TT2 + c0:
                                               gm * TT2 + c0 + 512],
                                           tsl, AF.Sin, scale=-1.0,
                                           bias=hpi_col[:, 0:1],
                                           reads=(tkk, "hpi"), writes=("ft",))
                        if debug and bt == 0 and i == 0:
                            tk.dma("sync", dbg_ft, ft_sb[:], reads=("ft",), writes=())
                        tk.dma("gpsimd",
                               wsh[:].rearrange("p (c n) -> p c n", c=6),
                               pw[i], reads=(), writes=("wsh",))
                        for oc in range(8):
                            for t in range(2):
                                c0 = t * 512
                                ps = (psA, psB)[t % 2]
                                bank = ("pA", "pB")[t % 2]
                                for kc in range(6):
                                    tk.mm(ps[:, :],
                                          wsh[:, kc * 1024 + oc * 128:
                                              kc * 1024 + oc * 128 + 128],
                                          ft_sb[:, kc * TT2 + c0: kc * TT2 + c0 + 512],
                                          start=(kc == 0), stop=(kc == 5),
                                          reads=("wsh", "ft"), writes=(bank,))
                                tk.ts(dst[:, oc * TO + t0 + c0:
                                          oc * TO + t0 + c0 + 512], ps[:, :],
                                      SF, ALU.mult,
                                      s2=pb_sb[:, i * 8 + oc: i * 8 + oc + 1],
                                      op1=ALU.add, reads=(bank, "pbv"), writes=(dkey,))

            if debug:
                tk.dma("sync", dbg_q, q_sb[:], reads=("qv",), writes=())
            sin_sb = ctx.enter_context(nc.sbuf_tensor(_un("sin_sb"), [128, TO], F32))
            cos_sb = ctx.enter_context(nc.sbuf_tensor(_un("cos_sb"), [128, TO], BF16))
            rot_sb = ctx.enter_context(nc.sbuf_tensor(_un("rot_sb"), [128, 128], BF16))
            v_ext = ctx.enter_context(nc.sbuf_tensor(_un("v_ext"), [128, 16 * 1040], BF16))
            kn_sb = ctx.enter_context(nc.sbuf_tensor(_un("kn_sb"), [128, 2048], BF16))
            kkr = ctx.enter_context(nc.sbuf_tensor(_un("kkr"), [1, 2048], mybir.dt.float32))
            knn_r = ctx.enter_context(nc.sbuf_tensor(_un("knn_r"), [1, 2048], mybir.dt.float32))
            ekr = ctx.enter_context(nc.sbuf_tensor(_un("ekr"), [1, 512], mybir.dt.float32))
            ekb = ctx.enter_context(nc.sbuf_tensor(_un("ekb"), [128, 512], BF16))
            bcol = ctx.enter_context(nc.sbuf_tensor(_un("bcol"), [128, 16], mybir.dt.float32))
            kp_scr = ctx.enter_context(nc.sbuf_tensor(_un("kp_scr"), [128, 2 * 256], BF16))
            for nm in ("sin", "cos", "rot", "vx", "kn", "kk", "knn", "ekr", "ekb",
                       "bc", "kp"):
                tk.reuse(nm, "xn", "hv", "ft", "wsh")
            tk.dma("gpsimd", sin_sb[:], sint, reads=(), writes=("sin",))
            tk.dma("gpsimd", cos_sb[:], cost, reads=(), writes=("cos",))
            tk.dma("gpsimd", rot_sb[:], rotm, reads=(), writes=("rot",))

            for di, (dst, dkey) in enumerate(((q_sb, "qv"), (k_sb, "kv_"))):
                for kc in range(8):
                    for t in range(4):
                        c0 = t * 512
                        sl = t % 4
                        ps, bank = ((psA, "pA"), (psB, "pB"),
                                    (psC, "pC"), (psD, "pD"))[sl]
                        sl2 = t % 2
                        rsl = rot_scr[:, sl2 * 512:(sl2 + 1) * 512]
                        t1sl = t1_scr[:, sl2 * 512:(sl2 + 1) * 512]
                        rk, t1k = "rsc%d" % sl2, "t1%d" % sl2
                        tk.mm(ps[:, :], rot_sb[:],
                              dst[:, kc * TO + c0: kc * TO + c0 + 512],
                              reads=("rot", dkey), writes=(bank,))
                        tk.tt(rsl, ps[:, :], sin_sb[:, c0:c0 + 512], ALU.mult,
                              reads=(bank, "sin"), writes=(rk,))
                        tk.tt(t1sl, dst[:, kc * TO + c0: kc * TO + c0 + 512],
                              cos_sb[:, c0:c0 + 512], ALU.mult,
                              reads=(dkey, "cos"), writes=(t1k,))
                        tk.tt(dst[:, kc * TO + c0: kc * TO + c0 + 512],
                              t1sl, rsl, ALU.add,
                              reads=(t1k, rk), writes=(dkey,))
            tk.dma("sync", qr_out.rearrange("(c p) n -> p c n", p=128),
                   q_sb[:].rearrange("p (c n) -> p c n", c=8),
                   reads=("qv",), writes=())

            for tb in range(16):
                base = tb * 1040
                tk.memset(v_ext[:, base + 64: base + 1040: 65], 1.0,
                          reads=(), writes=("vx",))
                for dc in range(8):
                    vsrc = v_sb[:, dc * TO + tb * 128: dc * TO + (tb + 1) * 128]
                    if dc % 2 == 0:
                        psx, bankx = psT, "pT"
                        tk.trans(psx[:, :], vsrc, ident_b[:],
                                 reads=("vv", "idb"), writes=(bankx,))
                    else:
                        psx, bankx = psTf, "pTf"
                        tk.mm(psx[:, :], vsrc, ident_b[:],
                              reads=("vv", "idb"), writes=(bankx,))
                    tk.copy(v_ext[:, base + (2 * dc) * 65: base + (2 * dc) * 65 + 64],
                            psx[:, 0:64], reads=(bankx,), writes=("vx",))
                    tk.copy(v_ext[:, base + (2 * dc + 1) * 65:
                                  base + (2 * dc + 1) * 65 + 64],
                            psx[:, 64:128], reads=(bankx,), writes=("vx",))

            tk.act(sq0[0:64, 0:64], lmt_sb[0:64, :], AF.Square,
                   reads=("lmt",), writes=("sq0",))
            tk.mm(psA[0:1, 0:64], ones_cf[0:64, 0:1], sq0[0:64, 0:64],
                  reads=("sq0", "ones_cf"), writes=("pA",))
            tk.ts(llr[:], psA[0:1, 0:64], -1.0 / 64.0, ALU.mult,
                  reads=("pA",), writes=("llr",))
            tk.trans(psTf[0:64, 0:1], llr[:], idf[:], reads=("llr", "idf"), writes=("pTf",))
            tk.copy(ll_col[0:64, 0:1], psTf[0:64, 0:1], reads=("pTf",), writes=("llc",))

            for h in range(16):
                r0 = (h % 2) * 64
                cbase = (h // 2) * TO
                ksl = lambda c0, c1: k_sb[r0:r0 + 64, cbase + c0: cbase + c1]
                for t in range(4):
                    c0 = t * 512
                    sqx, sqk = ((sq0, "sq0"), (sq1, "sq1"))[t % 2]
                    psx, bkx = ((psB, "pB"), (psC, "pC"))[t % 2]
                    tk.act(sqx[r0:r0 + 64, :], ksl(c0, c0 + 512), AF.Square,
                           reads=("kv_",), writes=(sqk,))
                    tk.mm(psx[0:1, :], ones_cf[r0:r0 + 64, 0:1], sqx[r0:r0 + 64, :],
                          reads=(sqk, "ones_cf"), writes=(bkx,))
                    tk.copy(kkr[:, c0:c0 + 512], psx[0:1, :], reads=(bkx,), writes=("kk",))
                for t in range(4):
                    c0 = t * 512
                    tk.mm(psA[0:64, :], lmt_sb[r0:r0 + 64, :], ksl(c0, c0 + 512),
                          reads=("lmt", "kv_"), writes=("pA",))
                    tk.act(t1_scr[0:64, 0:512], psA[0:64, :], AF.Exp, scale=2.0 / 64.0,
                           bias=ll_col[0:64, 0:1], reads=("pA", "llc"), writes=("t1",))
                    tk.act(ekr[:], kkr[:, c0:c0 + 512], AF.Exp, scale=-1.0 / 64.0,
                           reads=("kk",), writes=("ekr",))
                    tk.mm(psB[0:64, :], ones_row[0:1, 0:64], ekr[:],
                          reads=("ekr", "ones_row"), writes=("pB",))
                    tk.copy(ekb[0:64, :], psB[0:64, :], reads=("pB",), writes=("ekb",))
                    tk.tt(kn_sb[0:64, c0:c0 + 512], t1_scr[0:64, 0:512], ekb[0:64, :],
                          ALU.mult, reads=("t1", "ekb"), writes=("kn",))
                for t in range(4):
                    c0 = t * 512
                    tk.act(sq0[0:64, :], kn_sb[0:64, c0:c0 + 512], AF.Square,
                           reads=("kn",), writes=("sq0",))
                    tk.mm(psB[0:1, :], ones_cf[0:64, 0:1], sq0[0:64, :],
                          reads=("sq0", "ones_cf"), writes=("pB",))
                    tk.ts(knn_r[:, c0:c0 + 512], psB[0:1, :], -0.5, ALU.mult,
                          s2=-LN16, op1=ALU.add, reads=("pB",), writes=("knn",))
                for tb in range(16):
                    psx, bankx = ((psTf, "pTf"), (psC, "pC"))[tb % 2]
                    tk.trans(psx[:, 0:1], knn_r[:, tb * 128:(tb + 1) * 128], idf[:],
                             reads=("knn", "idf"), writes=(bankx,))
                    tk.copy(bcol[:, tb:tb + 1], psx[:, 0:1], reads=(bankx,), writes=("bc",))
                for tb in range(16):
                    sl = tb % 2
                    kpsl = kp_scr[:, sl * 256:(sl + 1) * 256]
                    kpk = "kp%d" % sl
                    tk.mm(psKP[:, :], kn_sb[0:64, tb * 128:(tb + 1) * 128],
                          omg_sb[0:64, :], reads=("kn", "omg"), writes=("pKP",))
                    tk.act(kpsl, psKP[:, :], AF.Exp, bias=bcol[:, tb:tb + 1],
                           reads=("pKP", "bc"), writes=(kpk,))
                    for mc in range(2):
                        tk.mm(psKV[:, mc * 65:(mc + 1) * 65],
                              kp_scr[:, sl * 256 + mc * 128:
                                     sl * 256 + (mc + 1) * 128],
                              v_ext[:, tb * 1040 + h * 65: tb * 1040 + h * 65 + 65],
                              start=(tb == 0), stop=(tb == 15),
                              reads=(kpk, "vx"), writes=("pKV",))
                tk.copy(kvs_scr[:], psKV[:, :], reads=("pKV",), writes=("kvs",))
                tk.dma("sync", kvp_out[:, h * 130:(h + 1) * 130], kvs_scr[:],
                       reads=("kvs",), writes=())

    return nc




def build_l1_attn(debug=False):
    nc = bass.Bass("TRN2", target_bir_lowering=False)
    TO = 2048
    TE = TO + 2
    x1e = nc.dram_tensor("x1e", [D, TE], F32, kind="ExternalInput").ap()
    ck = nc.dram_tensor("ck", [3, 3, 4, 2, 128, 256], F32, kind="ExternalInput").ap()
    cb = nc.dram_tensor("cb", [128, 24], F32, kind="ExternalInput").ap()
    rw = nc.dram_tensor("rw", [3, 3, 128, 8, 256], F32, kind="ExternalInput").ap()
    rb = nc.dram_tensor("rb", [128, 18], F32, kind="ExternalInput").ap()
    pw = nc.dram_tensor("pw", [3, 128, 6, 1024], F32, kind="ExternalInput").ap()
    pb = nc.dram_tensor("pb", [128, 24], F32, kind="ExternalInput").ap()
    omg = nc.dram_tensor("omg", [128, 256], F32, kind="ExternalInput").ap()
    lmt = nc.dram_tensor("lmt", [128, 64], F32, kind="ExternalInput").ap()
    sint = nc.dram_tensor("sint", [128, TO], F32, kind="ExternalInput").ap()
    cost = nc.dram_tensor("cost", [128, TO], F32, kind="ExternalInput").ap()
    rotm = nc.dram_tensor("rotm", [128, 128], F32, kind="ExternalInput").ap()
    epsc = nc.dram_tensor("epsc", [1, 1], F32, kind="ExternalInput").ap()
    qr_out = nc.dram_tensor("qr", [D, TO], BF16, kind="ExternalOutput").ap()
    kvp_out = nc.dram_tensor("kvp", [128, 16 * 130], F32, kind="ExternalOutput").ap()
    dbg = None
    if debug:
        dbg = (nc.dram_tensor("dbg_xn", [128, 8 * 1026], BF16, kind="ExternalOutput").ap(),
               nc.dram_tensor("dbg_h", [128, 8 * 1024], BF16, kind="ExternalOutput").ap(),
               nc.dram_tensor("dbg_ft", [128, 6 * 1024], BF16, kind="ExternalOutput").ap(),
               nc.dram_tensor("dbg_q", [128, 8 * 2048], BF16, kind="ExternalOutput").ap())
    with ExitStack() as ctx0:
        tk = Tracker(nc, ctx0)
        with nc.Block() as block:
            PHASE[0] = "l1a"
            emit_l1_attn(nc, tk, x1e, ck, cb, rw, rb, pw, pb, omg, lmt, sint, cost,
                         rotm, epsc, qr_out, kvp_out, dbg=dbg)
            tk.flush(block)
    return nc

def _rope_tables(S=4096, dh=64):
    inv = 1.0 / (10000.0 ** (np.arange(0, dh, 2, dtype=np.float64) / dh))
    f = np.arange(S, dtype=np.float64)[:, None] * inv[None, :]
    emb = np.concatenate([f, f], axis=1)
    return np.sin(emb).astype(np.float32), np.cos(emb).astype(np.float32)


def _rot_matrix():
    R = np.zeros((64, 64), np.float32)
    R[np.arange(32), np.arange(32) + 32] = -1.0
    R[np.arange(32, 64), np.arange(0, 32)] = 1.0
    lhsT = R.T
    out = np.zeros((128, 128), np.float32)
    out[:64, :64] = lhsT
    out[64:, 64:] = lhsT
    return out


def stage_l1_attn(inputs, x1_by_core):
    ckf = inputs["L1_attn_conv_k"]
    ck_h = _f32c(np.stack([ckf[i].reshape(3, 2, 128, 4, 256).transpose(0, 3, 1, 2, 4)
                           for i in range(3)]))
    cb_h = _f32c(inputs["L1_attn_conv_b"].reshape(3, 8, 128).transpose(2, 0, 1)
                 .reshape(128, 24))
    rw_h = _f32c(inputs["L1_attn_rff_w"].reshape(3, 3, 8, 128, 256)
                 .transpose(0, 1, 3, 2, 4))
    rb_h = _f32c(inputs["L1_attn_rff_b"].reshape(3, 3, 2, 128)
                 .transpose(3, 0, 1, 2).reshape(128, 18))
    pw_h = _f32c(inputs["L1_attn_proj_w"].reshape(3, 6, 128, 1024)
                 .transpose(0, 2, 1, 3))
    pb_h = _f32c(inputs["L1_attn_proj_b"].reshape(3, 8, 128).transpose(2, 0, 1)
                 .reshape(128, 24))
    omg_h = _f32c(np.concatenate([inputs["L1_attn_omega"]] * 2, 0))
    lmt_h = _f32c(np.concatenate([inputs["L1_attn_landmarks"].T] * 2, 0))
    sin_t, cos_t = _rope_tables()
    rot_h = _rot_matrix()
    in_maps = []
    for c in range(NCORES):
        b, half = c // 2, c % 2
        if c not in x1_by_core:
            in_maps.append(None)
            continue
        own = x1_by_core[c]
        left = (np.zeros((D, 1), np.float32) if half == 0
                else x1_by_core[2 * b][:, 2047:2048])
        right = (x1_by_core[2 * b + 1][:, 0:1] if half == 0
                 else np.zeros((D, 1), np.float32))
        x1e = _f32c(np.concatenate([left, own, right], 1))
        sl = slice(half * 2048, half * 2048 + 2048)
        sin_h = _f32c(np.concatenate([sin_t.T[:, sl]] * 2, 0))
        cos_h = _f32c(np.concatenate([cos_t.T[:, sl]] * 2, 0))
        in_maps.append({
            "x1e": x1e, "ck": ck_h, "cb": cb_h, "rw": rw_h, "rb": rb_h,
            "pw": pw_h, "pb": pb_h, "omg": omg_h, "lmt": lmt_h,
            "sint": sin_h, "cost": cos_h, "rotm": rot_h,
            "epsc": np.full((1, 1), 1e-5, np.float32),
        })
    return in_maps


def run_l1_attn(inputs, x1_by_core, core_ids=None):
    nc = build_l1_attn()
    in_maps = stage_l1_attn(inputs, x1_by_core)
    ids = core_ids if core_ids is not None else list(range(NCORES))
    res = run_bass_kernel_spmd(nc, [in_maps[i] for i in ids], core_ids=ids)
    return [(r["qr"], r["kvp"]) for r in res.results]



def emit_l1_moe(nc, tk, x1T, qr, kvp2, omg, outw, outb, nw, gw, w1, b1,
                w2, b2, xo, dbg=None, kvload=None):
    debug = dbg is not None
    if debug:
        dbg_c, dbg_g, dbg_t, dbg_gs, dbg_gw = dbg
    TO = 2048

    LN16C = math.log(16.0)


    with ExitStack() as ctx:

        def sb(name, shape, dt=BF16):
            return ctx.enter_context(nc.sbuf_tensor(_un(name), shape, dt))

        def psum(name, shape, dt=F32):
            return ctx.enter_context(nc.psum_tensor(_un(name), shape, dt))

        t_sb = sb("t_sb", [128, 8 * TO])
        attn1 = sb("attn1", [128, 8 * TO])
        outw_sb = sb("outw_sb", [128, 8 * 1024])
        omg_sb = sb("omg_sb", [128, 256])
        outb_sb = sb("outb_sb", [128, 8], F32)
        nw_sb = sb("nw_sb", [128, 8], F32)
        gw_sb = sb("gw_sb", [128, 8 * 8], F32)
        b1_sb = sb("b1_sb", [128, 128], F32)
        b2_sb = sb("b2_sb", [8, 1024], F32)
        c_fm = sb("c_fm", [8, TO], F32)
        muL = sb("muL", [1, TO])
        rL = sb("rL", [1, TO])
        ones_rowb = sb("ones_rowb", [1, 128])
        rows4 = sb("rows4", [1, 4 * 512], F32)
        ones_b = sb("ones_b", [128, 1])
        ones_cf = sb("ones_cf", [128, 1], F32)
        ones_row = sb("ones_row", [1, 128], F32)
        idf128 = sb("idf128", [128, 128], F32)
        nln16 = sb("nln16", [1, 1], F32)
        sq_s = sb("sq_s", [128, 512], F32)
        row_s = sb("row_s", [1, 512], F32)
        eqqb = sb("eqqb", [128, 512])
        qp_scr = sb("qp_scr", [128, 512])
        nm_scr = sb("nm_scr", [128, 512], F32)
        rb_scr = sb("rb_scr", [128, 512], F32)
        gt_scr = sb("gt_scr", [128, 8 * 4], F32)
        m1c = sb("m1c", [128, 1], F32)
        m2c = sb("m2c", [128, 1], F32)
        w1c = sb("w1c", [128, 1], F32)
        w2c = sb("w2c", [128, 1], F32)
        e1s = sb("e1s", [128, 8], F32)
        e2s = sb("e2s", [128, 8], F32)
        t2s = sb("t2s", [128, 8], F32)
        cb_scr = sb("cb_scr", [128, 512])
        gate_s = sb("gate_s", [8, 512], F32)
        sel_sb = sb("sel_sb", [8, 1024], F32)

        ps0 = psum("ps0", [128, 512])
        ps1 = psum("ps1", [128, 512])
        ps2 = psum("ps2", [128, 512])
        ps3 = psum("ps3", [128, 512])
        psG = psum("psG", [8, 512])
        psTf = psum("psTf", [128, 128])
        psN = psum("psN", [128, 512])
        psD = psum("psD", [1, 512])

        if True:
            tk.dma("sync", outb_sb[:], outb, reads=(), writes=("outb",))
            tk.dma("sync", nw_sb[:], nw, reads=(), writes=("nw",))
            tk.dma("sync", gw_sb[:].rearrange("p (c n) -> p c n", c=8),
                   gw.rearrange("c p n -> p c n"), reads=(), writes=("gw",))
            tk.dma("sync", b1_sb[:], b1, reads=(), writes=("b1",))
            tk.dma("sync", b2_sb[:], b2, reads=(), writes=("b2",))
            tk.dma("gpsimd", omg_sb[:], omg, reads=(), writes=("omg",))
            tk.dma("gpsimd", outw_sb[:].rearrange("p (c n) -> p c n", c=8),
                   outw.rearrange("c p n -> p c n"), reads=(), writes=("outw",))
            tk.memset(ones_b[:], 1.0, reads=(), writes=("ones_b",))
            tk.memset(ones_cf[:], 1.0, reads=(), writes=("ones_cf",))
            tk.memset(ones_row[:], 1.0, reads=(), writes=("ones_row",))
            tk.memset(nln16[:], -LN16C, reads=(), writes=("nln16",))
            tk.memset(ones_rowb[:], 1.0, reads=(), writes=("ones_rowb",))
            tk.memset(idf128[:], 0.0, reads=(), writes=("idf",), actor="gpsimd")
            tk.affine_sel(idf128[:], idf128[:], [[-1, 128]], ALU.not_equal, 1.0, 0, 1,
                          reads=("idf",), writes=("idf",))
            tk.memset(sel_sb[:], 1.0, reads=(), writes=("sel",), actor="gpsimd")
            tk.affine_sel(sel_sb[:], sel_sb[:], [[1, 1024]], ALU.is_ge, 0.0, 0, -128,
                          reads=("sel",), writes=("sel",))
            tk.affine_sel(sel_sb[:], sel_sb[:], [[-1, 1024]], ALU.is_ge, 0.0, 127, 128,
                          reads=("sel",), writes=("sel",))

            with ExitStack() as scopeA:
                def sbA(name, shape, dt=BF16):
                    return scopeA.enter_context(nc.sbuf_tensor(_un(name), shape, dt))

                qr_sb = sbA("qr_sb", [128, 8 * TO])
                kvf = sbA("kvf", [128, 2080], F32)
                kvb = sbA("kvb", [128, 2080])
                x1f = sbA("x1f", [128, 8 * 512], F32)
                xnf = sbA("xnf", [128, 8 * 512], F32)
                tf = xnf

                tk.dma("gpsimd", qr_sb[:].rearrange("p (c n) -> p c n", c=8),
                       qr.rearrange("(c p) n -> p c n", p=128),
                       reads=(), writes=("qr",))
                if kvload is not None:
                    kvload(tk, kvf)
                else:
                    tk.dma("gpsimd", kvf[:], kvp2[:, 0:2080], reads=(), writes=("kvf",))
                    tk.emit("gpsimd",
                            lambda e: e.dma_start(out=kvf[:], in_=kvp2[:, 2080:4160],
                                                  accum_op=ALU.add),
                            reads=("kvf",), writes=("kvf",), dma=True)
                tk.copy(kvb[:], kvf[:], reads=("kvf",), writes=("kvb",))

                for t in range(4):
                    t0 = t * 512
                    for kc in range(8):
                        tk.dma("sync", x1f[:, kc * 512:(kc + 1) * 512],
                               x1T[kc * 128:(kc + 1) * 128, t0:t0 + 512],
                               reads=(), writes=("x1f",))
                        tk.mm(ps0[0:1, :], ones_cf[:], x1f[:, kc * 512:(kc + 1) * 512],
                              start=(kc == 0), stop=(kc == 7),
                              reads=("x1f", "ones_cf"), writes=("p0",))
                        tk.act(sq_s[:], x1f[:, kc * 512:(kc + 1) * 512], AF.Square,
                               reads=("x1f",), writes=("sq",))
                        tk.mm(ps1[0:1, :], ones_cf[:], sq_s[:],
                              start=(kc == 0), stop=(kc == 7),
                              reads=("sq", "ones_cf"), writes=("p1",))
                    mu = rows4[:, 0:512]
                    ex2 = rows4[:, 512:1024]
                    var = rows4[:, 1024:1536]
                    rstd = rows4[:, 1536:2048]
                    tk.ts(mu, ps0[0:1, :], 1.0 / D, ALU.mult, reads=("p0",), writes=("mu",))
                    tk.ts(ex2, ps1[0:1, :], 1.0 / D, ALU.mult, reads=("p1",), writes=("ex2",))
                    tk.tt(var, mu, mu, ALU.mult, reads=("mu",), writes=("var",))
                    tk.tt(var, ex2, var, ALU.subtract, reads=("ex2", "var"), writes=("var",))
                    tk.ts(var, var, 1e-5, ALU.add, reads=("var",), writes=("var",))
                    tk.act(var, var, AF.Ln, reads=("var",), writes=("var",))
                    tk.act(rstd, var, AF.Exp, scale=-0.5, reads=("var",), writes=("rstd",))
                    tk.copy(muL[:, t0:t0 + 512], mu, reads=("mu",), writes=("muL",))
                    tk.copy(rL[:, t0:t0 + 512], rstd, reads=("rstd",), writes=("rL",))
                    tk.mm(ps0[:, :], ones_row[:], mu, reads=("mu", "ones_row"), writes=("p0",))
                    tk.mm(ps1[:, :], ones_row[:], rstd, reads=("rstd", "ones_row"), writes=("p1",))
                    for kc in range(8):
                        tk.tt(xnf[:, kc * 512:(kc + 1) * 512],
                              x1f[:, kc * 512:(kc + 1) * 512], ps0[:, :], ALU.subtract,
                              reads=("x1f", "p0"), writes=("xnf",))
                        tk.tt(xnf[:, kc * 512:(kc + 1) * 512],
                              xnf[:, kc * 512:(kc + 1) * 512], ps1[:, :], ALU.mult,
                              reads=("xnf", "p1"), writes=("xnf",))
                    for kc in range(8):
                        tk.act(sq_s[:], xnf[:, kc * 512:(kc + 1) * 512], AF.Square,
                               reads=("xnf",), writes=("sq",))
                        tk.mm(ps2[0:1, :], ones_cf[:], sq_s[:],
                              start=(kc == 0), stop=(kc == 7),
                              reads=("sq", "ones_cf"), writes=("p2",))
                    ms = rows4[:, 0:512]
                    rr = rows4[:, 512:1024]
                    tk.ts(ms, ps2[0:1, :], 1.0 / D, ALU.mult, s2=1e-6, op1=ALU.add,
                          reads=("p2",), writes=("ms",))
                    tk.act(ms, ms, AF.Ln, reads=("ms",), writes=("ms",))
                    tk.act(rr, ms, AF.Exp, scale=-0.5, reads=("ms",), writes=("rr",))
                    tk.mm(ps2[:, :], ones_row[:], rr, reads=("rr", "ones_row"), writes=("p2",))
                    for kc in range(8):
                        tk.stt(tf[:, kc * 512:(kc + 1) * 512],
                               xnf[:, kc * 512:(kc + 1) * 512], nw_sb[:, kc:kc + 1],
                               ps2[:, :], ALU.mult, ALU.mult,
                               reads=("xnf", "nw", "p2"), writes=("xnf",))
                        tk.copy(t_sb[:, kc * TO + t0: kc * TO + t0 + 512],
                                tf[:, kc * 512:(kc + 1) * 512],
                                reads=("xnf",), writes=("tv",))
                    for kc in range(8):
                        tk.mm(psG[:, :], gw_sb[:, kc * 8:(kc + 1) * 8],
                              tf[:, kc * 512:(kc + 1) * 512],
                              start=(kc == 0), stop=(kc == 7),
                              reads=("xnf", "gw"), writes=("pG",))
                    tk.copy(gate_s[:], psG[:, :], reads=("pG",), writes=("gs",))
                    if debug:
                        tk.dma("sync", dbg_gs[:, t0:t0 + 512], gate_s[:],
                               reads=("gs",), writes=())
                    for sub in range(4):
                        tk.trans(psTf[:, 0:8], gate_s[:, sub * 128:(sub + 1) * 128],
                                 idf128[0:8, 0:8], reads=("gs", "idf"), writes=("pTf",))
                        tk.copy(gt_scr[:, sub * 8:(sub + 1) * 8], psTf[:, 0:8],
                                reads=("pTf",), writes=("gt",))
                        g = gt_scr[:, sub * 8:(sub + 1) * 8]
                        tk.reduce(m1c[:], g, ALU.max, reads=("gt",), writes=("m1",))
                        tk.ts(e1s[:], g, m1c[:], ALU.is_ge, reads=("gt", "m1"), writes=("e1",))
                        tk.stt(t2s[:], e1s[:], -1e9, g, ALU.mult, ALU.add,
                               reads=("e1", "gt"), writes=("t2",))
                        tk.reduce(m2c[:], t2s[:], ALU.max, reads=("t2",), writes=("m2",))
                        tk.tt(w1c[:], m1c[:], m2c[:], ALU.subtract,
                              reads=("m1", "m2"), writes=("w1c",))
                        tk.act(w1c[:], w1c[:], AF.Sigmoid, reads=("w1c",), writes=("w1c",))
                        tk.ts(w2c[:], w1c[:], -1.0, ALU.mult, s2=1.0, op1=ALU.add,
                              reads=("w1c",), writes=("w2c",))
                        tk.ts(e2s[:], t2s[:], m2c[:], ALU.is_ge, reads=("t2", "m2"),
                              writes=("e2",))
                        tk.ts(e1s[:], e1s[:], w1c[:], ALU.mult, reads=("e1", "w1c"),
                              writes=("e1",))
                        tk.stt(e2s[:], e2s[:], w2c[:], e1s[:], ALU.mult, ALU.add,
                               reads=("e2", "w2c", "e1"), writes=("e2",))
                        tk.trans(psTf[0:8, 0:128], e2s[:], idf128[:],
                                 reads=("e2", "idf"), writes=("pTf",))
                        tk.copy(c_fm[:, t0 + sub * 128: t0 + (sub + 1) * 128],
                                psTf[0:8, 0:128], reads=("pTf",), writes=("cfm",))

                for h in range(16):
                    r0 = (h % 2) * 64
                    cb = (h // 2) * TO
                    for t in range(4):
                        t0 = t * 512
                        qsl = qr_sb[r0:r0 + 64, cb + t0: cb + t0 + 512]
                        tk.act(sq_s[r0:r0 + 64, :], qsl, AF.Square,
                               reads=("qr",), writes=("sq",))
                        tk.mm(psD[:, :], ones_cf[r0:r0 + 64, 0:1], sq_s[r0:r0 + 64, :],
                              reads=("sq", "ones_cf"), writes=("pD",))
                        tk.act(row_s[:], psD[:, :], AF.Exp, scale=-0.5,
                               bias=nln16[0:1, 0:1], reads=("pD", "nln16"), writes=("row",))
                        tk.mm(ps3[:, :], ones_row[:], row_s[:],
                              reads=("row", "ones_row"), writes=("p3",))
                        tk.copy(eqqb[:], ps3[:, :], reads=("p3",), writes=("eqqb",))
                        for mc in range(2):
                            tk.mm(ps0[:, :], omg_sb[r0:r0 + 64, mc * 128:(mc + 1) * 128],
                                  qsl, reads=("omg", "qr"), writes=("p0",))
                            tk.act(qp_scr[:], ps0[:, :], AF.Exp, reads=("p0",),
                                   writes=("qp",))
                            tk.tt(qp_scr[:], qp_scr[:], eqqb[:], ALU.mult,
                                  reads=("qp", "eqqb"), writes=("qp",))
                            tk.mm(psN[r0:r0 + 64, :],
                                  kvb[:, h * 130 + mc * 65: h * 130 + mc * 65 + 64],
                                  qp_scr[:], start=(mc == 0), stop=(mc == 1),
                                  tile_position=(0, r0),
                                  reads=("kvb", "qp"), writes=("pN",))
                            tk.mm(psD[:, :],
                                  kvb[:, h * 130 + mc * 65 + 64: h * 130 + mc * 65 + 65],
                                  qp_scr[:], start=(mc == 0), stop=(mc == 1),
                                  reads=("kvb", "qp"), writes=("pD",))
                        tk.ts(row_s[:], psD[:, :], 1e-6, ALU.add, reads=("pD",),
                              writes=("row",))
                        tk.mm(ps3[r0:r0 + 64, :], ones_row[0:1, 0:64], row_s[:],
                              tile_position=(0, r0),
                              reads=("row", "ones_row"), writes=("p3",))
                        tk.recip(rb_scr[r0:r0 + 64, :], ps3[r0:r0 + 64, :],
                                 reads=("p3",), writes=("rbs",))
                        tk.copy(nm_scr[r0:r0 + 64, :], psN[r0:r0 + 64, :],
                                reads=("pN",), writes=("nms",))
                        tk.tt(attn1[r0:r0 + 64, cb + t0: cb + t0 + 512],
                              nm_scr[r0:r0 + 64, :], rb_scr[r0:r0 + 64, :], ALU.mult,
                              reads=("nms", "rbs"), writes=("at1",))

            if debug:
                tk.dma("sync", dbg_gw, gw_sb[:], reads=("gw",), writes=())
                tk.dma("sync", dbg_c, c_fm[:], reads=("cfm",), writes=())
                tk.dma("sync", dbg_g, gt_scr[:], reads=("gt",), writes=())
                tk.dma("sync", dbg_t, t_sb[:], reads=("tv",), writes=())
            with ExitStack() as scopeB:
                def sbB(name, shape, dt=BF16):
                    return scopeB.enter_context(nc.sbuf_tensor(_un(name), shape, dt))

                BT = 1024
                h_e = sbB("h_e", [128, 16 * BT])
                moe_acc = sbB("moe_acc", [128, 8 * BT], F32)
                wA = sbB("wA", [128, 16 * 128])
                wB = sbB("wB", [128, 16 * 128])
                x1g = sbB("x1g", [128, 512], F32)
                xc_scr = sbB("xc_scr", [128, 512], F32)
                ao_scr = sbB("ao_scr", [128, 512], F32)
                for nm in ("he", "macc", "wA", "wB", "x1g", "xc", "ao"):
                    tk.reuse(nm, "qr", "kvf", "kvb", "x1f", "xnf", "tf")

                for bt in range(2):
                    bt0 = bt * BT
                    for e in range(8):
                        for hc in range(16):
                            sa, ka_ = (wA, "wA") if hc % 2 == 0 else (wB, "wB")
                            tk.dma("gpsimd", sa[:, 0:8 * 128], w1[e, hc],
                                   reads=(), writes=(ka_,))
                            for st in range(2):
                                c0 = bt0 + st * 512
                                ps = (ps0, ps1)[st]
                                bank = ("p0", "p1")[st]
                                for kc in range(8):
                                    tk.mm(ps[:, :], sa[:, kc * 128:(kc + 1) * 128],
                                          t_sb[:, kc * TO + c0: kc * TO + c0 + 512],
                                          start=(kc == 0), stop=(kc == 7),
                                          reads=(ka_, "tv"), writes=(bank,))
                                tk.act(h_e[:, hc * BT + st * 512: hc * BT + st * 512 + 512],
                                       ps[:, :], AF.Gelu,
                                       bias=b1_sb[:, e * 16 + hc: e * 16 + hc + 1],
                                       reads=(bank, "b1"), writes=("he",))
                        for st in range(2):
                            c0 = bt0 + st * 512
                            tk.mm(ps2[:, :], sel_sb[0:8, e * 128:(e + 1) * 128],
                                  c_fm[0:8, c0:c0 + 512],
                                  reads=("cfm", "sel"), writes=("p2",))
                            tk.copy(cb_scr[:], ps2[:, :], reads=("p2",), writes=("cbs",))
                            for hc in range(16):
                                tk.tt(h_e[:, hc * BT + st * 512: hc * BT + st * 512 + 512],
                                      h_e[:, hc * BT + st * 512: hc * BT + st * 512 + 512],
                                      cb_scr[:], ALU.mult, reads=("he", "cbs"), writes=("he",))
                        for oc in range(8):
                            sa, ka_ = (wA, "wA") if oc % 2 == 0 else (wB, "wB")
                            tk.dma("gpsimd", sa[:], w2[e, oc], reads=(), writes=(ka_,))
                            for st in range(2):
                                c0 = bt0 + st * 512
                                ps = (ps0, ps1)[st]
                                bank = ("p0", "p1")[st]
                                for kc in range(16):
                                    tk.mm(ps[:, :], sa[:, kc * 128:(kc + 1) * 128],
                                          h_e[:, kc * BT + st * 512: kc * BT + st * 512 + 512],
                                          start=(kc == 0), stop=False,
                                          reads=(ka_, "he"), writes=(bank,))
                                if e == 0:
                                    tk.mm(ps[:, :], b2_sb[:, oc * 128:(oc + 1) * 128],
                                          c_fm[:, c0:c0 + 512], start=False, stop=True,
                                          reads=("b2", "cfm"), writes=(bank,))
                                else:
                                    tk.mm(ps[:, :], b2_sb[0:8, oc * 128:(oc + 1) * 128],
                                          c_fm[0:8, c0:c0 + 512], start=False, stop=True,
                                          reads=("b2", "cfm"), writes=(bank,)) \
                                        if False else None
                                if e == 0:
                                    tk.copy(moe_acc[:, oc * BT + st * 512:
                                                    oc * BT + st * 512 + 512],
                                            ps[:, :], reads=(bank,), writes=("macc",))
                                else:
                                    tk.tt(moe_acc[:, oc * BT + st * 512:
                                                  oc * BT + st * 512 + 512],
                                          moe_acc[:, oc * BT + st * 512:
                                                  oc * BT + st * 512 + 512],
                                          ps[:, :], ALU.add, reads=("macc", bank),
                                          writes=("macc",))
                    for oc in range(8):
                        for st in range(2):
                            c0 = bt0 + st * 512
                            ps = (ps0, ps1)[st]
                            bank = ("p0", "p1")[st]
                            for kc in range(8):
                                tk.mm(ps[:, :],
                                      outw_sb[:, kc * 1024 + oc * 128:
                                              kc * 1024 + oc * 128 + 128],
                                      attn1[:, kc * TO + c0: kc * TO + c0 + 512],
                                      start=(kc == 0), stop=(kc == 7),
                                      reads=("outw", "at1"), writes=(bank,))
                            tk.ts(ao_scr[:], ps[:, :], outb_sb[:, oc:oc + 1], ALU.add,
                                  reads=(bank, "outb"), writes=("ao",))
                            tk.dma("sync", x1g[:],
                                   x1T[oc * 128:(oc + 1) * 128, c0:c0 + 512],
                                   reads=(), writes=("x1g",))
                            tk.mm(ps2[:, :], ones_rowb[:], muL[:, c0:c0 + 512],
                                  reads=("muL", "ones_rowb"), writes=("p2",))
                            tk.mm(ps3[:, :], ones_rowb[:], rL[:, c0:c0 + 512],
                                  reads=("rL", "ones_rowb"), writes=("p3",))
                            tk.tt(xc_scr[:], x1g[:], ps2[:, :], ALU.subtract,
                                  reads=("x1g", "p2"), writes=("xc",))
                            tk.tt(xc_scr[:], xc_scr[:], ps3[:, :], ALU.mult,
                                  reads=("xc", "p3"), writes=("xc",))
                            tk.tt(ao_scr[:], ao_scr[:], xc_scr[:], ALU.add,
                                  reads=("ao", "xc"), writes=("ao",))
                            tk.tt(ao_scr[:], ao_scr[:], x1g[:], ALU.add,
                                  reads=("ao", "x1g"), writes=("ao",))
                            tk.tt(ao_scr[:], ao_scr[:],
                                  moe_acc[:, oc * BT + st * 512: oc * BT + st * 512 + 512],
                                  ALU.add, reads=("ao", "macc"), writes=("ao",))
                            tk.dma("sync", xo[oc * 128:(oc + 1) * 128, c0:c0 + 512],
                                   ao_scr[:], reads=("ao",), writes=())

    return nc





def build_l1_moe(debug=False):
    nc = bass.Bass("TRN2", target_bir_lowering=False)
    TO = 2048
    x1T = nc.dram_tensor("x1T", [D, TO], F32, kind="ExternalInput").ap()
    qr = nc.dram_tensor("qr", [D, TO], BF16, kind="ExternalInput").ap()
    kvp2 = nc.dram_tensor("kvp2", [128, 2 * 2080], F32, kind="ExternalInput").ap()
    omg = nc.dram_tensor("omg", [128, 256], F32, kind="ExternalInput").ap()
    outw = nc.dram_tensor("outw", [8, 128, 1024], F32, kind="ExternalInput").ap()
    outb = nc.dram_tensor("outb", [128, 8], F32, kind="ExternalInput").ap()
    nw = nc.dram_tensor("nw", [128, 8], F32, kind="ExternalInput").ap()
    gw = nc.dram_tensor("gw", [8, 128, 8], F32, kind="ExternalInput").ap()
    w1 = nc.dram_tensor("w1", [8, 16, 128, 8, 128], F32, kind="ExternalInput").ap()
    b1 = nc.dram_tensor("b1", [128, 128], F32, kind="ExternalInput").ap()
    w2 = nc.dram_tensor("w2", [8, 8, 128, 16, 128], F32, kind="ExternalInput").ap()
    b2 = nc.dram_tensor("b2", [8, 1024], F32, kind="ExternalInput").ap()
    xo = nc.dram_tensor("xo", [D, TO], F32, kind="ExternalOutput").ap()
    dbg = None
    if debug:
        dbg = (nc.dram_tensor("dbg_c", [8, TO], F32, kind="ExternalOutput").ap(),
               nc.dram_tensor("dbg_g", [128, 32], F32, kind="ExternalOutput").ap(),
               nc.dram_tensor("dbg_t", [128, 8 * TO], BF16, kind="ExternalOutput").ap(),
               nc.dram_tensor("dbg_gs", [8, TO], F32, kind="ExternalOutput").ap(),
               nc.dram_tensor("dbg_gw", [128, 64], F32, kind="ExternalOutput").ap())
    with ExitStack() as ctx0:
        tk = Tracker(nc, ctx0)
        with nc.Block() as block:
            PHASE[0] = "l1m"
            emit_l1_moe(nc, tk, x1T, qr, kvp2, omg, outw, outb, nw, gw, w1, b1,
                        w2, b2, xo, dbg=dbg)
            tk.flush(block)
    return nc

def _f32c(a):
    return np.ascontiguousarray(a, dtype=np.float32)


def stage_l0_attn(inputs):
    x = inputs["L0_attn_x"]
    W = inputs["L0_attn_w_attn"]
    ba = inputs["L0_attn_b_attn"]
    Wk2 = inputs["L0_attn_w_k2"]
    bk2 = inputs["L0_attn_b_k2"]
    in_maps = []
    for c in range(NCORES):
        b, hg = c // 2, c % 2
        mine = slice(512 * hg, 512 * hg + 512)
        other = slice(512 * (1 - hg), 512 * (1 - hg) + 512)
        perm = np.r_[np.arange(512 * hg, 512 * hg + 512),
                     np.arange(512 * (1 - hg), 512 * (1 - hg) + 512)]
        xTc = _f32c(x[b].T[perm])
        qs = W[512 * hg: 512 * hg + 512][:, perm]
        ks = W[1024 + 512 * hg: 1024 + 512 * hg + 512][:, perm]
        Wc = np.concatenate([qs, ks], 0)
        wqk_h = _f32c(Wc.T.reshape(8, 128, 1024))
        wk2_h = _f32c(Wk2[512 * hg: 512 * hg + 512][:, perm].T.reshape(8, 128, 512))
        bqk_h = _f32c(np.concatenate([ba[512 * hg: 512 * hg + 512],
                                      ba[1024 + 512 * hg: 1024 + 512 * hg + 512]])
                      .reshape(8, 128).T)
        bk2_h = _f32c(bk2[512 * hg: 512 * hg + 512].reshape(4, 128).T)
        in_maps.append({
            "xT": xTc, "wqk": wqk_h, "wk2": wk2_h, "bqk": bqk_h, "bk2": bk2_h,
            "epsc": np.full((1, 1), 1e-5, np.float32),
        })
    return in_maps


def run_l0_attn(inputs, core_ids=None, debug=False):
    nc = build_l0_attn(debug=debug)
    in_maps = stage_l0_attn(inputs)
    ids = core_ids if core_ids is not None else list(range(NCORES))
    res = run_bass_kernel_spmd(nc, [in_maps[i] for i in ids], core_ids=ids)
    if debug:
        return res.results
    return [r["y"] for r in res.results]


def stage_l1_moe(inputs, x1_by_core, qr_by_core, kvp_by_core):
    outw_h = _f32c(inputs["L1_attn_out_w"].reshape(8, 128, 1024))
    outb_h = _f32c(inputs["L1_attn_out_b"].reshape(8, 128).T)
    nw_h = _f32c(inputs["L1_ffn_norm_weight"].reshape(8, 128).T)
    gw_h = _f32c(inputs["L1_ffn_gate_w"].T.reshape(8, 128, 8))
    w1_h = _f32c(inputs["L1_ffn_w1"].reshape(8, 16, 128, 8, 128).transpose(0, 1, 4, 3, 2))
    b1_h = _f32c(inputs["L1_ffn_b1"].reshape(8, 16, 128).transpose(2, 0, 1).reshape(128, 128))
    w2_h = _f32c(inputs["L1_ffn_w2"].reshape(8, 8, 128, 16, 128).transpose(0, 1, 4, 3, 2))
    b2_h = _f32c(inputs["L1_ffn_b2"])
    omg_h = _f32c(np.concatenate([inputs["L1_attn_omega"]] * 2, 0))
    in_maps = []
    for c in range(NCORES):
        b, half = c // 2, c % 2
        if c not in x1_by_core:
            in_maps.append(None)
            continue
        partner = 2 * b + (1 - half)
        kvp2 = _f32c(np.concatenate([kvp_by_core[c], kvp_by_core[partner]], 1))
        in_maps.append({
            "x1T": _f32c(x1_by_core[c]), "qr": np.ascontiguousarray(qr_by_core[c]),
            "kvp2": kvp2, "omg": omg_h, "outw": outw_h, "outb": outb_h,
            "nw": nw_h, "gw": gw_h, "w1": w1_h, "b1": b1_h, "w2": w2_h, "b2": b2_h,
        })
    return in_maps


def run_l1_moe(inputs, x1_by_core, qr_by_core, kvp_by_core, core_ids=None):
    nc = build_l1_moe()
    in_maps = stage_l1_moe(inputs, x1_by_core, qr_by_core, kvp_by_core)
    ids = core_ids if core_ids is not None else list(range(NCORES))
    res = run_bass_kernel_spmd(nc, [in_maps[i] for i in ids], core_ids=ids)
    return [r["xo"] for r in res.results]



def build_fused(nc8=8, debug=False):
    nc = bass.Bass("TRN2", target_bir_lowering=False)
    TO = 2048
    TE = TO + 2
    a_xT = nc.dram_tensor("a_xT", [D, T], F32, kind="ExternalInput").ap()
    a_wqk = nc.dram_tensor("a_wqk", [8, 128, 1024], F32, kind="ExternalInput").ap()
    a_wk2 = nc.dram_tensor("a_wk2", [8, 128, 512], F32, kind="ExternalInput").ap()
    a_bqk = nc.dram_tensor("a_bqk", [128, 8], F32, kind="ExternalInput").ap()
    a_bk2 = nc.dram_tensor("a_bk2", [128, 4], F32, kind="ExternalInput").ap()
    b_xT = nc.dram_tensor("b_xT", [D, TO], F32, kind="ExternalInput").ap()
    b_wproj = nc.dram_tensor("b_wproj", [8, 128, 1024], F32, kind="ExternalInput").ap()
    b_bproj = nc.dram_tensor("b_bproj", [128, 8], F32, kind="ExternalInput").ap()
    b_w1 = nc.dram_tensor("b_w1", [32, 128, 8, 128], F32, kind="ExternalInput").ap()
    b_w2 = nc.dram_tensor("b_w2", [32, 128, 8, 128], F32, kind="ExternalInput").ap()
    b_h1 = nc.dram_tensor("b_h1", [32, 128, 32, 128], F32, kind="ExternalInput").ap()
    b_h2 = nc.dram_tensor("b_h2", [32, 128, 32, 128], F32, kind="ExternalInput").ap()
    b_w3 = nc.dram_tensor("b_w3", [8, 128, 32, 128], F32, kind="ExternalInput").ap()
    b_s1 = nc.dram_tensor("b_s1", [128, 32], F32, kind="ExternalInput").ap()
    b_s2 = nc.dram_tensor("b_s2", [128, 32], F32, kind="ExternalInput").ap()
    b_sh1 = nc.dram_tensor("b_sh1", [128, 32], F32, kind="ExternalInput").ap()
    b_sh2 = nc.dram_tensor("b_sh2", [128, 32], F32, kind="ExternalInput").ap()
    b_s3 = nc.dram_tensor("b_s3", [128, 8], F32, kind="ExternalInput").ap()
    c_ck = nc.dram_tensor("c_ck", [3, 3, 4, 2, 128, 256], F32, kind="ExternalInput").ap()
    c_cb = nc.dram_tensor("c_cb", [128, 24], F32, kind="ExternalInput").ap()
    c_rw = nc.dram_tensor("c_rw", [3, 3, 128, 8, 256], F32, kind="ExternalInput").ap()
    c_rb = nc.dram_tensor("c_rb", [128, 18], F32, kind="ExternalInput").ap()
    c_pw = nc.dram_tensor("c_pw", [3, 128, 6, 1024], F32, kind="ExternalInput").ap()
    c_pb = nc.dram_tensor("c_pb", [128, 24], F32, kind="ExternalInput").ap()
    c_omg = nc.dram_tensor("c_omg", [128, 256], F32, kind="ExternalInput").ap()
    c_lmt = nc.dram_tensor("c_lmt", [128, 64], F32, kind="ExternalInput").ap()
    c_sint = nc.dram_tensor("c_sint", [128, TO], F32, kind="ExternalInput").ap()
    c_cost = nc.dram_tensor("c_cost", [128, TO], F32, kind="ExternalInput").ap()
    c_rotm = nc.dram_tensor("c_rotm", [128, 128], F32, kind="ExternalInput").ap()
    d_outw = nc.dram_tensor("d_outw", [8, 128, 1024], F32, kind="ExternalInput").ap()
    d_outb = nc.dram_tensor("d_outb", [128, 8], F32, kind="ExternalInput").ap()
    d_nw = nc.dram_tensor("d_nw", [128, 8], F32, kind="ExternalInput").ap()
    d_gw = nc.dram_tensor("d_gw", [8, 128, 8], F32, kind="ExternalInput").ap()
    d_w1 = nc.dram_tensor("d_w1", [8, 16, 128, 8, 128], F32, kind="ExternalInput").ap()
    d_b1 = nc.dram_tensor("d_b1", [128, 128], F32, kind="ExternalInput").ap()
    d_w2 = nc.dram_tensor("d_w2", [8, 8, 128, 16, 128], F32, kind="ExternalInput").ap()
    d_b2 = nc.dram_tensor("d_b2", [8, 1024], F32, kind="ExternalInput").ap()
    epsc = nc.dram_tensor("epsc", [1, 1], F32, kind="ExternalInput").ap()
    hmask = nc.dram_tensor("hmask", [128, 2], F32, kind="ExternalInput").ap()
    tok = nc.dram_tensor("tok", [1, 16], mybir.dt.uint32, kind="ExternalInput").ap()
    xo = nc.dram_tensor("xo", [D, TO], F32, kind="ExternalOutput").ap()
    if debug:
        dbg_ymine = nc.dram_tensor("dbg_ymine", [D, TO], BF16, kind="ExternalOutput").ap()
        dbg_yloc = nc.dram_tensor("dbg_yloc", [512, T], BF16, kind="ExternalOutput").ap()
        dbg_x1e = nc.dram_tensor("dbg_x1e", [D, TO + 2], F32, kind="ExternalOutput").ap()
        dbg_qr = nc.dram_tensor("dbg_qr", [D, TO], BF16, kind="ExternalOutput").ap()
        dbg_kv = nc.dram_tensor("dbg_kv", [128, 2080], F32, kind="ExternalOutput").ap()
        dbg_wpx = tuple(nc.dram_tensor("dbg_wpx%d" % i, [128, 8192], BF16,
                                       kind="ExternalOutput").ap() for i in range(3))
        dbg2 = (nc.dram_tensor("dbg_attn", [D, TO], F32, kind="ExternalOutput").ap(),
                nc.dram_tensor("dbg_ffn", [D, TO], F32, kind="ExternalOutput").ap(),
                nc.dram_tensor("dbg_wp", [128, 8192], BF16, kind="ExternalOutput").ap(),
                nc.dram_tensor("dbg_yt", [128, 8192], BF16, kind="ExternalOutput").ap(),
                nc.dram_tensor("dbg_raw", [128, 512], F32, kind="ExternalOutput").ap())
    y_loc = nc.dram_tensor("y_loc", [512, T], BF16, kind="Internal").ap()
    y_all = nc.dram_tensor("y_all", [nc8 * 512, T], BF16, kind="Internal",
                           addr_space="Shared").ap()
    y_mine = nc.dram_tensor("y_mine", [D, 2048], BF16, kind="Internal").ap()
    x1e_loc = nc.dram_tensor("x1e_loc", [D, TE], F32, kind="Internal").ap()
    E_all = nc.dram_tensor("E_all", [nc8 * 128, 16], F32, kind="Internal",
                           addr_space="Shared").ap()
    qr_loc = nc.dram_tensor("qr_loc", [D, TO], BF16, kind="Internal").ap()
    kvp_loc = nc.dram_tensor("kvp_loc", [128, 2080], F32, kind="Internal").ap()
    kvp_all = nc.dram_tensor("kvp_all", [nc8 * 128, 2080], F32, kind="Internal",
                             addr_space="Shared").ap()
    flags = nc.dram_tensor("flags", [nc8, 64], mybir.dt.uint32, kind="Internal",
                           addr_space="Shared").ap()
    _exprs = {}

    def pid_exprs(e):
        k = id(e)
        if k not in _exprs:
            pid = e.partition_id()
            half = pid % 2
            _exprs[k] = {
                "rowoff": (pid // 2) * 1024,
                "coloff": half * 2048,
                "prt128": (pid // 2) * 256 + 128 - half * 128,
                "pid128": pid * 128,
                "pid512": pid * 512,
                "prtrow": (pid // 2) * 2 + 1 - half,
                "pidrow": pid,
                "ypub": {c: pid * 512 + c * 128 for c in range(4)},
                "ymine": {c: (pid // 2) * 1024 + c * 128 for c in range(8)},
            }
        return _exprs[k]

    with ExitStack() as ctx0:
        tk = Tracker(nc, ctx0)
        poll_sem = ctx0.enter_context(nc.semaphore("poll_sem"))
        tok_sb = ctx0.enter_context(nc.sbuf_tensor("tok_sb", [1, 16], mybir.dt.uint32))
        fp_sb = ctx0.enter_context(nc.sbuf_tensor("fp_sb", [1, 16], mybir.dt.uint32))

        def publish_flag(bk, datakey):
            def f(e):
                ex = pid_exprs(e)
                return e.dma_start(out=flags[bass.ds(ex["pidrow"], 1),
                                             16 * bk:16 * bk + 16],
                                   in_=tok_sb[0:1, :])
            tk.emit("sync", f, reads=("tok", datakey), writes=("flag%d" % bk,), dma=True)

        def poll_partner(bk):
            import concourse.expressions as _ex

            def f(e):
                ex = pid_exprs(e)
                with e.register() as rf, e.register() as rt:
                    e.reg_load(rt, tok_sb[0:1, 0:1])
                    e.reg_mov(rf, 0)
                    with e.While(lambda: _ex.RuntimeValue(val=rf) !=
                                 _ex.RuntimeValue(val=rt)):
                        e.dma_start(out=fp_sb[0:1, :],
                                    in_=flags[bass.ds(ex["prtrow"], 1),
                                              16 * bk:16 * bk + 16]).then_inc(poll_sem, 16)
                        e.wait_ge(poll_sem, 16)
                        e.sem_clear(poll_sem)
                        e.reg_load(rf, fp_sb[0:1, 0:1])
                return e.nop()
            tk.emit("sync", f, reads=("tok", "flag%d" % bk), writes=("barrier%d" % bk,))

        with nc.Block() as block:
            tk.dma("sync", tok_sb[:], tok, reads=(), writes=("tok",))
            PHASE[0] = "l0a"
            emit_l0_attn(nc, tk, a_xT, a_wqk, a_wk2, a_bqk, a_bk2, epsc, y_loc)
            tk.fence()

            PHASE[0] = "xch1"
            with ExitStack() as xctx:
                xch = xctx.enter_context(
                    nc.sbuf_tensor(_un("xch"), [128, 4096], BF16))
                tk.reuse("xch", "yh")
                for cb4 in range(4):
                    tk.dma("sync", xch[:], y_loc[cb4 * 128:(cb4 + 1) * 128, :],
                           reads=(), writes=("xch",))

                    def f_ypub(e, cb4=cb4):
                        ex = pid_exprs(e)
                        return e.dma_start(
                            out=y_all[bass.ds(ex["ypub"][cb4], 128), :],
                            in_=xch[:])
                    tk.emit("scalar", f_ypub, reads=("xch",), writes=("y_all",),
                            dma=True)
            publish_flag(0, "y_all")
            poll_partner(0)
            tk.fence()

            def f_ymine(e):
                ex = pid_exprs(e)
                return e.dma_start(out=y_mine[:, :],
                                   in_=y_all[bass.ds(ex["rowoff"], 1024),
                                             bass.ds(ex["coloff"], 2048)])
            tk.emit("scalar", f_ymine, reads=("y_all", "barrier0"), writes=("y_mine",), dma=True)
            tk.fence()

            PHASE[0] = "l0f"
            emit_l0_ffn(nc, tk, b_xT, y_mine, b_wproj, b_bproj, b_w1, b_w2, b_h1,
                        b_h2, b_w3, b_s1, b_s2, b_sh1, b_sh2, b_s3, epsc,
                        x1e_loc[:, 1:TE - 1], dbg=(dbg2 if debug else None),
                        dbg_wpx=(dbg_wpx if debug else None))
            tk.fence()

            if debug:
                tk.dma("sync", dbg_yloc, y_loc, reads=("barrier0",), writes=())
                tk.dma("sync", dbg_ymine, y_mine, reads=("y_mine",), writes=())
            PHASE[0] = "halo"
            with ExitStack() as hctx:
                e_scr = hctx.enter_context(nc.sbuf_tensor(_un("e_scr"), [128, 16], F32))
                ep_scr = hctx.enter_context(nc.sbuf_tensor(_un("ep_scr"), [128, 16], F32))
                hm_sb = hctx.enter_context(nc.sbuf_tensor(_un("hm_sb"), [128, 2], F32))
                hv_sb = hctx.enter_context(nc.sbuf_tensor(_un("hv_sb"), [128, 2], F32))
                tk.dma("sync", hm_sb[:], hmask, reads=(), writes=("hm",))
                for kc in range(8):
                    for (dcol, scol) in ((kc * 2, 1), (kc * 2 + 1, TE - 2)):
                        tk.emit("sync",
                                lambda e, dcol=dcol, scol=scol, kc=kc: e.dma_start(
                                    out=e_scr[:, dcol:dcol + 1],
                                    in_=x1e_loc[kc * 128:(kc + 1) * 128, scol:scol + 1],
                                    allow_slow_non_contiguous=True),
                                reads=(), writes=("e_scr",), dma=True)
                def f_epub(e):
                    ex = pid_exprs(e)
                    return e.dma_start(out=E_all[bass.ds(ex["pid128"], 128), :],
                                       in_=e_scr[:])
                tk.emit("scalar", f_epub, reads=("e_scr",), writes=("E_all",), dma=True)
                publish_flag(1, "E_all")
                poll_partner(1)

                def f_ep(e):
                    ex = pid_exprs(e)
                    return e.dma_start(out=ep_scr[:], in_=E_all[bass.ds(ex["prt128"], 128), :])
                tk.emit("scalar", f_ep, reads=("E_all", "barrier1"), writes=("ep",), dma=True)
                for kc in range(8):
                    tk.tt(hv_sb[:, 0:1], ep_scr[:, kc * 2 + 1: kc * 2 + 2],
                          hm_sb[:, 0:1], ALU.mult, reads=("ep", "hm"), writes=("hv",))
                    tk.tt(hv_sb[:, 1:2], ep_scr[:, kc * 2: kc * 2 + 1],
                          hm_sb[:, 1:2], ALU.mult, reads=("ep", "hm"), writes=("hv",))
                    for (scol, dcol) in ((0, 0), (1, TE - 1)):
                        tk.emit("sync",
                                lambda e, scol=scol, dcol=dcol, kc=kc: e.dma_start(
                                    out=x1e_loc[kc * 128:(kc + 1) * 128, dcol:dcol + 1],
                                    in_=hv_sb[:, scol:scol + 1],
                                    allow_slow_non_contiguous=True),
                                reads=("hv",), writes=("x1eh",), dma=True)
            tk.fence()

            if debug:
                tk.dma("sync", dbg_x1e, x1e_loc, reads=("x1eh",), writes=())
            PHASE[0] = "l1a"
            emit_l1_attn(nc, tk, x1e_loc, c_ck, c_cb, c_rw, c_rb, c_pw, c_pb,
                         c_omg, c_lmt, c_sint, c_cost, c_rotm, epsc,
                         qr_loc, kvp_loc)
            tk.fence()

            PHASE[0] = "xch3"
            with ExitStack() as xctx:
                xch3 = xctx.enter_context(
                    nc.sbuf_tensor(_un("xch3"), [128, 2080], F32))
                tk.reuse("xch3", "kvs")
                tk.dma("sync", xch3[:], kvp_loc, reads=(), writes=("xch3",))

                def f_kpub(e):
                    ex = pid_exprs(e)
                    return e.dma_start(out=kvp_all[bass.ds(ex["pid128"], 128), :],
                                       in_=xch3[:])
                tk.emit("scalar", f_kpub, reads=("xch3",), writes=("kvp_all",), dma=True)
            publish_flag(2, "kvp_all")
            poll_partner(2)
            tk.fence()

            if debug:
                tk.dma("sync", dbg_qr, qr_loc, reads=("barrier2",), writes=())
                tk.dma("sync", dbg_kv, kvp_loc, reads=("barrier2",), writes=())

            def kvload(tk_, kvf):
                def f1(e):
                    ex = pid_exprs(e)
                    return e.dma_start(out=kvf[:], in_=kvp_all[bass.ds(ex["pid128"], 128), :])
                def f2(e):
                    ex = pid_exprs(e)
                    return e.dma_start(out=kvf[:], in_=kvp_all[bass.ds(ex["prt128"], 128), :],
                                       accum_op=ALU.add)
                tk_.emit("gpsimd", f1, reads=("kvp_all", "barrier2"), writes=("kvf",), dma=True)
                tk_.emit("gpsimd", f2, reads=("kvp_all", "barrier2", "kvf"), writes=("kvf",), dma=True)

            PHASE[0] = "l1m"
            emit_l1_moe(nc, tk, x1e_loc[:, 1:TE - 1], qr_loc, None, c_omg, d_outw,
                        d_outb, d_nw, d_gw, d_w1, d_b1, d_w2, d_b2, xo,
                        kvload=kvload)
            tk.flush(block)
    return nc


def stage_fused(inputs):
    dummy_y = {c: np.zeros((512, T), bfloat16) for c in range(NCORES)}
    dummy_x1 = {c: np.zeros((D, 2048), np.float32) for c in range(NCORES)}
    dummy_qr = {c: np.zeros((D, 2048), bfloat16) for c in range(NCORES)}
    dummy_kv = {c: np.zeros((128, 2080), np.float32) for c in range(NCORES)}
    m1 = stage_l0_attn(inputs)
    m2 = stage_l0_ffn(inputs, dummy_y)
    m3 = stage_l1_attn(inputs, dummy_x1)
    m4 = stage_l1_moe(inputs, dummy_x1, dummy_qr, dummy_kv)
    in_maps = []
    for c in range(NCORES):
        half = c % 2
        im = {}
        for k, v in m1[c].items():
            if k != "epsc":
                im["a_" + k] = v
        for k, v in m2[c].items():
            if k not in ("epsc", "yb"):
                im["b_" + k] = v
        for k, v in m3[c].items():
            if k not in ("epsc", "x1e"):
                im["c_" + k] = v
        for k, v in m4[c].items():
            if k not in ("x1T", "qr", "kvp2", "omg"):
                im["d_" + k] = v
        im["epsc"] = np.full((1, 1), 1e-5, np.float32)
        maskL = 1.0 if half == 1 else 0.0
        maskR = 1.0 if half == 0 else 0.0
        im["hmask"] = _f32c(np.tile(np.array([[maskL, maskR]], np.float32), (128, 1)))
        in_maps.append(im)
    nonce = np.uint32(np.random.default_rng().integers(1, 2 ** 31))
    for im in in_maps:
        im["tok"] = np.full((1, 16), nonce, np.uint32)
    return in_maps


def kernel(**inputs):
    nc = build_fused()
    in_maps = stage_fused(inputs)
    res = run_bass_kernel_spmd(nc, in_maps, core_ids=list(range(NCORES)))
    out = np.empty((B, T, D), np.float32)
    for c in range(NCORES):
        b, half = c // 2, c % 2
        out[b, half * 2048:(half + 1) * 2048] = res.results[c]["xo"].T
    return out


def kernel_unfused(**inputs):

    ys = run_l0_attn(inputs)
    y_map = {c: ys[c] for c in range(NCORES)}
    x1s = run_l0_ffn(inputs, y_map)
    x1_map = {c: x1s[c] for c in range(NCORES)}
    l3 = run_l1_attn(inputs, x1_map)
    qr_map = {c: l3[c][0] for c in range(NCORES)}
    kvp_map = {c: l3[c][1] for c in range(NCORES)}
    outs = run_l1_moe(inputs, x1_map, qr_map, kvp_map)
    out = np.empty((B, T, D), np.float32)
    for c in range(NCORES):
        b, half = c // 2, c % 2
        out[b, half * 2048:(half + 1) * 2048] = outs[c].T
    return out


# version 18
# speedup vs baseline: 1.0181x; 1.0181x over previous
import math
from contextlib import ExitStack

import numpy as np
import ml_dtypes

import concourse.bass as bass
import concourse.mybir as mybir
from concourse.bass_utils import run_bass_kernel_spmd

F32 = mybir.dt.float32
BF16 = mybir.dt.bfloat16
AF = mybir.ActivationFunctionType
ALU = mybir.AluOpType
AX = mybir.AxisListType
bfloat16 = ml_dtypes.bfloat16

NCORES = 8
B, T, D = 4, 4096, 1024
NH, HD = 16, 64
SCALE = 0.125

ACTORS = ("tensor", "vector", "scalar", "gpsimd", "sync")

PHASE = [""]
STATS = None

_NAME_N = [0]


def _un(n):
    _NAME_N[0] += 1
    return "%s_u%d" % (n, _NAME_N[0])


class Tracker:

    NSLOT = 8

    def __init__(self, nc, ctx):
        self.nc = nc
        self.sems = {}
        self.counts = {}
        self.queues = {a: [] for a in ACTORS}
        for name in ACTORS:
            self.sems[name] = ctx.enter_context(nc.semaphore("trk_" + name))
            self.counts[name] = 0
        self.dma_sems = {}
        self.dma_counts = {}
        self.dma_rr = {}
        for name in ("sync", "gpsimd", "scalar"):
            self.dma_sems[name] = [ctx.enter_context(nc.semaphore("dma_%s%d" % (name, i)))
                                   for i in range(self.NSLOT)]
            self.dma_counts[name] = [0] * self.NSLOT
            self.dma_rr[name] = 0
        self.last_w = {}
        self.readers = {}
        self.fence_tokens = []
        self.seen_since_fence = set()
        self.waited = {a: {} for a in ACTORS}

    def fence(self):
        toks = {}
        for a, c in self.counts.items():
            if c > 0:
                toks[id(self.sems[a])] = (a, self.sems[a], c)
        for a, slots in self.dma_counts.items():
            for s, c in enumerate(slots):
                if c > 0:
                    toks[id(self.dma_sems[a][s])] = (a + "_dma", self.dma_sems[a][s], 16 * c)
        self.fence_tokens = list(toks.values())
        self.seen_since_fence = set()

    def _deps_for(self, reads, writes):
        deps = {}

        def add(tok):
            if tok is None:
                return
            a, sem, v = tok
            k = id(sem)
            cur = deps.get(k)
            if cur is None or v > cur[2]:
                deps[k] = (a, sem, v)

        for b_ in reads:
            add(self.last_w.get(b_))
        fresh = False
        for b_ in writes:
            if b_ not in self.seen_since_fence:
                fresh = True
            add(self.last_w.get(b_))
            for tok in self.readers.get(b_, ()):
                add(tok)
        if fresh:
            for tok in self.fence_tokens:
                add(tok)
        return list(deps.values())

    def emit(self, actor, fn, reads=(), writes=(), dma=False, trust_same=False,
             meta=None):
        deps = self._deps_for(reads, writes)
        waits = []
        for a, sem, v in deps:
            if a == actor and not dma and (
                    actor == "tensor" or trust_same):
                continue
            waits.append((sem, v))
        if STATS is not None:
            STATS.append((PHASE[0], actor, meta, len(waits), dma))
        if dma:
            s = self.dma_rr[actor]
            self.dma_rr[actor] = (s + 1) % self.NSLOT
            sem = self.dma_sems[actor][s]
            if self.dma_counts[actor][s] > 0:
                waits.append((sem, 16 * self.dma_counts[actor][s]))
            self.dma_counts[actor][s] += 1
            val = 16 * self.dma_counts[actor][s]
            inc = 16
        else:
            pass
        wd = self.waited[actor]
        fw = []
        for sem_, v_ in waits:
            if wd.get(id(sem_), 0) < v_:
                wd[id(sem_)] = v_
                fw.append((sem_, v_))
        waits = fw
        if dma:
            pass
        else:
            self.counts[actor] += 1
            sem = self.sems[actor]
            val = self.counts[actor]
            inc = 1

        def play(eng, waits=waits, fn=fn, sem=sem, inc=inc):
            for s_, c_ in waits:
                eng.wait_ge(s_, c_)
            fn(eng).then_inc(sem, inc)

        self.queues[actor].append(play)
        tok = (actor + "_dma" if dma else actor, sem, val)
        for b_ in writes:
            self.seen_since_fence.add(b_)
            self.last_w[b_] = tok
            self.readers[b_] = []
        for b_ in reads:
            self.readers.setdefault(b_, []).append(tok)
        return tok

    def reuse(self, newkey, *oldkeys):
        toks = []
        for ok in oldkeys:
            if ok in self.last_w:
                toks.append(self.last_w[ok])
            toks.extend(self.readers.get(ok, ()))
        self.readers.setdefault(newkey, []).extend(toks)

    def flush(self, block):
        final = []
        for a, c in self.counts.items():
            if c > 0:
                final.append((self.sems[a], c))
        for a, slots in self.dma_counts.items():
            for s, c in enumerate(slots):
                if c > 0:
                    final.append((self.dma_sems[a][s], 16 * c))
        for actor in ACTORS:
            q = self.queues[actor]
            if not q and actor != "sync":
                continue

            def body(eng, q=q, actor=actor):
                for f in q:
                    f(eng)
                if actor == "sync":
                    for s_, c_ in final:
                        eng.wait_ge(s_, c_)

            getattr(block, actor)(body)

    def mm(self, out_ap, lhsT, rhs, reads, writes, start=True, stop=True,
           tile_position=None):
        return self.emit(
            "tensor",
            lambda e: e.matmul(out_ap, lhsT, rhs, start=start, stop=stop,
                               tile_position=tile_position),
            reads=reads, writes=writes,
            meta=("mm", rhs.shape[-1], lhsT.shape[-1], str(getattr(rhs, "dtype", ""))))

    def trans(self, out_ap, in_ap, ident, reads, writes):
        return self.emit("tensor", lambda e: e.transpose(out_ap, in_ap, ident),
                         reads=reads, writes=writes,
                         meta=("trans", in_ap.shape[-1], in_ap.shape[0], ""))

    def act(self, out_ap, in_ap, func, reads, writes, bias=0.0, scale=1.0,
            accum_out=None):
        return self.emit(
            "scalar",
            lambda e: e.activation(out_ap, in_ap, func, bias=bias, scale=scale,
                                   accum_out=accum_out),
            reads=reads, writes=writes,
            meta=("act", out_ap.shape[-1], out_ap.shape[0], str(getattr(out_ap, "dtype", ""))))

    def dma(self, actor, out_ap, in_ap, reads, writes):
        import numpy as _np
        nb = 1
        for s in out_ap.shape:
            nb *= s
        return self.emit(actor, lambda e: e.dma_start(out=out_ap, in_=in_ap),
                         reads=reads, writes=writes, dma=True,
                         meta=("dma", nb, 0, str(getattr(out_ap, "dtype", ""))))

    def tt(self, out_ap, a, b_, op, reads, writes):
        return self.emit("vector",
                         lambda e: e.tensor_tensor(out=out_ap, in0=a, in1=b_, op=op),
                         reads=reads, writes=writes,
                         meta=("tt", out_ap.shape[-1], out_ap.shape[0], str(getattr(out_ap, "dtype", ""))))

    def ts(self, out_ap, a, s1, op0, reads, writes, s2=None, op1=None):
        kw = {"scalar2": s2}
        if op1 is not None:
            kw["op1"] = op1
        return self.emit(
            "vector",
            lambda e: e.tensor_scalar(out=out_ap, in0=a, scalar1=s1, op0=op0, **kw),
            reads=reads, writes=writes,
            meta=("ts", out_ap.shape[-1], out_ap.shape[0], str(getattr(out_ap, "dtype", ""))))

    def stt(self, out_ap, in0, scalar, in1, op0, op1, reads, writes):
        return self.emit(
            "vector",
            lambda e: e.scalar_tensor_tensor(out=out_ap, in0=in0, scalar=scalar,
                                             in1=in1, op0=op0, op1=op1),
            reads=reads, writes=writes,
            meta=("stt", out_ap.shape[-1], out_ap.shape[0], str(getattr(out_ap, "dtype", ""))))

    def copy(self, out_ap, in_ap, reads, writes):
        return self.emit("vector", lambda e: e.tensor_copy(out_ap, in_ap),
                         reads=reads, writes=writes,
                         meta=("copy", out_ap.shape[-1], out_ap.shape[0], str(getattr(out_ap, "dtype", ""))))

    def memset(self, ap, val, reads, writes, actor="vector"):
        return self.emit(actor, lambda e: e.memset(ap, val),
                         reads=reads, writes=writes,
                         meta=("memset", ap.shape[-1], ap.shape[0], str(getattr(ap, "dtype", ""))))

    def recip(self, out_ap, in_ap, reads, writes):
        return self.emit("vector", lambda e: e.reciprocal(out_ap, in_ap),
                         reads=reads, writes=writes,
                         meta=("recip", out_ap.shape[-1], out_ap.shape[0], str(getattr(out_ap, "dtype", ""))))

    def reduce(self, out_ap, in_ap, op, reads, writes, negate=None):
        return self.emit(
            "vector",
            lambda e: e.tensor_reduce(out_ap, in_ap, AX.X, op, negate=negate),
            reads=reads, writes=writes,
            meta=("reduce", in_ap.shape[-1], in_ap.shape[0], str(getattr(in_ap, "dtype", ""))))

    def affine_sel(self, out_ap, in_ap, pattern, compare_op, fill, base, cm,
                   reads, writes):
        return self.emit(
            "gpsimd",
            lambda e: e.affine_select(out_ap, in_ap, pattern=pattern,
                                      compare_op=compare_op, fill=fill, base=base,
                                      channel_multiplier=cm),
            reads=reads, writes=writes,
            meta=("asel", out_ap.shape[-1], out_ap.shape[0], str(getattr(out_ap, "dtype", ""))))


def build_identity(tk, ident_ap, key):
    tk.memset(ident_ap, 0.0, reads=(), writes=(key,), actor="gpsimd")
    tk.affine_sel(ident_ap, ident_ap, [[-1, ident_ap.shape[1]]], ALU.not_equal,
                  1.0, 0, 1, reads=(key,), writes=(key,))


def emit_ln(tk, nchunks, W, xsl, ones_cf, ones_row, sq0, sq1, rows,
            eps_ap, psS0, psS1, rb, mb, kx, ones_b, kb0="b0", kb1="b1"):
    psA = psS0[0:1, 0:W]
    psB = psS1[0:1, 0:W]
    for kc in range(nchunks):
        xap = xsl(kc)
        tk.mm(psA, ones_b, xap, start=(kc == 0), stop=(kc == nchunks - 1),
              reads=(kx, "ones_b"), writes=(kb0,))
        sq = (sq0, sq1)[kc % 2][:, 0:W]
        ksq = "sq" + str(kc % 2)
        tk.act(sq, xap, AF.Square, reads=(kx,), writes=(ksq,))
        tk.mm(psB, ones_cf, sq, start=(kc == 0), stop=(kc == nchunks - 1),
              reads=(ksq, "ones_cf"), writes=(kb1,))
    mu = rows[:, 0 * W:1 * W]
    ex2 = rows[:, 1 * W:2 * W]
    var = rows[:, 2 * W:3 * W]
    rstd = rows[:, 3 * W:4 * W]
    mur = rows[:, 4 * W:5 * W]
    n = nchunks * 128
    tk.ts(mu, psA, 1.0 / n, ALU.mult, reads=(kb0,), writes=("mu",))
    tk.ts(ex2, psB, 1.0 / n, ALU.mult, reads=(kb1,), writes=("ex2",))
    tk.tt(var, mu, mu, ALU.mult, reads=("mu",), writes=("var",))
    tk.tt(var, ex2, var, ALU.subtract, reads=("ex2", "var"), writes=("var",))
    tk.act(var, var, AF.Ln, bias=eps_ap, reads=("var", "eps"), writes=("var",))
    tk.act(rstd, var, AF.Exp, scale=-0.5, reads=("var",), writes=("rstd",))
    tk.stt(mur, mu, -1.0, rstd, ALU.mult, ALU.mult,
           reads=("mu", "rstd"), writes=("mur",))
    tk.mm(psS0[:, 0:W], ones_row, rstd, reads=("rstd", "ones_row"), writes=(kb0,))
    tk.mm(psS1[:, 0:W], ones_row, mur, reads=("mur", "ones_row"), writes=(kb1,))
    tk.copy(rb[:, 0:W], psS0[:, 0:W], reads=(kb0,), writes=("rb",))
    tk.copy(mb[:, 0:W], psS1[:, 0:W], reads=(kb1,), writes=("mb",))
    for kc in range(nchunks):
        xap = xsl(kc)
        tk.tt(xap, xap, rb[:, 0:W], ALU.mult, reads=(kx, "rb"), writes=(kx,))
        tk.tt(xap, xap, mb[:, 0:W], ALU.add, reads=(kx, "mb"), writes=(kx,))



def emit_l0_attn(nc, tk, xT, wqk, wk2, bqk, bk2, epsc, y_out,
                 dbg=None):
    debug = dbg is not None
    if debug:
        dbg_q, dbg_k, dbg_ka, dbg_xn = dbg

    NT = T // 512
    CL = T


    with ExitStack() as ctx:

        def sbL(name, shape, dt=BF16):
            return ctx.enter_context(nc.sbuf_tensor(_un(name), shape, dt, side="left"))

        def psum(name, shape, dt=F32):
            return ctx.enter_context(nc.psum_tensor(_un(name), shape, dt))

        q_sb = sbL("q_sb", [128, 4 * T])
        k_sb = sbL("k_sb", [128, 4 * T])
        ka_sb = sbL("ka_sb", [128, 4 * T])
        xn_lo = sbL("xn_lo", [128, 4 * CL + 128])
        bqk_sb = sbL("bqk_sb", [128, 8], F32)
        bk2s_sb = sbL("bk2s_sb", [128, 4], F32)
        ident = sbL("ident", [128, 128])
        identf = sbL("identf", [128, 128], F32)
        ones_b = sbL("ones_b", [128, 1])
        ones_cf = sbL("ones_cf", [128, 1], F32)
        ones_row = sbL("ones_row", [1, 128], F32)
        eps_sb = sbL("eps_sb", [1, 1], F32)

        psS0 = psum("psS0", [128, 512])
        psS1 = psum("psS1", [128, 512])
        psYa = psum("psYa", [65, 512])
        psYb = psum("psYb", [65, 512])
        psYM = psum("psYM", [64, 512])
        psMa = psum("psMa", [128, 128])
        psT = psum("psT", [128, 128], BF16)
        psState = psum("psState", [128, 512])

        if True:
            tk.dma("sync", bqk_sb[:], bqk, reads=(), writes=("bqk",))
            tk.dma("sync", bk2s_sb[:], bk2, reads=(), writes=("bk2s",))
            tk.dma("sync", eps_sb[:], epsc, reads=(), writes=("eps",))
            tk.ts(bk2s_sb[:], bk2s_sb[:], SCALE * 0.02, ALU.mult,
                  reads=("bk2s",), writes=("bk2s",))
            build_identity(tk, ident[:], "id")
            build_identity(tk, identf[:], "idf32")
            tk.memset(ones_b[:], 1.0, reads=(), writes=("ones_b",))
            tk.memset(ones_cf[:], 1.0, reads=(), writes=("ones_cf",))
            tk.memset(ones_row[:], 1.0, reads=(), writes=("ones_row",))

            with ExitStack() as phase12:
                def sbR(name, shape, dt=BF16):
                    return phase12.enter_context(
                        nc.sbuf_tensor(_un(name), shape, dt, side="right"))

                wqk_sb = sbR("wqk_sb", [128, 8 * 1024])
                wk2_sb = sbR("wk2_sb", [128, 8 * 512])
                xn_hi = sbR("xn_hi", [128, 4 * CL])
                rowsc = sbR("rowsc", [1, 5 * 512], F32)
                sq0 = sbR("sq0", [128, 512], F32)
                sq1 = sbR("sq1", [128, 512], F32)
                rb_b = sbR("rb_b", [128, 512])
                mb_b = sbR("mb_b", [128, 512])

                tk.dma("gpsimd", wqk_sb[:].rearrange("p (c n) -> p c n", c=8),
                       wqk.rearrange("c p n -> p c n"),
                       reads=(), writes=("wqk",))
                tk.dma("gpsimd", wk2_sb[:].rearrange("p (c n) -> p c n", c=8),
                       wk2.rearrange("c p n -> p c n"),
                       reads=(), writes=("wk2",))

                def xn_sl(kc, c0, c1):
                    if kc < 4:
                        return xn_lo[:, kc * CL + c0: kc * CL + c1]
                    return xn_hi[:, (kc - 4) * CL + c0: (kc - 4) * CL + c1]

                for kc in range(8):
                    tk.dma("gpsimd", xn_sl(kc, 0, T),
                           xT[kc * 128:(kc + 1) * 128, :], reads=(), writes=("xn",))
                tk.memset(xn_lo[:, 4 * CL:], 0.0, reads=(), writes=("xn",))
                for t in range(NT):
                    c0, c1 = t * 512, (t + 1) * 512
                    emit_ln(tk, 8, 512, lambda kc: xn_sl(kc, c0, c1),
                            ones_cf[:], ones_row[:], sq0[:], sq1[:], rowsc[:],
                            eps_sb[0:1, 0:1], psS0, psS1, rb_b[:], mb_b[:],
                            "xn", ones_b[:])

                projs = (
                    (q_sb, "qo", wqk_sb, "wqk", 1024, bqk_sb, "bqk", None, 1.0, 0),
                    (k_sb, "ko", wqk_sb, "wqk", 1024, bqk_sb, "bqk", None, 1.0, 4),
                    (ka_sb, "kao", wk2_sb, "wk2", 512, bk2s_sb, "bk2s", AF.Sigmoid,
                     SCALE * 0.02, 0),
                )
                pcount = 0
                for dst, dkey, wsb, wkey, wstride, bias_sb, bias_key, func, scl, boc in projs:
                    for oc in range(4):
                        oc_g = boc + oc
                        for t in range(NT):
                            c0, c1 = t * 512, (t + 1) * 512
                            ps = (psS0, psS1)[pcount % 2]
                            bank = "b" + str(pcount % 2)
                            pcount += 1
                            for kc in range(8):
                                lhsT = wsb[:, kc * wstride + oc_g * 128:
                                           kc * wstride + oc_g * 128 + 128]
                                tk.mm(ps[:, :], lhsT, xn_sl(kc, c0, c1),
                                      start=(kc == 0), stop=(kc == 7),
                                      reads=(wkey, "xn"), writes=(bank,))
                            if func is None:
                                tk.ts(dst[:, oc * T + c0: oc * T + c1], ps[:, :],
                                      bias_sb[:, oc_g:oc_g + 1], ALU.add,
                                      reads=(bank, bias_key), writes=(dkey,))
                            else:
                                tk.act(dst[:, oc * T + c0: oc * T + c1], ps[:, :], func,
                                       bias=bias_sb[:, oc_g:oc_g + 1], scale=scl,
                                       reads=(bank, bias_key), writes=(dkey,))

            with ExitStack() as phase34:
                def sbR2(name, shape, dt=BF16):
                    return phase34.enter_context(
                        nc.sbuf_tensor(_un(name), shape, dt, side="right"))

                v_ext = sbR2("v_ext", [128, 32 * 520])
                maskS = sbR2("maskS", [128, 4 * 512])
                maskM = sbR2("maskM", [128, 128], F32)
                S_sb = sbR2("S_sb", [128, 8 * 64])
                Sf_sb = sbR2("Sf_sb", [128, 8 * 64], F32)
                qa8 = sbR2("qa8", [128, 4 * 512])
                qa_t = sbR2("qa_t", [128, 512])
                etm8 = sbR2("etm8", [128, 8 * 64])
                katm8 = sbR2("katm8", [128, 4 * 128])
                vtm8 = sbR2("vtm8", [128, 4 * 128])
                mascr8 = sbR2("mascr8", [128, 8 * 128])
                ya8 = sbR2("ya8", [64, 8 * 512])
                ymj8 = sbR2("ymj8", [64, 8 * 128])
                rb64 = sbR2("rb64", [64, 512], F32)
                rd_sb = sbR2("rd_sb", [1, 512], F32)
                pt_sb = sbR2("pt_sb", [128, 6 * 512])
                names = ["v_ext", "maskS", "maskM", "qa_t", "rb64", "rd",
                         "pt0", "pt1", "pt2", "pt3", "pt4", "pt5"]
                for h in range(8):
                    names += ["S_sb%d" % h, "Sf%d" % h, "qa%d" % (h // 2), "etm%d" % h,
                              "katm%d" % (h // 2), "vtm%d" % (h // 2),
                              "mascr%d" % h, "ya%d" % h, "ymj%d" % h]
                for nm in names:
                    tk.reuse(nm, "wqk", "wk2", "xn", "sq0", "sq1", "rb", "mb",
                             "mu", "ex2", "var", "rstd", "mur")
                tk.memset(maskS[:], 0.0, reads=(), writes=("maskS",), actor="gpsimd")
                for r in range(4):
                    tk.affine_sel(maskS[:, r * 512:(r + 1) * 512],
                                  maskS[:, r * 512:(r + 1) * 512],
                                  [[1, 512]], ALU.is_ge, -1.0e5, -128 * r, -1,
                                  reads=("maskS",), writes=("maskS",))
                tk.memset(maskM[:], 1.0, reads=(), writes=("maskM",), actor="gpsimd")
                tk.affine_sel(maskM[:], maskM[:], [[1, 128]], ALU.is_ge, 0.0, -1, -1,
                              reads=("maskM",), writes=("maskM",))

                for tb in range(32):
                    base = tb * 520
                    tk.memset(v_ext[:, base + 64: base + 520: 65], 1.0,
                              reads=(), writes=("v_ext",))
                    for dc in range(4):
                        vsrc = xn_lo[:, dc * CL + tb * 128: dc * CL + (tb + 1) * 128]
                        if dc % 2 == 0:
                            psx, bankx = psT, "bT"
                            tk.trans(psx[:, :], vsrc, ident[:],
                                     reads=("xn", "id"), writes=(bankx,))
                        else:
                            psx, bankx = psMa, "bMa"
                            tk.mm(psx[:, :], vsrc, ident[:],
                                  reads=("xn", "id"), writes=(bankx,))
                        tk.copy(v_ext[:, base + (2 * dc) * 65: base + (2 * dc) * 65 + 64],
                                psx[:, 0:64], reads=(bankx,), writes=("v_ext",))
                        tk.copy(v_ext[:, base + (2 * dc + 1) * 65:
                                      base + (2 * dc + 1) * 65 + 64],
                                psx[:, 64:128], reads=(bankx,), writes=("v_ext",))

                if debug:
                    tk.dma("sync", dbg_q, q_sb[:], reads=("qo",), writes=())
                    tk.dma("sync", dbg_k, k_sb[:], reads=("ko",), writes=())
                    tk.dma("sync", dbg_ka, ka_sb[:], reads=("kao",), writes=())
                    tk.dma("sync", dbg_xn, xn_lo[:], reads=("xn",), writes=())
                for qt in range(NT):
                    q0, q1 = qt * 512, (qt + 1) * 512
                    nkb = 4 * qt + 4
                    pend = []
                    pend_n = [None]

                    def flushpv(all_=False):
                        while len(pend) > (0 if all_ else 1):
                            pend.pop(0)()

                    def flushnorm():
                        if pend_n[0] is not None:
                            pend_n[0]()
                            pend_n[0] = None

                    for h in range(8):
                        r0 = (h % 2) * 64
                        cb = (h // 2) * T
                        qk = "qa%d" % (h // 2)
                        psY, yk_b = ((psYa, "bYa"), (psYb, "bYb"))[h % 2]
                        if h % 2 == 0:
                            tk.ts(qa_t[:], q_sb[:, cb + q0:cb + q1], SCALE * 0.02,
                                  ALU.mult, reads=("qo",), writes=("qa_t",))
                            tk.stt(qa8[:, (h // 2) * 512:(h // 2 + 1) * 512],
                                   q_sb[:, cb + q0:cb + q1], SCALE,
                                   qa_t[:], ALU.mult, ALU.min,
                                   reads=("qo", "qa_t"), writes=(qk,))
                        for kb in range(nkb):
                            ps = (psS0, psS1)[kb % 2]
                            bank = "b" + str(kb % 2)
                            slot = (h % 2) * 3 + kb % 3
                            ptsl = pt_sb[:, slot * 512:(slot + 1) * 512]
                            ptk = "pt%d" % slot
                            r = kb - (nkb - 4)
                            tk.mm(ps[:, :],
                                  k_sb[r0:r0 + 64, cb + kb * 128:cb + (kb + 1) * 128],
                                  q_sb[r0:r0 + 64, cb + q0:cb + q1],
                                  stop=(r < 0),
                                  reads=("ko", "qo"), writes=(bank,))
                            if r >= 0:
                                tk.mm(ps[:, :], ident[:],
                                      maskS[:, r * 512:(r + 1) * 512],
                                      start=False, stop=True,
                                      reads=("id", "maskS"), writes=(bank,))
                            tk.act(ptsl, ps[:, :], AF.Exp, scale=SCALE,
                                   reads=(bank,), writes=(ptk,))
                            flushpv()
                            if kb == 1:
                                flushnorm()

                            def pv(h=h, kb=kb, ptsl=ptsl, ptk=ptk, nkb=nkb,
                                   psY=psY, yk_b=yk_b):
                                tk.mm(psY[:, :],
                                      v_ext[:, kb * 520 + h * 65: kb * 520 + h * 65 + 65],
                                      ptsl, start=(kb == 0), stop=(kb == nkb - 1),
                                      reads=("v_ext", ptk), writes=(yk_b,))
                            pend.append(pv)
                        flushpv(all_=True)
                        tk.copy(rd_sb[:], psY[64:65, :], reads=(yk_b,), writes=("rd",))

                        def norm(h=h, psY=psY, yk_b=yk_b):
                            tk.mm(psYM[0:64, :], ones_row[0:1, 0:64], rd_sb[:],
                                  reads=("rd", "ones_row"), writes=("bYM",))
                            tk.recip(rb64[:], psYM[0:64, :],
                                     reads=("bYM",), writes=("rb64",))
                            tk.stt(ya8[:, h * 512:(h + 1) * 512], psY[0:64, :], 1.0,
                                   rb64[:], ALU.mult, ALU.mult,
                                   reads=(yk_b, "rb64"), writes=("ya%d" % h,))
                        pend_n[0] = norm
                    flushnorm()
                    for j in range(4):
                        i = 4 * qt + j
                        for pr in range(4):
                            sh0 = pr * CL + i * 128 + 1
                            psx, bankx = ((psMa, "bMa"), (psT, "bT"))[pr % 2]
                            if pr % 2 == 0:
                                tk.mm(psx[:, :], xn_lo[:, sh0: sh0 + 128], ident[:],
                                      reads=("xn", "id"), writes=(bankx,))
                            else:
                                tk.trans(psx[:, :], xn_lo[:, sh0: sh0 + 128],
                                         ident[:], reads=("xn", "id"), writes=(bankx,))
                            tk.copy(vtm8[:, pr * 128:(pr + 1) * 128], psx[:, 0:128],
                                    reads=(bankx,), writes=("vtm%d" % pr,))
                            if i < 31:
                                ksrc = ka_sb[:, pr * T + i * 128:pr * T + (i + 1) * 128]
                                if pr % 2 == 0:
                                    tk.mm(psx[:, :], ksrc, ident[:],
                                          reads=("kao", "id"), writes=(bankx,))
                                else:
                                    tk.trans(psx[:, :], ksrc, ident[:],
                                             reads=("kao", "id"), writes=(bankx,))
                                tk.copy(katm8[:, pr * 128:(pr + 1) * 128],
                                        psx[:, 0:128],
                                        reads=(bankx,), writes=("katm%d" % pr,))
                        for h in range(8):
                            r0 = (h % 2) * 64
                            pr = h // 2
                            qk = "qa%d" % pr
                            vk, kk = "vtm%d" % pr, "katm%d" % pr
                            ek, mk = "etm%d" % h, "mascr%d" % h
                            sk, yk, jk = "S_sb%d" % h, "ya%d" % h, "ymj%d" % h
                            vsl = vtm8[:, pr * 128 + r0: pr * 128 + r0 + 64]
                            ksl = katm8[:, pr * 128 + r0: pr * 128 + r0 + 64]
                            esl = etm8[:, h * 64:(h + 1) * 64]
                            msl = mascr8[:, h * 128:(h + 1) * 128]
                            tk.trans(psT[:, 0:64], ya8[:, h * 512 + j * 128:
                                                       h * 512 + (j + 1) * 128],
                                     ident[0:64, 0:64], reads=(yk, "id"), writes=("bT",))
                            tk.tt(esl, vsl, psT[:, 0:64], ALU.subtract,
                                  reads=(vk, "bT"), writes=(ek,))
                            tk.mm(psMa[:, :],
                                  ka_sb[r0:r0 + 64, pr * T + i * 128:
                                        pr * T + (i + 1) * 128],
                                  qa8[r0:r0 + 64, pr * 512 + j * 128:
                                      pr * 512 + (j + 1) * 128],
                                  reads=("kao", qk), writes=("bMa",))
                            tk.stt(msl, psMa[:, :], 1.0, maskM[:], ALU.mult,
                                   ALU.mult, reads=("bMa", "maskM"), writes=(mk,))
                            have_state = i > 0
                            if have_state:
                                tk.mm(psYM[0:64, 0:128], S_sb[r0:r0 + 64, h * 64:(h + 1) * 64],
                                      qa8[r0:r0 + 64, pr * 512 + j * 128:
                                          pr * 512 + (j + 1) * 128],
                                      start=True, stop=False,
                                      reads=(sk, qk), writes=("bYM",))
                            tk.mm(psYM[0:64, 0:128], esl, msl,
                                  start=(not have_state), stop=True,
                                  reads=(ek, mk), writes=("bYM",))
                            tk.copy(ymj8[:, h * 128:(h + 1) * 128], psYM[0:64, 0:128],
                                    reads=("bYM",), writes=(jk,))
                            tk.tt(ya8[:, h * 512 + j * 128: h * 512 + (j + 1) * 128],
                                  ya8[:, h * 512 + j * 128: h * 512 + (j + 1) * 128],
                                  ymj8[:, h * 128:(h + 1) * 128], ALU.add,
                                  reads=(yk, jk), writes=(yk,))
                            if i < 31:
                                sfk = "Sf%d" % h
                                if have_state:
                                    tk.mm(psState[r0:r0 + 64, h * 64:(h + 1) * 64],
                                          identf[r0:r0 + 64, r0:r0 + 64],
                                          Sf_sb[r0:r0 + 64, h * 64:(h + 1) * 64],
                                          start=True, stop=False,
                                          tile_position=(r0, r0),
                                          reads=("idf32", sfk), writes=("bSt",))
                                tk.mm(psState[r0:r0 + 64, h * 64:(h + 1) * 64],
                                      ksl, esl, start=(not have_state), stop=True,
                                      tile_position=(0, r0),
                                      reads=(kk, ek), writes=("bSt",))
                                tk.copy(Sf_sb[r0:r0 + 64, h * 64:(h + 1) * 64],
                                        psState[r0:r0 + 64, h * 64:(h + 1) * 64],
                                        reads=("bSt",), writes=(sfk,))
                                tk.copy(S_sb[r0:r0 + 64, h * 64:(h + 1) * 64],
                                        Sf_sb[r0:r0 + 64, h * 64:(h + 1) * 64],
                                        reads=(sfk,), writes=(sk,))
                    for h in range(8):
                        tk.dma("sync", y_out[h * 64:(h + 1) * 64, q0:q1],
                               ya8[:, h * 512:(h + 1) * 512],
                               reads=("ya%d" % h,), writes=())

    return nc





def build_l0_attn(debug=False):
    nc = bass.Bass("TRN2", target_bir_lowering=False)
    xT = nc.dram_tensor("xT", [D, T], F32, kind="ExternalInput").ap()
    wqk = nc.dram_tensor("wqk", [8, 128, 1024], F32, kind="ExternalInput").ap()
    wk2 = nc.dram_tensor("wk2", [8, 128, 512], F32, kind="ExternalInput").ap()
    bqk = nc.dram_tensor("bqk", [128, 8], F32, kind="ExternalInput").ap()
    bk2 = nc.dram_tensor("bk2", [128, 4], F32, kind="ExternalInput").ap()
    epsc = nc.dram_tensor("epsc", [1, 1], F32, kind="ExternalInput").ap()
    y_out = nc.dram_tensor("y", [512, T], BF16, kind="ExternalOutput").ap()
    dbg = None
    if debug:
        dbg = (nc.dram_tensor("dbg_q", [128, 4 * T], BF16, kind="ExternalOutput").ap(),
               nc.dram_tensor("dbg_k", [128, 4 * T], BF16, kind="ExternalOutput").ap(),
               nc.dram_tensor("dbg_ka", [128, 4 * T], BF16, kind="ExternalOutput").ap(),
               nc.dram_tensor("dbg_xn", [128, 4 * T + 128], BF16, kind="ExternalOutput").ap())
    with ExitStack() as ctx0:
        tk = Tracker(nc, ctx0)
        with nc.Block() as block:
            emit_l0_attn(nc, tk, xT, wqk, wk2, bqk, bk2, epsc, y_out, dbg=dbg)
            tk.flush(block)
    return nc

def emit_l0_ffn(nc, tk, xT, yb, wproj, bproj, w1, w2, h1, h2, w3,
                s1, s2, sh1, sh2, s3, epsc, x1_out, dbg=None, yload=None,
                dbg_wpx=None):
    debug = dbg is not None
    if debug:
        dbg_attn, dbg_ffn, dbg_wp, dbg_yt, dbg_raw = dbg
    TO = 2048

    NBT = 2
    BT = 1024


    with ExitStack() as ctx:

        def sb(name, shape, dt=BF16):
            return ctx.enter_context(nc.sbuf_tensor(_un(name), shape, dt))

        def psum(name, shape, dt=F32):
            return ctx.enter_context(nc.psum_tensor(_un(name), shape, dt))

        m_sb = sb("m_sb", [128, 32 * BT])
        g_sb = sb("g_sb", [128, 32 * BT])
        xn_t = sb("xn_t", [128, 8 * BT])
        wproj_sb = sb("wproj_sb", [128, 8 * 1024])
        wA = sb("wA", [128, 32 * 128])
        wB = sb("wB", [128, 32 * 128])
        s1_sb = sb("s1_sb", [128, 32], F32)
        s2_sb = sb("s2_sb", [128, 32], F32)
        sh1_sb = sb("sh1_sb", [128, 32], F32)
        sh2_sb = sb("sh2_sb", [128, 32], F32)
        s3_sb = sb("s3_sb", [128, 8], F32)
        bproj_sb = sb("bproj_sb", [128, 8], F32)
        ones_b = sb("ones_b", [128, 1])
        ones_cf = sb("ones_cf", [128, 1], F32)
        ones_row = sb("ones_row", [1, 128], F32)
        eps_sb = sb("eps_sb", [1, 1], F32)
        rowsc = sb("rowsc", [1, 5 * 512], F32)
        sq0 = sb("sq0", [128, 512], F32)
        sq1 = sb("sq1", [128, 512], F32)
        rb_b = sb("rb_b", [128, 512])
        mb_b = sb("mb_b", [128, 512])
        sl_scr = sb("sl_scr", [128, 2 * 512])
        b2_scr = sb("b2_scr", [128, 2 * 512])
        fo_scr = sb("fo_scr", [128, 512], F32)
        po_scr = sb("po_scr", [128, 512], F32)
        xf_scr = sb("xf_scr", [128, 512], F32)
        x1_scr = sb("x1_scr", [128, 512], F32)

        psb = [psum("psb%d" % i, [128, 512]) for i in range(4)]

        if True:
            tk.dma("sync", s1_sb[:], s1, reads=(), writes=("s1",))
            tk.dma("sync", s2_sb[:], s2, reads=(), writes=("s2",))
            tk.dma("sync", sh1_sb[:], sh1, reads=(), writes=("sh1",))
            tk.dma("sync", sh2_sb[:], sh2, reads=(), writes=("sh2",))
            tk.dma("sync", s3_sb[:], s3, reads=(), writes=("s3",))
            tk.dma("sync", bproj_sb[:], bproj, reads=(), writes=("bp",))
            tk.dma("sync", eps_sb[:], epsc, reads=(), writes=("eps",))
            tk.ts(bproj_sb[:], bproj_sb[:], 2.0, ALU.mult, reads=("bp",), writes=("bp",))
            tk.memset(ones_b[:], 1.0, reads=(), writes=("ones_b",))
            tk.memset(ones_cf[:], 1.0, reads=(), writes=("ones_cf",))
            tk.memset(ones_row[:], 1.0, reads=(), writes=("ones_row",))
            tk.dma("gpsimd", wproj_sb[:].rearrange("p (c n) -> p c n", c=8),
                   wproj.rearrange("c p n -> p c n"), reads=(), writes=("wproj",))
            if dbg_wpx is not None:
                tk.dma("sync", dbg_wpx[0], wproj_sb[:], reads=("wproj",), writes=())

            wslots = [(wA, "wA"), (wB, "wB")]
            y_t = xn_t

            for bt in range(NBT):
                t0 = bt * BT
                tk.reuse("xn", "yt")
                for kc in range(8):
                    tk.dma("gpsimd", xn_t[:, kc * BT:(kc + 1) * BT],
                           xT[kc * 128:(kc + 1) * 128, t0:t0 + BT],
                           reads=(), writes=("xn",))
                for st in range(2):
                    c0 = st * 512
                    emit_ln(tk, 8, 512,
                            lambda kc: xn_t[:, kc * BT + c0: kc * BT + c0 + 512],
                            ones_cf[:], ones_row[:], sq0[:], sq1[:], rowsc[:],
                            eps_sb[0:1, 0:1], psb[0], psb[1], rb_b[:], mb_b[:],
                            "xn", ones_b[:], kb0="pb0", kb1="pb1")
                for hc in range(32):
                    sa, ka_ = wslots[0]
                    sb_, kb_ = wslots[1]
                    tk.dma("gpsimd", sa[:, 0:8 * 128], w1[hc], reads=(), writes=(ka_,))
                    tk.dma("gpsimd", sb_[:, 0:8 * 128], w2[hc], reads=(), writes=(kb_,))
                    for st in range(2):
                        c0 = st * 512
                        pA = psb[2 * st]
                        pB = psb[2 * st + 1]
                        bkA, bkB = "pb%d" % (2 * st), "pb%d" % (2 * st + 1)
                        for kc in range(8):
                            tk.mm(pA[:, :], sa[:, kc * 128:(kc + 1) * 128],
                                  xn_t[:, kc * BT + c0: kc * BT + c0 + 512],
                                  start=(kc == 0), stop=(kc == 7),
                                  reads=(ka_, "xn"), writes=(bkA,))
                        for kc in range(8):
                            tk.mm(pB[:, :], sb_[:, kc * 128:(kc + 1) * 128],
                                  xn_t[:, kc * BT + c0: kc * BT + c0 + 512],
                                  start=(kc == 0), stop=(kc == 7),
                                  reads=(kb_, "xn"), writes=(bkB,))
                        tk.act(sl_scr[:, c0:c0 + 512], pA[:, :], AF.Silu,
                               scale=s1_sb[:, hc:hc + 1],
                               reads=(bkA, "s1"), writes=("sl",))
                        tk.ts(b2_scr[:, c0:c0 + 512], pB[:, :], s2_sb[:, hc:hc + 1],
                              ALU.mult, reads=(bkB, "s2"), writes=("b2",))
                        tk.tt(m_sb[:, hc * BT + c0: hc * BT + c0 + 512],
                              sl_scr[:, c0:c0 + 512], b2_scr[:, c0:c0 + 512],
                              ALU.mult, reads=("sl", "b2"), writes=("m",))

                if dbg_wpx is not None and bt == 0:
                    tk.dma("sync", dbg_wpx[1], wproj_sb[:], reads=("wproj", "m"), writes=())
                for oc in range(32):
                    sa, ka_ = wslots[oc % 2]
                    tk.dma("gpsimd", sa[:], h1[oc], reads=(), writes=(ka_,))
                    for st in range(2):
                        c0 = st * 512
                        pA = psb[(oc + st) % 4]
                        bkA = "pb%d" % ((oc + st) % 4)
                        for kc in range(32):
                            tk.mm(pA[:, :], sa[:, kc * 128:(kc + 1) * 128],
                                  m_sb[:, kc * BT + c0: kc * BT + c0 + 512],
                                  start=(kc == 0), stop=(kc == 31),
                                  reads=(ka_, "m"), writes=(bkA,))
                        tk.act(g_sb[:, oc * BT + c0: oc * BT + c0 + 512], pA[:, :],
                               AF.Gelu, scale=sh1_sb[:, oc:oc + 1],
                               reads=(bkA, "sh1"), writes=("g",))

                if dbg_wpx is not None and bt == 0:
                    tk.dma("sync", dbg_wpx[2], wproj_sb[:], reads=("wproj", "g"), writes=())
                for oc in range(32):
                    sa, ka_ = wslots[oc % 2]
                    tk.dma("gpsimd", sa[:], h2[oc], reads=(), writes=(ka_,))
                    for st in range(2):
                        c0 = st * 512
                        pA = psb[(oc + st) % 4]
                        bkA = "pb%d" % ((oc + st) % 4)
                        for kc in range(32):
                            tk.mm(pA[:, :], sa[:, kc * 128:(kc + 1) * 128],
                                  g_sb[:, kc * BT + c0: kc * BT + c0 + 512],
                                  start=(kc == 0), stop=(kc == 31),
                                  reads=(ka_, "g"), writes=(bkA,))
                        tk.act(m_sb[:, oc * BT + c0: oc * BT + c0 + 512], pA[:, :],
                               AF.Gelu, scale=sh2_sb[:, oc:oc + 1],
                               reads=(bkA, "sh2"), writes=("m",))

                tk.reuse("yt", "xn")
                for kc in range(8):
                    if yload is not None:
                        yload(tk, y_t[:, kc * BT:(kc + 1) * BT], kc, t0, BT)
                    else:
                        tk.dma("sync", y_t[:, kc * BT:(kc + 1) * BT],
                               yb[kc * 128:(kc + 1) * 128, t0:t0 + BT],
                               reads=(), writes=("yt",))
                for oc in range(8):
                    sa, ka_ = wslots[oc % 2]
                    tk.dma("gpsimd", sa[:], w3[oc], reads=(), writes=(ka_,))
                    for st in range(2):
                        c0 = st * 512
                        pA = psb[(2 * oc + st) % 3]
                        bkA = "pb%d" % ((2 * oc + st) % 3)
                        pB = psb[3]
                        bkB = "pb3"
                        for kc in range(32):
                            tk.mm(pA[:, :], sa[:, kc * 128:(kc + 1) * 128],
                                  m_sb[:, kc * BT + c0: kc * BT + c0 + 512],
                                  start=(kc == 0), stop=(kc == 31),
                                  reads=(ka_, "m"), writes=(bkA,))
                        for kc in range(8):
                            tk.mm(pB[:, :], wproj_sb[:, kc * 1024 + oc * 128:
                                                     kc * 1024 + oc * 128 + 128],
                                  y_t[:, kc * BT + c0: kc * BT + c0 + 512],
                                  start=(kc == 0), stop=(kc == 7),
                                  reads=("wproj", "yt"), writes=(bkB,))
                        tk.ts(fo_scr[:], pA[:, :], s3_sb[:, oc:oc + 1], ALU.mult,
                              reads=(bkA, "s3"), writes=("fo",))
                        if debug and bt == 0 and oc == 0 and st == 0:
                            tk.copy(x1_scr[:], pB[:, :], reads=(bkB,), writes=("x1s",))
                            tk.dma("sync", dbg_raw, x1_scr[:], reads=("x1s",), writes=())
                            tk.dma("sync", dbg_wp, wproj_sb[:], reads=("wproj",), writes=())
                            tk.dma("sync", dbg_yt, y_t[:], reads=("yt",), writes=())
                        tk.ts(po_scr[:], pB[:, :], bproj_sb[:, oc:oc + 1], ALU.add,
                              reads=(bkB, "bp"), writes=("po",))
                        tk.dma("sync", xf_scr[:, 0:512],
                               xT[oc * 128:(oc + 1) * 128, t0 + c0: t0 + c0 + 512],
                               reads=(), writes=("xf",))
                        if debug:
                            tk.dma("sync", dbg_attn[oc * 128:(oc + 1) * 128,
                                                    t0 + c0: t0 + c0 + 512],
                                   po_scr[:], reads=("po",), writes=())
                            tk.dma("sync", dbg_ffn[oc * 128:(oc + 1) * 128,
                                                   t0 + c0: t0 + c0 + 512],
                                   fo_scr[:], reads=("fo",), writes=())
                        tk.tt(fo_scr[:], fo_scr[:], po_scr[:], ALU.add,
                              reads=("fo", "po"), writes=("fo",))
                        tk.tt(x1_scr[:], fo_scr[:], xf_scr[:, 0:512], ALU.add,
                              reads=("fo", "xf"), writes=("x1s",))
                        tk.dma("sync", x1_out[oc * 128:(oc + 1) * 128, t0 + c0: t0 + c0 + 512],
                               x1_scr[:], reads=("x1s",), writes=())

    return nc




def build_l0_ffn(debug=False):
    nc = bass.Bass("TRN2", target_bir_lowering=False)
    TO = 2048
    xT = nc.dram_tensor("xT", [D, TO], F32, kind="ExternalInput").ap()
    yb = nc.dram_tensor("yb", [D, TO], BF16, kind="ExternalInput").ap()
    wproj = nc.dram_tensor("wproj", [8, 128, 1024], F32, kind="ExternalInput").ap()
    bproj = nc.dram_tensor("bproj", [128, 8], F32, kind="ExternalInput").ap()
    w1 = nc.dram_tensor("w1", [32, 128, 8, 128], F32, kind="ExternalInput").ap()
    w2 = nc.dram_tensor("w2", [32, 128, 8, 128], F32, kind="ExternalInput").ap()
    h1 = nc.dram_tensor("h1", [32, 128, 32, 128], F32, kind="ExternalInput").ap()
    h2 = nc.dram_tensor("h2", [32, 128, 32, 128], F32, kind="ExternalInput").ap()
    w3 = nc.dram_tensor("w3", [8, 128, 32, 128], F32, kind="ExternalInput").ap()
    s1 = nc.dram_tensor("s1", [128, 32], F32, kind="ExternalInput").ap()
    s2 = nc.dram_tensor("s2", [128, 32], F32, kind="ExternalInput").ap()
    sh1 = nc.dram_tensor("sh1", [128, 32], F32, kind="ExternalInput").ap()
    sh2 = nc.dram_tensor("sh2", [128, 32], F32, kind="ExternalInput").ap()
    s3 = nc.dram_tensor("s3", [128, 8], F32, kind="ExternalInput").ap()
    epsc = nc.dram_tensor("epsc", [1, 1], F32, kind="ExternalInput").ap()
    x1_out = nc.dram_tensor("x1", [D, TO], F32, kind="ExternalOutput").ap()
    dbg = None
    if debug:
        dbg = (nc.dram_tensor("dbg_attn", [D, TO], F32, kind="ExternalOutput").ap(),
               nc.dram_tensor("dbg_ffn", [D, TO], F32, kind="ExternalOutput").ap(),
               nc.dram_tensor("dbg_wp", [128, 8192], BF16, kind="ExternalOutput").ap(),
               nc.dram_tensor("dbg_yt", [128, 8192], BF16, kind="ExternalOutput").ap(),
               nc.dram_tensor("dbg_raw", [128, 512], F32, kind="ExternalOutput").ap())
    with ExitStack() as ctx0:
        tk = Tracker(nc, ctx0)
        with nc.Block() as block:
            PHASE[0] = "l0f"
            emit_l0_ffn(nc, tk, xT, yb, wproj, bproj, w1, w2, h1, h2, w3,
                        s1, s2, sh1, sh2, s3, epsc, x1_out, dbg=dbg)
            tk.flush(block)
    return nc

def stage_l0_ffn(inputs, y_by_core):
    x = inputs["L0_attn_x"]
    wproj_h = _f32c(inputs["L0_attn_w_proj"].T.reshape(8, 128, 1024))
    bproj_h = _f32c(inputs["L0_attn_b_proj"].reshape(8, 128).T)
    w1_h = _f32c(inputs["L0_ffn_w1_w"].T.reshape(8, 128, 32, 128).transpose(2, 1, 0, 3))
    w2_h = _f32c(inputs["L0_ffn_w2_w"].T.reshape(8, 128, 32, 128).transpose(2, 1, 0, 3))
    h1_h = _f32c(inputs["L0_ffn_h1_w"].T.reshape(32, 128, 32, 128).transpose(2, 1, 0, 3))
    h2_h = _f32c(inputs["L0_ffn_h2_w"].T.reshape(32, 128, 32, 128).transpose(2, 1, 0, 3))
    w3_h = _f32c(inputs["L0_ffn_w3_w"].T.reshape(32, 128, 8, 128).transpose(2, 1, 0, 3))
    s1_h = _f32c(inputs["L0_ffn_w1_s"].reshape(32, 128).T)
    s2_h = _f32c(inputs["L0_ffn_w2_s"].reshape(32, 128).T)
    sh1_h = _f32c(inputs["L0_ffn_h1_s"].reshape(32, 128).T)
    sh2_h = _f32c(inputs["L0_ffn_h2_s"].reshape(32, 128).T)
    s3_h = _f32c(inputs["L0_ffn_w3_s"].reshape(8, 128).T)
    in_maps = []
    for c in range(NCORES):
        b, half = c // 2, c % 2
        sl = slice(half * 2048, half * 2048 + 2048)
        y_full = np.concatenate([y_by_core[2 * b], y_by_core[2 * b + 1]], 0)
        in_maps.append({
            "xT": _f32c(x[b].T[:, sl]),
            "yb": np.ascontiguousarray(y_full[:, sl]),
            "wproj": wproj_h, "bproj": bproj_h,
            "w1": w1_h, "w2": w2_h, "h1": h1_h, "h2": h2_h, "w3": w3_h,
            "s1": s1_h, "s2": s2_h, "sh1": sh1_h, "sh2": sh2_h, "s3": s3_h,
            "epsc": np.full((1, 1), 1e-5, np.float32),
        })
    return in_maps


def run_l0_ffn(inputs, y_by_core, core_ids=None):
    nc = build_l0_ffn()
    in_maps = stage_l0_ffn(inputs, y_by_core)
    ids = core_ids if core_ids is not None else list(range(NCORES))
    res = run_bass_kernel_spmd(nc, [in_maps[i] for i in ids], core_ids=ids)
    return [r["x1"] for r in res.results]



LN16 = math.log(16.0)


def emit_l1_attn(nc, tk, x1e, ck, cb, rw, rb, pw, pb, omg, lmt, sint, cost,
                 rotm, epsc, qr_out, kvp_out, dbg=None):
    debug = dbg is not None
    if debug:
        dbg_xn, dbg_h, dbg_ft, dbg_q = dbg
    TO = 2048
    TE = TO + 2

    G2 = [math.sqrt(2 * g) for g in (0.5, 1.0, 2.0)]
    SF = math.sqrt(2.0 / 256.0)
    CDK = 1.5 * 2 ** 23


    with ExitStack() as ctx:

        def sb(name, shape, dt=BF16):
            return ctx.enter_context(nc.sbuf_tensor(_un(name), shape, dt))

        def psum(name, shape, dt=F32):
            return ctx.enter_context(nc.psum_tensor(_un(name), shape, dt))

        q_sb = sb("q_sb", [128, 8 * TO])
        k_sb = sb("k_sb", [128, 8 * TO])
        v_sb = sb("v_sb", [128, 8 * TO])
        wsh = sb("wsh", [128, 6144])
        cb_sb = sb("cb_sb", [128, 24], F32)
        rb_sb = sb("rb_sb", [128, 18], F32)
        pb_sb = sb("pb_sb", [128, 24], F32)
        omg_sb = sb("omg_sb", [128, 256])
        lmt_sb = sb("lmt_sb", [128, 64])
        ones_b = sb("ones_b", [128, 1])
        ones_cf = sb("ones_cf", [128, 1], F32)
        ones_row = sb("ones_row", [1, 128], F32)
        eps_sb = sb("eps_sb", [1, 1], F32)
        idf = sb("idf", [1, 1], F32)
        hpi_col = sb("hpi_col", [128, 1], F32)
        ident_b = sb("ident_b", [128, 128])
        rowsc = sb("rowsc", [1, 5 * 512], F32)
        sq0 = sb("sq0", [128, 512], F32)
        sq1 = sb("sq1", [128, 512], F32)
        rb_b = sb("rb_b", [128, 512])
        mb_b = sb("mb_b", [128, 512])
        u_scr = sb("u_scr", [128, 2 * 512], F32)
        t_scr = sb("t_scr", [128, 2 * 512], F32)
        rot_scr = sb("rot_scr", [128, 2 * 512])
        t1_scr = sb("t1_scr", [128, 2 * 512])
        ll_col = sb("ll_col", [128, 1], F32)
        llr = sb("llr", [1, 64], F32)
        kvs_scr = sb("kvs_scr", [128, 130], F32)

        psA = psum("psA", [128, 512])
        psB = psum("psB", [128, 512])
        psC = psum("psC", [128, 512])
        psD = psum("psD", [128, 512])
        psT = psum("psT", [128, 128], BF16)
        psTf = psum("psTf", [128, 128])
        psKV = psum("psKV", [128, 130])
        psKP = psum("psKP", [128, 256])

        if True:
            for ap_, sbuf_, key in ((cb, cb_sb, "cb"), (rb, rb_sb, "rbv"),
                                    (pb, pb_sb, "pbv"), (epsc, eps_sb, "eps")):
                tk.dma("sync", sbuf_[:], ap_, reads=(), writes=(key,))
            tk.dma("gpsimd", omg_sb[:], omg, reads=(), writes=("omg",))
            tk.dma("gpsimd", lmt_sb[:], lmt, reads=(), writes=("lmt",))
            tk.memset(ones_b[:], 1.0, reads=(), writes=("ones_b",))
            tk.memset(ones_cf[:], 1.0, reads=(), writes=("ones_cf",))
            tk.memset(ones_row[:], 1.0, reads=(), writes=("ones_row",))
            tk.memset(idf[:], 1.0, reads=(), writes=("idf",))
            tk.memset(hpi_col[:], math.pi / 2, reads=(), writes=("hpi",))
            build_identity(tk, ident_b[:], "idb")

            with ExitStack() as scopeA:
                def sbA(name, shape, dt=BF16):
                    return scopeA.enter_context(nc.sbuf_tensor(_un(name), shape, dt))

                TT2 = 1024
                TE2 = TT2 + 2
                h_sb = sbA("h_sb", [128, 8 * TT2])
                ft_sb = sbA("ft_sb", [128, 6 * TT2])
                xn1 = sbA("xn1", [128, 8 * TE2])

                for bt in range(2):
                    t0 = bt * TT2
                    tk.reuse("xn", "hv", "ft")
                    for kc in range(8):
                        tk.dma("gpsimd", xn1[:, kc * TE2:(kc + 1) * TE2],
                               x1e[kc * 128:(kc + 1) * 128, t0:t0 + TE2],
                               reads=(), writes=("xn",))
                    for t in range(2):
                        c0 = 1 + t * 512
                        emit_ln(tk, 8, 512,
                                lambda kc: xn1[:, kc * TE2 + c0: kc * TE2 + c0 + 512],
                                ones_cf[:], ones_row[:], sq0[:], sq1[:], rowsc[:],
                                eps_sb[0:1, 0:1], psA, psB, rb_b[:], mb_b[:],
                                "xn", ones_b[:], kb0="pA", kb1="pB")

                    def halo_sl(kc):
                        base = kc * TE2
                        return xn1[:, base: base + TE2: TE2 - 1]
                    emit_ln(tk, 8, 2, halo_sl, ones_cf[:], ones_row[:], sq0[:], sq1[:],
                            rowsc[:], eps_sb[0:1, 0:1], psA, psB, rb_b[:], mb_b[:],
                            "xn", ones_b[:], kb0="pA", kb1="pB")

                    for i, (dst, dkey) in enumerate(((q_sb, "qv"), (k_sb, "kv_"),
                                                     (v_sb, "vv"))):
                        tk.dma("gpsimd",
                               wsh[:].rearrange("p (w g c n) -> p w g c n",
                                                w=3, g=4, c=2),
                               ck[i].rearrange("w g c p n -> p w g c n"),
                               reads=(), writes=("wsh",))
                        for oc in range(8):
                            g = oc // 2
                            oc2 = oc % 2
                            for t in range(2):
                                c0 = t * 512
                                ps = (psA, psB)[t % 2]
                                bank = ("pA", "pB")[t % 2]
                                first = True
                                for w in range(3):
                                    for c2 in range(2):
                                        kc = 2 * g + c2
                                        lhsT = wsh[:, ((w * 4 + g) * 2 + c2) * 256 +
                                                   oc2 * 128:
                                                   ((w * 4 + g) * 2 + c2) * 256 +
                                                   oc2 * 128 + 128]
                                        tk.mm(ps[:, :], lhsT,
                                              xn1[:, kc * TE2 + w + c0:
                                                  kc * TE2 + w + c0 + 512],
                                              start=first, stop=(w == 2 and c2 == 1),
                                              reads=("wsh", "xn"), writes=(bank,))
                                        first = False
                                tk.ts(h_sb[:, oc * TT2 + c0: oc * TT2 + c0 + 512],
                                      ps[:, :],
                                      cb_sb[:, i * 8 + oc: i * 8 + oc + 1], ALU.add,
                                      reads=(bank, "cb"), writes=("hv",))
                        if debug and bt == 0 and i == 0:
                            tk.dma("sync", dbg_xn, xn1[:], reads=("xn",), writes=())
                            tk.dma("sync", dbg_h, h_sb[:], reads=("hv",), writes=())
                        for g in range(3):
                            tk.dma("gpsimd",
                                   wsh[:, 0:2048].rearrange("p (c n) -> p c n", c=8),
                                   rw[i, g], reads=(), writes=("wsh",))
                            for mc in range(2):
                                for t in range(2):
                                    c0 = t * 512
                                    ps = (psC, psD)[t % 2]
                                    bank = ("pC", "pD")[t % 2]
                                    for kc in range(8):
                                        tk.mm(ps[:, :],
                                              wsh[:, kc * 256 + mc * 128:
                                                  kc * 256 + mc * 128 + 128],
                                              h_sb[:, kc * TT2 + c0:
                                                   kc * TT2 + c0 + 512],
                                              start=(kc == 0), stop=(kc == 7),
                                              reads=("wsh", "hv"), writes=(bank,))
                                    gm = g * 2 + mc
                                    sl = (mc * 2 + t) % 2
                                    usl = u_scr[:, sl * 512:(sl + 1) * 512]
                                    tsl = t_scr[:, sl * 512:(sl + 1) * 512]
                                    uk, tkk = "u%d" % sl, "t%d" % sl
                                    tk.ts(usl, ps[:, :], G2[g], ALU.mult,
                                          s2=rb_sb[:, i * 6 + gm: i * 6 + gm + 1],
                                          op1=ALU.add, reads=(bank, "rbv"),
                                          writes=(uk,))
                                    tk.ts(tsl, usl, 1.0 / (2 * math.pi),
                                          ALU.mult, s2=CDK, op1=ALU.add,
                                          reads=(uk,), writes=(tkk,))
                                    tk.ts(tsl, tsl, CDK, ALU.subtract,
                                          reads=(tkk,), writes=(tkk,))
                                    tk.stt(tsl, tsl, -2 * math.pi, usl,
                                           ALU.mult, ALU.add, reads=(tkk, uk),
                                           writes=(tkk,))
                                    tk.act(tsl, tsl, AF.Abs, reads=(tkk,),
                                           writes=(tkk,))
                                    tk.act(ft_sb[:, gm * TT2 + c0:
                                               gm * TT2 + c0 + 512],
                                           tsl, AF.Sin, scale=-1.0,
                                           bias=hpi_col[:, 0:1],
                                           reads=(tkk, "hpi"), writes=("ft",))
                        if debug and bt == 0 and i == 0:
                            tk.dma("sync", dbg_ft, ft_sb[:], reads=("ft",), writes=())
                        tk.dma("gpsimd",
                               wsh[:].rearrange("p (c n) -> p c n", c=6),
                               pw[i], reads=(), writes=("wsh",))
                        for oc in range(8):
                            for t in range(2):
                                c0 = t * 512
                                ps = (psA, psB)[t % 2]
                                bank = ("pA", "pB")[t % 2]
                                for kc in range(6):
                                    tk.mm(ps[:, :],
                                          wsh[:, kc * 1024 + oc * 128:
                                              kc * 1024 + oc * 128 + 128],
                                          ft_sb[:, kc * TT2 + c0: kc * TT2 + c0 + 512],
                                          start=(kc == 0), stop=(kc == 5),
                                          reads=("wsh", "ft"), writes=(bank,))
                                tk.ts(dst[:, oc * TO + t0 + c0:
                                          oc * TO + t0 + c0 + 512], ps[:, :],
                                      SF, ALU.mult,
                                      s2=pb_sb[:, i * 8 + oc: i * 8 + oc + 1],
                                      op1=ALU.add, reads=(bank, "pbv"), writes=(dkey,))

            if debug:
                tk.dma("sync", dbg_q, q_sb[:], reads=("qv",), writes=())
            sin_sb = ctx.enter_context(nc.sbuf_tensor(_un("sin_sb"), [128, TO], F32))
            cos_sb = ctx.enter_context(nc.sbuf_tensor(_un("cos_sb"), [128, TO], BF16))
            rot_sb = ctx.enter_context(nc.sbuf_tensor(_un("rot_sb"), [128, 128], BF16))
            v_ext = ctx.enter_context(nc.sbuf_tensor(_un("v_ext"), [128, 16 * 1040], BF16))
            kn_sb = ctx.enter_context(nc.sbuf_tensor(_un("kn_sb"), [128, 2048], BF16))
            kkr = ctx.enter_context(nc.sbuf_tensor(_un("kkr"), [1, 2048], mybir.dt.float32))
            knn_r = ctx.enter_context(nc.sbuf_tensor(_un("knn_r"), [1, 2048], mybir.dt.float32))
            ekr = ctx.enter_context(nc.sbuf_tensor(_un("ekr"), [1, 512], mybir.dt.float32))
            ekb = ctx.enter_context(nc.sbuf_tensor(_un("ekb"), [128, 512], BF16))
            bcol = ctx.enter_context(nc.sbuf_tensor(_un("bcol"), [128, 16], mybir.dt.float32))
            kp_scr = ctx.enter_context(nc.sbuf_tensor(_un("kp_scr"), [128, 2 * 256], BF16))
            for nm in ("sin", "cos", "rot", "vx", "kn", "kk", "knn", "ekr", "ekb",
                       "bc", "kp"):
                tk.reuse(nm, "xn", "hv", "ft", "wsh")
            tk.dma("gpsimd", sin_sb[:], sint, reads=(), writes=("sin",))
            tk.dma("gpsimd", cos_sb[:], cost, reads=(), writes=("cos",))
            tk.dma("gpsimd", rot_sb[:], rotm, reads=(), writes=("rot",))

            for di, (dst, dkey) in enumerate(((q_sb, "qv"), (k_sb, "kv_"))):
                for kc in range(8):
                    for t in range(4):
                        c0 = t * 512
                        sl = t % 4
                        ps, bank = ((psA, "pA"), (psB, "pB"),
                                    (psC, "pC"), (psD, "pD"))[sl]
                        sl2 = t % 2
                        rsl = rot_scr[:, sl2 * 512:(sl2 + 1) * 512]
                        t1sl = t1_scr[:, sl2 * 512:(sl2 + 1) * 512]
                        rk, t1k = "rsc%d" % sl2, "t1%d" % sl2
                        tk.mm(ps[:, :], rot_sb[:],
                              dst[:, kc * TO + c0: kc * TO + c0 + 512],
                              reads=("rot", dkey), writes=(bank,))
                        tk.tt(rsl, ps[:, :], sin_sb[:, c0:c0 + 512], ALU.mult,
                              reads=(bank, "sin"), writes=(rk,))
                        tk.tt(t1sl, dst[:, kc * TO + c0: kc * TO + c0 + 512],
                              cos_sb[:, c0:c0 + 512], ALU.mult,
                              reads=(dkey, "cos"), writes=(t1k,))
                        tk.tt(dst[:, kc * TO + c0: kc * TO + c0 + 512],
                              t1sl, rsl, ALU.add,
                              reads=(t1k, rk), writes=(dkey,))
            tk.dma("sync", qr_out.rearrange("(c p) n -> p c n", p=128),
                   q_sb[:].rearrange("p (c n) -> p c n", c=8),
                   reads=("qv",), writes=())

            for tb in range(16):
                base = tb * 1040
                tk.memset(v_ext[:, base + 64: base + 1040: 65], 1.0,
                          reads=(), writes=("vx",))
                for dc in range(8):
                    vsrc = v_sb[:, dc * TO + tb * 128: dc * TO + (tb + 1) * 128]
                    if dc % 2 == 0:
                        psx, bankx = psT, "pT"
                        tk.trans(psx[:, :], vsrc, ident_b[:],
                                 reads=("vv", "idb"), writes=(bankx,))
                    else:
                        psx, bankx = psTf, "pTf"
                        tk.mm(psx[:, :], vsrc, ident_b[:],
                              reads=("vv", "idb"), writes=(bankx,))
                    tk.copy(v_ext[:, base + (2 * dc) * 65: base + (2 * dc) * 65 + 64],
                            psx[:, 0:64], reads=(bankx,), writes=("vx",))
                    tk.copy(v_ext[:, base + (2 * dc + 1) * 65:
                                  base + (2 * dc + 1) * 65 + 64],
                            psx[:, 64:128], reads=(bankx,), writes=("vx",))

            tk.act(sq0[0:64, 0:64], lmt_sb[0:64, :], AF.Square,
                   reads=("lmt",), writes=("sq0",))
            tk.mm(psA[0:1, 0:64], ones_cf[0:64, 0:1], sq0[0:64, 0:64],
                  reads=("sq0", "ones_cf"), writes=("pA",))
            tk.ts(llr[:], psA[0:1, 0:64], -1.0 / 64.0, ALU.mult,
                  reads=("pA",), writes=("llr",))
            tk.trans(psTf[0:64, 0:1], llr[:], idf[:], reads=("llr", "idf"), writes=("pTf",))
            tk.copy(ll_col[0:64, 0:1], psTf[0:64, 0:1], reads=("pTf",), writes=("llc",))

            for h in range(16):
                r0 = (h % 2) * 64
                cbase = (h // 2) * TO
                ksl = lambda c0, c1: k_sb[r0:r0 + 64, cbase + c0: cbase + c1]
                for t in range(4):
                    c0 = t * 512
                    sqx, sqk = ((sq0, "sq0"), (sq1, "sq1"))[t % 2]
                    psx, bkx = ((psB, "pB"), (psC, "pC"))[t % 2]
                    tk.act(sqx[r0:r0 + 64, :], ksl(c0, c0 + 512), AF.Square,
                           reads=("kv_",), writes=(sqk,))
                    tk.mm(psx[0:1, :], ones_cf[r0:r0 + 64, 0:1], sqx[r0:r0 + 64, :],
                          reads=(sqk, "ones_cf"), writes=(bkx,))
                    tk.copy(kkr[:, c0:c0 + 512], psx[0:1, :], reads=(bkx,), writes=("kk",))
                for t in range(4):
                    c0 = t * 512
                    tk.mm(psA[0:64, :], lmt_sb[r0:r0 + 64, :], ksl(c0, c0 + 512),
                          reads=("lmt", "kv_"), writes=("pA",))
                    tk.act(t1_scr[0:64, 0:512], psA[0:64, :], AF.Exp, scale=2.0 / 64.0,
                           bias=ll_col[0:64, 0:1], reads=("pA", "llc"), writes=("t1",))
                    tk.act(ekr[:], kkr[:, c0:c0 + 512], AF.Exp, scale=-1.0 / 64.0,
                           reads=("kk",), writes=("ekr",))
                    tk.mm(psB[0:64, :], ones_row[0:1, 0:64], ekr[:],
                          reads=("ekr", "ones_row"), writes=("pB",))
                    tk.copy(ekb[0:64, :], psB[0:64, :], reads=("pB",), writes=("ekb",))
                    tk.tt(kn_sb[0:64, c0:c0 + 512], t1_scr[0:64, 0:512], ekb[0:64, :],
                          ALU.mult, reads=("t1", "ekb"), writes=("kn",))
                for t in range(4):
                    c0 = t * 512
                    tk.act(sq0[0:64, :], kn_sb[0:64, c0:c0 + 512], AF.Square,
                           reads=("kn",), writes=("sq0",))
                    tk.mm(psB[0:1, :], ones_cf[0:64, 0:1], sq0[0:64, :],
                          reads=("sq0", "ones_cf"), writes=("pB",))
                    tk.ts(knn_r[:, c0:c0 + 512], psB[0:1, :], -0.5, ALU.mult,
                          s2=-LN16, op1=ALU.add, reads=("pB",), writes=("knn",))
                for tb in range(16):
                    psx, bankx = ((psTf, "pTf"), (psC, "pC"))[tb % 2]
                    tk.trans(psx[:, 0:1], knn_r[:, tb * 128:(tb + 1) * 128], idf[:],
                             reads=("knn", "idf"), writes=(bankx,))
                    tk.copy(bcol[:, tb:tb + 1], psx[:, 0:1], reads=(bankx,), writes=("bc",))
                for tb in range(16):
                    sl = tb % 2
                    kpsl = kp_scr[:, sl * 256:(sl + 1) * 256]
                    kpk = "kp%d" % sl
                    tk.mm(psKP[:, :], kn_sb[0:64, tb * 128:(tb + 1) * 128],
                          omg_sb[0:64, :], reads=("kn", "omg"), writes=("pKP",))
                    tk.act(kpsl, psKP[:, :], AF.Exp, bias=bcol[:, tb:tb + 1],
                           reads=("pKP", "bc"), writes=(kpk,))
                    for mc in range(2):
                        tk.mm(psKV[:, mc * 65:(mc + 1) * 65],
                              kp_scr[:, sl * 256 + mc * 128:
                                     sl * 256 + (mc + 1) * 128],
                              v_ext[:, tb * 1040 + h * 65: tb * 1040 + h * 65 + 65],
                              start=(tb == 0), stop=(tb == 15),
                              reads=(kpk, "vx"), writes=("pKV",))
                tk.copy(kvs_scr[:], psKV[:, :], reads=("pKV",), writes=("kvs",))
                tk.dma("sync", kvp_out[:, h * 130:(h + 1) * 130], kvs_scr[:],
                       reads=("kvs",), writes=())

    return nc




def build_l1_attn(debug=False):
    nc = bass.Bass("TRN2", target_bir_lowering=False)
    TO = 2048
    TE = TO + 2
    x1e = nc.dram_tensor("x1e", [D, TE], F32, kind="ExternalInput").ap()
    ck = nc.dram_tensor("ck", [3, 3, 4, 2, 128, 256], F32, kind="ExternalInput").ap()
    cb = nc.dram_tensor("cb", [128, 24], F32, kind="ExternalInput").ap()
    rw = nc.dram_tensor("rw", [3, 3, 128, 8, 256], F32, kind="ExternalInput").ap()
    rb = nc.dram_tensor("rb", [128, 18], F32, kind="ExternalInput").ap()
    pw = nc.dram_tensor("pw", [3, 128, 6, 1024], F32, kind="ExternalInput").ap()
    pb = nc.dram_tensor("pb", [128, 24], F32, kind="ExternalInput").ap()
    omg = nc.dram_tensor("omg", [128, 256], F32, kind="ExternalInput").ap()
    lmt = nc.dram_tensor("lmt", [128, 64], F32, kind="ExternalInput").ap()
    sint = nc.dram_tensor("sint", [128, TO], F32, kind="ExternalInput").ap()
    cost = nc.dram_tensor("cost", [128, TO], F32, kind="ExternalInput").ap()
    rotm = nc.dram_tensor("rotm", [128, 128], F32, kind="ExternalInput").ap()
    epsc = nc.dram_tensor("epsc", [1, 1], F32, kind="ExternalInput").ap()
    qr_out = nc.dram_tensor("qr", [D, TO], BF16, kind="ExternalOutput").ap()
    kvp_out = nc.dram_tensor("kvp", [128, 16 * 130], F32, kind="ExternalOutput").ap()
    dbg = None
    if debug:
        dbg = (nc.dram_tensor("dbg_xn", [128, 8 * 1026], BF16, kind="ExternalOutput").ap(),
               nc.dram_tensor("dbg_h", [128, 8 * 1024], BF16, kind="ExternalOutput").ap(),
               nc.dram_tensor("dbg_ft", [128, 6 * 1024], BF16, kind="ExternalOutput").ap(),
               nc.dram_tensor("dbg_q", [128, 8 * 2048], BF16, kind="ExternalOutput").ap())
    with ExitStack() as ctx0:
        tk = Tracker(nc, ctx0)
        with nc.Block() as block:
            PHASE[0] = "l1a"
            emit_l1_attn(nc, tk, x1e, ck, cb, rw, rb, pw, pb, omg, lmt, sint, cost,
                         rotm, epsc, qr_out, kvp_out, dbg=dbg)
            tk.flush(block)
    return nc

def _rope_tables(S=4096, dh=64):
    inv = 1.0 / (10000.0 ** (np.arange(0, dh, 2, dtype=np.float64) / dh))
    f = np.arange(S, dtype=np.float64)[:, None] * inv[None, :]
    emb = np.concatenate([f, f], axis=1)
    return np.sin(emb).astype(np.float32), np.cos(emb).astype(np.float32)


def _rot_matrix():
    R = np.zeros((64, 64), np.float32)
    R[np.arange(32), np.arange(32) + 32] = -1.0
    R[np.arange(32, 64), np.arange(0, 32)] = 1.0
    lhsT = R.T
    out = np.zeros((128, 128), np.float32)
    out[:64, :64] = lhsT
    out[64:, 64:] = lhsT
    return out


def stage_l1_attn(inputs, x1_by_core):
    ckf = inputs["L1_attn_conv_k"]
    ck_h = _f32c(np.stack([ckf[i].reshape(3, 2, 128, 4, 256).transpose(0, 3, 1, 2, 4)
                           for i in range(3)]))
    cb_h = _f32c(inputs["L1_attn_conv_b"].reshape(3, 8, 128).transpose(2, 0, 1)
                 .reshape(128, 24))
    rw_h = _f32c(inputs["L1_attn_rff_w"].reshape(3, 3, 8, 128, 256)
                 .transpose(0, 1, 3, 2, 4))
    rb_h = _f32c(inputs["L1_attn_rff_b"].reshape(3, 3, 2, 128)
                 .transpose(3, 0, 1, 2).reshape(128, 18))
    pw_h = _f32c(inputs["L1_attn_proj_w"].reshape(3, 6, 128, 1024)
                 .transpose(0, 2, 1, 3))
    pb_h = _f32c(inputs["L1_attn_proj_b"].reshape(3, 8, 128).transpose(2, 0, 1)
                 .reshape(128, 24))
    omg_h = _f32c(np.concatenate([inputs["L1_attn_omega"]] * 2, 0))
    lmt_h = _f32c(np.concatenate([inputs["L1_attn_landmarks"].T] * 2, 0))
    sin_t, cos_t = _rope_tables()
    rot_h = _rot_matrix()
    in_maps = []
    for c in range(NCORES):
        b, half = c // 2, c % 2
        if c not in x1_by_core:
            in_maps.append(None)
            continue
        own = x1_by_core[c]
        left = (np.zeros((D, 1), np.float32) if half == 0
                else x1_by_core[2 * b][:, 2047:2048])
        right = (x1_by_core[2 * b + 1][:, 0:1] if half == 0
                 else np.zeros((D, 1), np.float32))
        x1e = _f32c(np.concatenate([left, own, right], 1))
        sl = slice(half * 2048, half * 2048 + 2048)
        sin_h = _f32c(np.concatenate([sin_t.T[:, sl]] * 2, 0))
        cos_h = _f32c(np.concatenate([cos_t.T[:, sl]] * 2, 0))
        in_maps.append({
            "x1e": x1e, "ck": ck_h, "cb": cb_h, "rw": rw_h, "rb": rb_h,
            "pw": pw_h, "pb": pb_h, "omg": omg_h, "lmt": lmt_h,
            "sint": sin_h, "cost": cos_h, "rotm": rot_h,
            "epsc": np.full((1, 1), 1e-5, np.float32),
        })
    return in_maps


def run_l1_attn(inputs, x1_by_core, core_ids=None):
    nc = build_l1_attn()
    in_maps = stage_l1_attn(inputs, x1_by_core)
    ids = core_ids if core_ids is not None else list(range(NCORES))
    res = run_bass_kernel_spmd(nc, [in_maps[i] for i in ids], core_ids=ids)
    return [(r["qr"], r["kvp"]) for r in res.results]



def emit_l1_moe(nc, tk, x1T, qr, kvp2, omg, outw, outb, nw, gw, w1, b1,
                w2, b2, xo, dbg=None, kvload=None):
    debug = dbg is not None
    if debug:
        dbg_c, dbg_g, dbg_t, dbg_gs, dbg_gw = dbg
    TO = 2048

    LN16C = math.log(16.0)


    with ExitStack() as ctx:

        def sb(name, shape, dt=BF16):
            return ctx.enter_context(nc.sbuf_tensor(_un(name), shape, dt))

        def psum(name, shape, dt=F32):
            return ctx.enter_context(nc.psum_tensor(_un(name), shape, dt))

        t_sb = sb("t_sb", [128, 8 * TO])
        attn1 = sb("attn1", [128, 8 * TO])
        outw_sb = sb("outw_sb", [128, 8 * 1024])
        omg_sb = sb("omg_sb", [128, 256])
        outb_sb = sb("outb_sb", [128, 8], F32)
        nw_sb = sb("nw_sb", [128, 8], F32)
        gw_sb = sb("gw_sb", [128, 8 * 8], F32)
        b1_sb = sb("b1_sb", [128, 128], F32)
        b2_sb = sb("b2_sb", [8, 1024], F32)
        c_fm = sb("c_fm", [8, TO], F32)
        muL = sb("muL", [1, TO])
        rL = sb("rL", [1, TO])
        ones_rowb = sb("ones_rowb", [1, 128])
        rows4 = sb("rows4", [1, 4 * 512], F32)
        ones_b = sb("ones_b", [128, 1])
        ones_cf = sb("ones_cf", [128, 1], F32)
        ones_row = sb("ones_row", [1, 128], F32)
        idf128 = sb("idf128", [128, 128], F32)
        nln16 = sb("nln16", [1, 1], F32)
        sq_s = sb("sq_s", [128, 512], F32)
        row_s = sb("row_s", [1, 512], F32)
        eqqb = sb("eqqb", [128, 512])
        qp_scr = sb("qp_scr", [128, 512])
        nm_scr = sb("nm_scr", [128, 512], F32)
        rb_scr = sb("rb_scr", [128, 512], F32)
        gt_scr = sb("gt_scr", [128, 8 * 4], F32)
        m1c = sb("m1c", [128, 1], F32)
        m2c = sb("m2c", [128, 1], F32)
        w1c = sb("w1c", [128, 1], F32)
        w2c = sb("w2c", [128, 1], F32)
        e1s = sb("e1s", [128, 8], F32)
        e2s = sb("e2s", [128, 8], F32)
        t2s = sb("t2s", [128, 8], F32)
        cb_scr = sb("cb_scr", [128, 512])
        gate_s = sb("gate_s", [8, 512], F32)
        sel_sb = sb("sel_sb", [8, 1024], F32)

        ps0 = psum("ps0", [128, 512])
        ps1 = psum("ps1", [128, 512])
        ps2 = psum("ps2", [128, 512])
        ps3 = psum("ps3", [128, 512])
        psG = psum("psG", [8, 512])
        psTf = psum("psTf", [128, 128])
        psN = psum("psN", [128, 512])
        psD = psum("psD", [1, 512])

        if True:
            tk.dma("sync", outb_sb[:], outb, reads=(), writes=("outb",))
            tk.dma("sync", nw_sb[:], nw, reads=(), writes=("nw",))
            tk.dma("sync", gw_sb[:].rearrange("p (c n) -> p c n", c=8),
                   gw.rearrange("c p n -> p c n"), reads=(), writes=("gw",))
            tk.dma("sync", b1_sb[:], b1, reads=(), writes=("b1",))
            tk.dma("sync", b2_sb[:], b2, reads=(), writes=("b2",))
            tk.dma("gpsimd", omg_sb[:], omg, reads=(), writes=("omg",))
            tk.dma("gpsimd", outw_sb[:].rearrange("p (c n) -> p c n", c=8),
                   outw.rearrange("c p n -> p c n"), reads=(), writes=("outw",))
            tk.memset(ones_b[:], 1.0, reads=(), writes=("ones_b",))
            tk.memset(ones_cf[:], 1.0, reads=(), writes=("ones_cf",))
            tk.memset(ones_row[:], 1.0, reads=(), writes=("ones_row",))
            tk.memset(nln16[:], -LN16C, reads=(), writes=("nln16",))
            tk.memset(ones_rowb[:], 1.0, reads=(), writes=("ones_rowb",))
            tk.memset(idf128[:], 0.0, reads=(), writes=("idf",), actor="gpsimd")
            tk.affine_sel(idf128[:], idf128[:], [[-1, 128]], ALU.not_equal, 1.0, 0, 1,
                          reads=("idf",), writes=("idf",))
            tk.memset(sel_sb[:], 1.0, reads=(), writes=("sel",), actor="gpsimd")
            tk.affine_sel(sel_sb[:], sel_sb[:], [[1, 1024]], ALU.is_ge, 0.0, 0, -128,
                          reads=("sel",), writes=("sel",))
            tk.affine_sel(sel_sb[:], sel_sb[:], [[-1, 1024]], ALU.is_ge, 0.0, 127, 128,
                          reads=("sel",), writes=("sel",))

            with ExitStack() as scopeA:
                def sbA(name, shape, dt=BF16):
                    return scopeA.enter_context(nc.sbuf_tensor(_un(name), shape, dt))

                qr_sb = sbA("qr_sb", [128, 8 * TO])
                kvf = sbA("kvf", [128, 2080], F32)
                kvb = sbA("kvb", [128, 2080])
                x1f = sbA("x1f", [128, 8 * 512], F32)
                xnf = sbA("xnf", [128, 8 * 512], F32)
                tf = xnf

                tk.dma("gpsimd", qr_sb[:].rearrange("p (c n) -> p c n", c=8),
                       qr.rearrange("(c p) n -> p c n", p=128),
                       reads=(), writes=("qr",))
                if kvload is not None:
                    kvload(tk, kvf)
                else:
                    tk.dma("gpsimd", kvf[:], kvp2[:, 0:2080], reads=(), writes=("kvf",))
                    tk.emit("gpsimd",
                            lambda e: e.dma_start(out=kvf[:], in_=kvp2[:, 2080:4160],
                                                  accum_op=ALU.add),
                            reads=("kvf",), writes=("kvf",), dma=True)
                tk.copy(kvb[:], kvf[:], reads=("kvf",), writes=("kvb",))

                for t in range(4):
                    t0 = t * 512
                    for kc in range(8):
                        tk.dma("sync", x1f[:, kc * 512:(kc + 1) * 512],
                               x1T[kc * 128:(kc + 1) * 128, t0:t0 + 512],
                               reads=(), writes=("x1f",))
                        tk.mm(ps0[0:1, :], ones_cf[:], x1f[:, kc * 512:(kc + 1) * 512],
                              start=(kc == 0), stop=(kc == 7),
                              reads=("x1f", "ones_cf"), writes=("p0",))
                        tk.act(sq_s[:], x1f[:, kc * 512:(kc + 1) * 512], AF.Square,
                               reads=("x1f",), writes=("sq",))
                        tk.mm(ps1[0:1, :], ones_cf[:], sq_s[:],
                              start=(kc == 0), stop=(kc == 7),
                              reads=("sq", "ones_cf"), writes=("p1",))
                    mu = rows4[:, 0:512]
                    ex2 = rows4[:, 512:1024]
                    var = rows4[:, 1024:1536]
                    rstd = rows4[:, 1536:2048]
                    tk.ts(mu, ps0[0:1, :], 1.0 / D, ALU.mult, reads=("p0",), writes=("mu",))
                    tk.ts(ex2, ps1[0:1, :], 1.0 / D, ALU.mult, reads=("p1",), writes=("ex2",))
                    tk.tt(var, mu, mu, ALU.mult, reads=("mu",), writes=("var",))
                    tk.tt(var, ex2, var, ALU.subtract, reads=("ex2", "var"), writes=("var",))
                    tk.ts(var, var, 1e-5, ALU.add, reads=("var",), writes=("var",))
                    tk.act(var, var, AF.Ln, reads=("var",), writes=("var",))
                    tk.act(rstd, var, AF.Exp, scale=-0.5, reads=("var",), writes=("rstd",))
                    tk.copy(muL[:, t0:t0 + 512], mu, reads=("mu",), writes=("muL",))
                    tk.copy(rL[:, t0:t0 + 512], rstd, reads=("rstd",), writes=("rL",))
                    tk.mm(ps0[:, :], ones_row[:], mu, reads=("mu", "ones_row"), writes=("p0",))
                    tk.mm(ps1[:, :], ones_row[:], rstd, reads=("rstd", "ones_row"), writes=("p1",))
                    for kc in range(8):
                        tk.tt(xnf[:, kc * 512:(kc + 1) * 512],
                              x1f[:, kc * 512:(kc + 1) * 512], ps0[:, :], ALU.subtract,
                              reads=("x1f", "p0"), writes=("xnf",))
                        tk.tt(xnf[:, kc * 512:(kc + 1) * 512],
                              xnf[:, kc * 512:(kc + 1) * 512], ps1[:, :], ALU.mult,
                              reads=("xnf", "p1"), writes=("xnf",))
                    for kc in range(8):
                        tk.act(sq_s[:], xnf[:, kc * 512:(kc + 1) * 512], AF.Square,
                               reads=("xnf",), writes=("sq",))
                        tk.mm(ps2[0:1, :], ones_cf[:], sq_s[:],
                              start=(kc == 0), stop=(kc == 7),
                              reads=("sq", "ones_cf"), writes=("p2",))
                    ms = rows4[:, 0:512]
                    rr = rows4[:, 512:1024]
                    tk.ts(ms, ps2[0:1, :], 1.0 / D, ALU.mult, s2=1e-6, op1=ALU.add,
                          reads=("p2",), writes=("ms",))
                    tk.act(ms, ms, AF.Ln, reads=("ms",), writes=("ms",))
                    tk.act(rr, ms, AF.Exp, scale=-0.5, reads=("ms",), writes=("rr",))
                    tk.mm(ps2[:, :], ones_row[:], rr, reads=("rr", "ones_row"), writes=("p2",))
                    for kc in range(8):
                        tk.stt(tf[:, kc * 512:(kc + 1) * 512],
                               xnf[:, kc * 512:(kc + 1) * 512], nw_sb[:, kc:kc + 1],
                               ps2[:, :], ALU.mult, ALU.mult,
                               reads=("xnf", "nw", "p2"), writes=("xnf",))
                        tk.copy(t_sb[:, kc * TO + t0: kc * TO + t0 + 512],
                                tf[:, kc * 512:(kc + 1) * 512],
                                reads=("xnf",), writes=("tv",))
                    for kc in range(8):
                        tk.mm(psG[:, :], gw_sb[:, kc * 8:(kc + 1) * 8],
                              tf[:, kc * 512:(kc + 1) * 512],
                              start=(kc == 0), stop=(kc == 7),
                              reads=("xnf", "gw"), writes=("pG",))
                    tk.copy(gate_s[:], psG[:, :], reads=("pG",), writes=("gs",))
                    if debug:
                        tk.dma("sync", dbg_gs[:, t0:t0 + 512], gate_s[:],
                               reads=("gs",), writes=())
                    for sub in range(4):
                        tk.trans(psTf[:, 0:8], gate_s[:, sub * 128:(sub + 1) * 128],
                                 idf128[0:8, 0:8], reads=("gs", "idf"), writes=("pTf",))
                        tk.copy(gt_scr[:, sub * 8:(sub + 1) * 8], psTf[:, 0:8],
                                reads=("pTf",), writes=("gt",))
                        g = gt_scr[:, sub * 8:(sub + 1) * 8]
                        tk.reduce(m1c[:], g, ALU.max, reads=("gt",), writes=("m1",))
                        tk.ts(e1s[:], g, m1c[:], ALU.is_ge, reads=("gt", "m1"), writes=("e1",))
                        tk.stt(t2s[:], e1s[:], -1e9, g, ALU.mult, ALU.add,
                               reads=("e1", "gt"), writes=("t2",))
                        tk.reduce(m2c[:], t2s[:], ALU.max, reads=("t2",), writes=("m2",))
                        tk.tt(w1c[:], m1c[:], m2c[:], ALU.subtract,
                              reads=("m1", "m2"), writes=("w1c",))
                        tk.act(w1c[:], w1c[:], AF.Sigmoid, reads=("w1c",), writes=("w1c",))
                        tk.ts(w2c[:], w1c[:], -1.0, ALU.mult, s2=1.0, op1=ALU.add,
                              reads=("w1c",), writes=("w2c",))
                        tk.ts(e2s[:], t2s[:], m2c[:], ALU.is_ge, reads=("t2", "m2"),
                              writes=("e2",))
                        tk.ts(e1s[:], e1s[:], w1c[:], ALU.mult, reads=("e1", "w1c"),
                              writes=("e1",))
                        tk.stt(e2s[:], e2s[:], w2c[:], e1s[:], ALU.mult, ALU.add,
                               reads=("e2", "w2c", "e1"), writes=("e2",))
                        tk.trans(psTf[0:8, 0:128], e2s[:], idf128[:],
                                 reads=("e2", "idf"), writes=("pTf",))
                        tk.copy(c_fm[:, t0 + sub * 128: t0 + (sub + 1) * 128],
                                psTf[0:8, 0:128], reads=("pTf",), writes=("cfm",))

                for h in range(16):
                    r0 = (h % 2) * 64
                    cb = (h // 2) * TO
                    for t in range(4):
                        t0 = t * 512
                        qsl = qr_sb[r0:r0 + 64, cb + t0: cb + t0 + 512]
                        tk.act(sq_s[r0:r0 + 64, :], qsl, AF.Square,
                               reads=("qr",), writes=("sq",))
                        tk.mm(psD[:, :], ones_cf[r0:r0 + 64, 0:1], sq_s[r0:r0 + 64, :],
                              reads=("sq", "ones_cf"), writes=("pD",))
                        tk.act(row_s[:], psD[:, :], AF.Exp, scale=-0.5,
                               bias=nln16[0:1, 0:1], reads=("pD", "nln16"), writes=("row",))
                        tk.mm(ps3[:, :], ones_row[:], row_s[:],
                              reads=("row", "ones_row"), writes=("p3",))
                        tk.copy(eqqb[:], ps3[:, :], reads=("p3",), writes=("eqqb",))
                        for mc in range(2):
                            tk.mm(ps0[:, :], omg_sb[r0:r0 + 64, mc * 128:(mc + 1) * 128],
                                  qsl, reads=("omg", "qr"), writes=("p0",))
                            tk.act(qp_scr[:], ps0[:, :], AF.Exp, reads=("p0",),
                                   writes=("qp",))
                            tk.tt(qp_scr[:], qp_scr[:], eqqb[:], ALU.mult,
                                  reads=("qp", "eqqb"), writes=("qp",))
                            tk.mm(psN[r0:r0 + 64, :],
                                  kvb[:, h * 130 + mc * 65: h * 130 + mc * 65 + 64],
                                  qp_scr[:], start=(mc == 0), stop=(mc == 1),
                                  tile_position=(0, r0),
                                  reads=("kvb", "qp"), writes=("pN",))
                            tk.mm(psD[:, :],
                                  kvb[:, h * 130 + mc * 65 + 64: h * 130 + mc * 65 + 65],
                                  qp_scr[:], start=(mc == 0), stop=(mc == 1),
                                  reads=("kvb", "qp"), writes=("pD",))
                        tk.ts(row_s[:], psD[:, :], 1e-6, ALU.add, reads=("pD",),
                              writes=("row",))
                        tk.mm(ps3[r0:r0 + 64, :], ones_row[0:1, 0:64], row_s[:],
                              tile_position=(0, r0),
                              reads=("row", "ones_row"), writes=("p3",))
                        tk.recip(rb_scr[r0:r0 + 64, :], ps3[r0:r0 + 64, :],
                                 reads=("p3",), writes=("rbs",))
                        tk.copy(nm_scr[r0:r0 + 64, :], psN[r0:r0 + 64, :],
                                reads=("pN",), writes=("nms",))
                        tk.tt(attn1[r0:r0 + 64, cb + t0: cb + t0 + 512],
                              nm_scr[r0:r0 + 64, :], rb_scr[r0:r0 + 64, :], ALU.mult,
                              reads=("nms", "rbs"), writes=("at1",))

            if debug:
                tk.dma("sync", dbg_gw, gw_sb[:], reads=("gw",), writes=())
                tk.dma("sync", dbg_c, c_fm[:], reads=("cfm",), writes=())
                tk.dma("sync", dbg_g, gt_scr[:], reads=("gt",), writes=())
                tk.dma("sync", dbg_t, t_sb[:], reads=("tv",), writes=())
            with ExitStack() as scopeB:
                def sbB(name, shape, dt=BF16):
                    return scopeB.enter_context(nc.sbuf_tensor(_un(name), shape, dt))

                BT = 1024
                h_e = sbB("h_e", [128, 16 * BT])
                moe_acc = sbB("moe_acc", [128, 8 * BT], F32)
                wA = sbB("wA", [128, 16 * 128])
                wB = sbB("wB", [128, 16 * 128])
                x1g = sbB("x1g", [128, 512], F32)
                xc_scr = sbB("xc_scr", [128, 512], F32)
                ao_scr = sbB("ao_scr", [128, 512], F32)
                for nm in ("he", "macc", "wA", "wB", "x1g", "xc", "ao"):
                    tk.reuse(nm, "qr", "kvf", "kvb", "x1f", "xnf", "tf")

                for bt in range(2):
                    bt0 = bt * BT
                    for e in range(8):
                        for hc in range(16):
                            sa, ka_ = (wA, "wA") if hc % 2 == 0 else (wB, "wB")
                            tk.dma("gpsimd", sa[:, 0:8 * 128], w1[e, hc],
                                   reads=(), writes=(ka_,))
                            for st in range(2):
                                c0 = bt0 + st * 512
                                ps = (ps0, ps1)[st]
                                bank = ("p0", "p1")[st]
                                for kc in range(8):
                                    tk.mm(ps[:, :], sa[:, kc * 128:(kc + 1) * 128],
                                          t_sb[:, kc * TO + c0: kc * TO + c0 + 512],
                                          start=(kc == 0), stop=(kc == 7),
                                          reads=(ka_, "tv"), writes=(bank,))
                                tk.act(h_e[:, hc * BT + st * 512: hc * BT + st * 512 + 512],
                                       ps[:, :], AF.Gelu,
                                       bias=b1_sb[:, e * 16 + hc: e * 16 + hc + 1],
                                       reads=(bank, "b1"), writes=("he",))
                        for st in range(2):
                            c0 = bt0 + st * 512
                            tk.mm(ps2[:, :], sel_sb[0:8, e * 128:(e + 1) * 128],
                                  c_fm[0:8, c0:c0 + 512],
                                  reads=("cfm", "sel"), writes=("p2",))
                            tk.copy(cb_scr[:], ps2[:, :], reads=("p2",), writes=("cbs",))
                            for hc in range(16):
                                tk.tt(h_e[:, hc * BT + st * 512: hc * BT + st * 512 + 512],
                                      h_e[:, hc * BT + st * 512: hc * BT + st * 512 + 512],
                                      cb_scr[:], ALU.mult, reads=("he", "cbs"), writes=("he",))
                        for oc in range(8):
                            sa, ka_ = (wA, "wA") if oc % 2 == 0 else (wB, "wB")
                            tk.dma("gpsimd", sa[:], w2[e, oc], reads=(), writes=(ka_,))
                            for st in range(2):
                                c0 = bt0 + st * 512
                                ps = (ps0, ps1)[st]
                                bank = ("p0", "p1")[st]
                                for kc in range(16):
                                    tk.mm(ps[:, :], sa[:, kc * 128:(kc + 1) * 128],
                                          h_e[:, kc * BT + st * 512: kc * BT + st * 512 + 512],
                                          start=(kc == 0), stop=False,
                                          reads=(ka_, "he"), writes=(bank,))
                                if e == 0:
                                    tk.mm(ps[:, :], b2_sb[:, oc * 128:(oc + 1) * 128],
                                          c_fm[:, c0:c0 + 512], start=False, stop=True,
                                          reads=("b2", "cfm"), writes=(bank,))
                                else:
                                    tk.mm(ps[:, :], b2_sb[0:8, oc * 128:(oc + 1) * 128],
                                          c_fm[0:8, c0:c0 + 512], start=False, stop=True,
                                          reads=("b2", "cfm"), writes=(bank,)) \
                                        if False else None
                                if e == 0:
                                    tk.copy(moe_acc[:, oc * BT + st * 512:
                                                    oc * BT + st * 512 + 512],
                                            ps[:, :], reads=(bank,), writes=("macc",))
                                else:
                                    tk.tt(moe_acc[:, oc * BT + st * 512:
                                                  oc * BT + st * 512 + 512],
                                          moe_acc[:, oc * BT + st * 512:
                                                  oc * BT + st * 512 + 512],
                                          ps[:, :], ALU.add, reads=("macc", bank),
                                          writes=("macc",))
                    for oc in range(8):
                        for st in range(2):
                            c0 = bt0 + st * 512
                            ps = (ps0, ps1)[st]
                            bank = ("p0", "p1")[st]
                            for kc in range(8):
                                tk.mm(ps[:, :],
                                      outw_sb[:, kc * 1024 + oc * 128:
                                              kc * 1024 + oc * 128 + 128],
                                      attn1[:, kc * TO + c0: kc * TO + c0 + 512],
                                      start=(kc == 0), stop=(kc == 7),
                                      reads=("outw", "at1"), writes=(bank,))
                            tk.ts(ao_scr[:], ps[:, :], outb_sb[:, oc:oc + 1], ALU.add,
                                  reads=(bank, "outb"), writes=("ao",))
                            tk.dma("sync", x1g[:],
                                   x1T[oc * 128:(oc + 1) * 128, c0:c0 + 512],
                                   reads=(), writes=("x1g",))
                            tk.mm(ps2[:, :], ones_rowb[:], muL[:, c0:c0 + 512],
                                  reads=("muL", "ones_rowb"), writes=("p2",))
                            tk.mm(ps3[:, :], ones_rowb[:], rL[:, c0:c0 + 512],
                                  reads=("rL", "ones_rowb"), writes=("p3",))
                            tk.tt(xc_scr[:], x1g[:], ps2[:, :], ALU.subtract,
                                  reads=("x1g", "p2"), writes=("xc",))
                            tk.tt(xc_scr[:], xc_scr[:], ps3[:, :], ALU.mult,
                                  reads=("xc", "p3"), writes=("xc",))
                            tk.tt(ao_scr[:], ao_scr[:], xc_scr[:], ALU.add,
                                  reads=("ao", "xc"), writes=("ao",))
                            tk.tt(ao_scr[:], ao_scr[:], x1g[:], ALU.add,
                                  reads=("ao", "x1g"), writes=("ao",))
                            tk.tt(ao_scr[:], ao_scr[:],
                                  moe_acc[:, oc * BT + st * 512: oc * BT + st * 512 + 512],
                                  ALU.add, reads=("ao", "macc"), writes=("ao",))
                            tk.dma("sync", xo[oc * 128:(oc + 1) * 128, c0:c0 + 512],
                                   ao_scr[:], reads=("ao",), writes=())

    return nc





def build_l1_moe(debug=False):
    nc = bass.Bass("TRN2", target_bir_lowering=False)
    TO = 2048
    x1T = nc.dram_tensor("x1T", [D, TO], F32, kind="ExternalInput").ap()
    qr = nc.dram_tensor("qr", [D, TO], BF16, kind="ExternalInput").ap()
    kvp2 = nc.dram_tensor("kvp2", [128, 2 * 2080], F32, kind="ExternalInput").ap()
    omg = nc.dram_tensor("omg", [128, 256], F32, kind="ExternalInput").ap()
    outw = nc.dram_tensor("outw", [8, 128, 1024], F32, kind="ExternalInput").ap()
    outb = nc.dram_tensor("outb", [128, 8], F32, kind="ExternalInput").ap()
    nw = nc.dram_tensor("nw", [128, 8], F32, kind="ExternalInput").ap()
    gw = nc.dram_tensor("gw", [8, 128, 8], F32, kind="ExternalInput").ap()
    w1 = nc.dram_tensor("w1", [8, 16, 128, 8, 128], F32, kind="ExternalInput").ap()
    b1 = nc.dram_tensor("b1", [128, 128], F32, kind="ExternalInput").ap()
    w2 = nc.dram_tensor("w2", [8, 8, 128, 16, 128], F32, kind="ExternalInput").ap()
    b2 = nc.dram_tensor("b2", [8, 1024], F32, kind="ExternalInput").ap()
    xo = nc.dram_tensor("xo", [D, TO], F32, kind="ExternalOutput").ap()
    dbg = None
    if debug:
        dbg = (nc.dram_tensor("dbg_c", [8, TO], F32, kind="ExternalOutput").ap(),
               nc.dram_tensor("dbg_g", [128, 32], F32, kind="ExternalOutput").ap(),
               nc.dram_tensor("dbg_t", [128, 8 * TO], BF16, kind="ExternalOutput").ap(),
               nc.dram_tensor("dbg_gs", [8, TO], F32, kind="ExternalOutput").ap(),
               nc.dram_tensor("dbg_gw", [128, 64], F32, kind="ExternalOutput").ap())
    with ExitStack() as ctx0:
        tk = Tracker(nc, ctx0)
        with nc.Block() as block:
            PHASE[0] = "l1m"
            emit_l1_moe(nc, tk, x1T, qr, kvp2, omg, outw, outb, nw, gw, w1, b1,
                        w2, b2, xo, dbg=dbg)
            tk.flush(block)
    return nc

def _f32c(a):
    return np.ascontiguousarray(a, dtype=np.float32)


def stage_l0_attn(inputs):
    x = inputs["L0_attn_x"]
    W = inputs["L0_attn_w_attn"]
    ba = inputs["L0_attn_b_attn"]
    Wk2 = inputs["L0_attn_w_k2"]
    bk2 = inputs["L0_attn_b_k2"]
    in_maps = []
    for c in range(NCORES):
        b, hg = c // 2, c % 2
        mine = slice(512 * hg, 512 * hg + 512)
        other = slice(512 * (1 - hg), 512 * (1 - hg) + 512)
        perm = np.r_[np.arange(512 * hg, 512 * hg + 512),
                     np.arange(512 * (1 - hg), 512 * (1 - hg) + 512)]
        xTc = _f32c(x[b].T[perm])
        qs = W[512 * hg: 512 * hg + 512][:, perm]
        ks = W[1024 + 512 * hg: 1024 + 512 * hg + 512][:, perm]
        Wc = np.concatenate([qs, ks], 0)
        wqk_h = _f32c(Wc.T.reshape(8, 128, 1024))
        wk2_h = _f32c(Wk2[512 * hg: 512 * hg + 512][:, perm].T.reshape(8, 128, 512))
        bqk_h = _f32c(np.concatenate([ba[512 * hg: 512 * hg + 512],
                                      ba[1024 + 512 * hg: 1024 + 512 * hg + 512]])
                      .reshape(8, 128).T)
        bk2_h = _f32c(bk2[512 * hg: 512 * hg + 512].reshape(4, 128).T)
        in_maps.append({
            "xT": xTc, "wqk": wqk_h, "wk2": wk2_h, "bqk": bqk_h, "bk2": bk2_h,
            "epsc": np.full((1, 1), 1e-5, np.float32),
        })
    return in_maps


def run_l0_attn(inputs, core_ids=None, debug=False):
    nc = build_l0_attn(debug=debug)
    in_maps = stage_l0_attn(inputs)
    ids = core_ids if core_ids is not None else list(range(NCORES))
    res = run_bass_kernel_spmd(nc, [in_maps[i] for i in ids], core_ids=ids)
    if debug:
        return res.results
    return [r["y"] for r in res.results]


def stage_l1_moe(inputs, x1_by_core, qr_by_core, kvp_by_core):
    outw_h = _f32c(inputs["L1_attn_out_w"].reshape(8, 128, 1024))
    outb_h = _f32c(inputs["L1_attn_out_b"].reshape(8, 128).T)
    nw_h = _f32c(inputs["L1_ffn_norm_weight"].reshape(8, 128).T)
    gw_h = _f32c(inputs["L1_ffn_gate_w"].T.reshape(8, 128, 8))
    w1_h = _f32c(inputs["L1_ffn_w1"].reshape(8, 16, 128, 8, 128).transpose(0, 1, 4, 3, 2))
    b1_h = _f32c(inputs["L1_ffn_b1"].reshape(8, 16, 128).transpose(2, 0, 1).reshape(128, 128))
    w2_h = _f32c(inputs["L1_ffn_w2"].reshape(8, 8, 128, 16, 128).transpose(0, 1, 4, 3, 2))
    b2_h = _f32c(inputs["L1_ffn_b2"])
    omg_h = _f32c(np.concatenate([inputs["L1_attn_omega"]] * 2, 0))
    in_maps = []
    for c in range(NCORES):
        b, half = c // 2, c % 2
        if c not in x1_by_core:
            in_maps.append(None)
            continue
        partner = 2 * b + (1 - half)
        kvp2 = _f32c(np.concatenate([kvp_by_core[c], kvp_by_core[partner]], 1))
        in_maps.append({
            "x1T": _f32c(x1_by_core[c]), "qr": np.ascontiguousarray(qr_by_core[c]),
            "kvp2": kvp2, "omg": omg_h, "outw": outw_h, "outb": outb_h,
            "nw": nw_h, "gw": gw_h, "w1": w1_h, "b1": b1_h, "w2": w2_h, "b2": b2_h,
        })
    return in_maps


def run_l1_moe(inputs, x1_by_core, qr_by_core, kvp_by_core, core_ids=None):
    nc = build_l1_moe()
    in_maps = stage_l1_moe(inputs, x1_by_core, qr_by_core, kvp_by_core)
    ids = core_ids if core_ids is not None else list(range(NCORES))
    res = run_bass_kernel_spmd(nc, [in_maps[i] for i in ids], core_ids=ids)
    return [r["xo"] for r in res.results]



def build_fused(nc8=8, debug=False):
    nc = bass.Bass("TRN2", target_bir_lowering=False)
    TO = 2048
    TE = TO + 2
    a_xT = nc.dram_tensor("a_xT", [D, T], F32, kind="ExternalInput").ap()
    a_wqk = nc.dram_tensor("a_wqk", [8, 128, 1024], F32, kind="ExternalInput").ap()
    a_wk2 = nc.dram_tensor("a_wk2", [8, 128, 512], F32, kind="ExternalInput").ap()
    a_bqk = nc.dram_tensor("a_bqk", [128, 8], F32, kind="ExternalInput").ap()
    a_bk2 = nc.dram_tensor("a_bk2", [128, 4], F32, kind="ExternalInput").ap()
    b_xT = nc.dram_tensor("b_xT", [D, TO], F32, kind="ExternalInput").ap()
    b_wproj = nc.dram_tensor("b_wproj", [8, 128, 1024], F32, kind="ExternalInput").ap()
    b_bproj = nc.dram_tensor("b_bproj", [128, 8], F32, kind="ExternalInput").ap()
    b_w1 = nc.dram_tensor("b_w1", [32, 128, 8, 128], F32, kind="ExternalInput").ap()
    b_w2 = nc.dram_tensor("b_w2", [32, 128, 8, 128], F32, kind="ExternalInput").ap()
    b_h1 = nc.dram_tensor("b_h1", [32, 128, 32, 128], F32, kind="ExternalInput").ap()
    b_h2 = nc.dram_tensor("b_h2", [32, 128, 32, 128], F32, kind="ExternalInput").ap()
    b_w3 = nc.dram_tensor("b_w3", [8, 128, 32, 128], F32, kind="ExternalInput").ap()
    b_s1 = nc.dram_tensor("b_s1", [128, 32], F32, kind="ExternalInput").ap()
    b_s2 = nc.dram_tensor("b_s2", [128, 32], F32, kind="ExternalInput").ap()
    b_sh1 = nc.dram_tensor("b_sh1", [128, 32], F32, kind="ExternalInput").ap()
    b_sh2 = nc.dram_tensor("b_sh2", [128, 32], F32, kind="ExternalInput").ap()
    b_s3 = nc.dram_tensor("b_s3", [128, 8], F32, kind="ExternalInput").ap()
    c_ck = nc.dram_tensor("c_ck", [3, 3, 4, 2, 128, 256], F32, kind="ExternalInput").ap()
    c_cb = nc.dram_tensor("c_cb", [128, 24], F32, kind="ExternalInput").ap()
    c_rw = nc.dram_tensor("c_rw", [3, 3, 128, 8, 256], F32, kind="ExternalInput").ap()
    c_rb = nc.dram_tensor("c_rb", [128, 18], F32, kind="ExternalInput").ap()
    c_pw = nc.dram_tensor("c_pw", [3, 128, 6, 1024], F32, kind="ExternalInput").ap()
    c_pb = nc.dram_tensor("c_pb", [128, 24], F32, kind="ExternalInput").ap()
    c_omg = nc.dram_tensor("c_omg", [128, 256], F32, kind="ExternalInput").ap()
    c_lmt = nc.dram_tensor("c_lmt", [128, 64], F32, kind="ExternalInput").ap()
    c_sint = nc.dram_tensor("c_sint", [128, TO], F32, kind="ExternalInput").ap()
    c_cost = nc.dram_tensor("c_cost", [128, TO], F32, kind="ExternalInput").ap()
    c_rotm = nc.dram_tensor("c_rotm", [128, 128], F32, kind="ExternalInput").ap()
    d_outw = nc.dram_tensor("d_outw", [8, 128, 1024], F32, kind="ExternalInput").ap()
    d_outb = nc.dram_tensor("d_outb", [128, 8], F32, kind="ExternalInput").ap()
    d_nw = nc.dram_tensor("d_nw", [128, 8], F32, kind="ExternalInput").ap()
    d_gw = nc.dram_tensor("d_gw", [8, 128, 8], F32, kind="ExternalInput").ap()
    d_w1 = nc.dram_tensor("d_w1", [8, 16, 128, 8, 128], F32, kind="ExternalInput").ap()
    d_b1 = nc.dram_tensor("d_b1", [128, 128], F32, kind="ExternalInput").ap()
    d_w2 = nc.dram_tensor("d_w2", [8, 8, 128, 16, 128], F32, kind="ExternalInput").ap()
    d_b2 = nc.dram_tensor("d_b2", [8, 1024], F32, kind="ExternalInput").ap()
    epsc = nc.dram_tensor("epsc", [1, 1], F32, kind="ExternalInput").ap()
    hmask = nc.dram_tensor("hmask", [128, 2], F32, kind="ExternalInput").ap()
    tok = nc.dram_tensor("tok", [1, 16], mybir.dt.uint32, kind="ExternalInput").ap()
    xo = nc.dram_tensor("xo", [D, TO], F32, kind="ExternalOutput").ap()
    if debug:
        dbg_ymine = nc.dram_tensor("dbg_ymine", [D, TO], BF16, kind="ExternalOutput").ap()
        dbg_yloc = nc.dram_tensor("dbg_yloc", [512, T], BF16, kind="ExternalOutput").ap()
        dbg_x1e = nc.dram_tensor("dbg_x1e", [D, TO + 2], F32, kind="ExternalOutput").ap()
        dbg_qr = nc.dram_tensor("dbg_qr", [D, TO], BF16, kind="ExternalOutput").ap()
        dbg_kv = nc.dram_tensor("dbg_kv", [128, 2080], F32, kind="ExternalOutput").ap()
        dbg_wpx = tuple(nc.dram_tensor("dbg_wpx%d" % i, [128, 8192], BF16,
                                       kind="ExternalOutput").ap() for i in range(3))
        dbg2 = (nc.dram_tensor("dbg_attn", [D, TO], F32, kind="ExternalOutput").ap(),
                nc.dram_tensor("dbg_ffn", [D, TO], F32, kind="ExternalOutput").ap(),
                nc.dram_tensor("dbg_wp", [128, 8192], BF16, kind="ExternalOutput").ap(),
                nc.dram_tensor("dbg_yt", [128, 8192], BF16, kind="ExternalOutput").ap(),
                nc.dram_tensor("dbg_raw", [128, 512], F32, kind="ExternalOutput").ap())
    y_loc = nc.dram_tensor("y_loc", [512, T], BF16, kind="Internal").ap()
    y_all = nc.dram_tensor("y_all", [nc8 * 512, T], BF16, kind="Internal",
                           addr_space="Shared").ap()
    y_mine = nc.dram_tensor("y_mine", [D, 2048], BF16, kind="Internal").ap()
    x1e_loc = nc.dram_tensor("x1e_loc", [D, TE], F32, kind="Internal").ap()
    E_all = nc.dram_tensor("E_all", [nc8 * 128, 16], F32, kind="Internal",
                           addr_space="Shared").ap()
    qr_loc = nc.dram_tensor("qr_loc", [D, TO], BF16, kind="Internal").ap()
    kvp_loc = nc.dram_tensor("kvp_loc", [128, 2080], F32, kind="Internal").ap()
    kvp_all = nc.dram_tensor("kvp_all", [nc8 * 128, 2080], F32, kind="Internal",
                             addr_space="Shared").ap()
    flags = nc.dram_tensor("flags", [nc8, 64], mybir.dt.uint32, kind="Internal",
                           addr_space="Shared").ap()
    _exprs = {}

    def pid_exprs(e):
        k = id(e)
        if k not in _exprs:
            pid = e.partition_id()
            half = pid % 2
            _exprs[k] = {
                "rowoff": (pid // 2) * 1024,
                "coloff": half * 2048,
                "prt128": (pid // 2) * 256 + 128 - half * 128,
                "pid128": pid * 128,
                "pid512": pid * 512,
                "prtrow": (pid // 2) * 2 + 1 - half,
                "pidrow": pid,
                "ypub": {c: pid * 512 + c * 128 for c in range(4)},
                "ymine": {c: (pid // 2) * 1024 + c * 128 for c in range(8)},
            }
        return _exprs[k]

    with ExitStack() as ctx0:
        tk = Tracker(nc, ctx0)
        poll_sem = ctx0.enter_context(nc.semaphore("poll_sem"))
        tok_sb = ctx0.enter_context(nc.sbuf_tensor("tok_sb", [1, 16], mybir.dt.uint32))
        fp_sb = ctx0.enter_context(nc.sbuf_tensor("fp_sb", [1, 16], mybir.dt.uint32))

        def publish_flag(bk, datakey):
            def f(e):
                ex = pid_exprs(e)
                return e.dma_start(out=flags[bass.ds(ex["pidrow"], 1),
                                             16 * bk:16 * bk + 16],
                                   in_=tok_sb[0:1, :])
            tk.emit("sync", f, reads=("tok", datakey), writes=("flag%d" % bk,), dma=True)

        def poll_partner(bk):
            import concourse.expressions as _ex

            def f(e):
                ex = pid_exprs(e)
                with e.register() as rf, e.register() as rt:
                    e.reg_load(rt, tok_sb[0:1, 0:1])
                    e.reg_mov(rf, 0)
                    with e.While(lambda: _ex.RuntimeValue(val=rf) !=
                                 _ex.RuntimeValue(val=rt)):
                        e.dma_start(out=fp_sb[0:1, :],
                                    in_=flags[bass.ds(ex["prtrow"], 1),
                                              16 * bk:16 * bk + 16]).then_inc(poll_sem, 16)
                        e.wait_ge(poll_sem, 16)
                        e.sem_clear(poll_sem)
                        e.reg_load(rf, fp_sb[0:1, 0:1])
                return e.nop()
            tk.emit("sync", f, reads=("tok", "flag%d" % bk), writes=("barrier%d" % bk,))

        with nc.Block() as block:
            tk.dma("sync", tok_sb[:], tok, reads=(), writes=("tok",))
            PHASE[0] = "l0a"
            emit_l0_attn(nc, tk, a_xT, a_wqk, a_wk2, a_bqk, a_bk2, epsc, y_loc)
            tk.fence()

            PHASE[0] = "xch1"
            with ExitStack() as xctx:
                xch = xctx.enter_context(
                    nc.sbuf_tensor(_un("xch"), [128, 4096], BF16))
                tk.reuse("xch", "yh")
                for cb4 in range(4):
                    tk.dma("sync", xch[:], y_loc[cb4 * 128:(cb4 + 1) * 128, :],
                           reads=(), writes=("xch",))

                    def f_ypub(e, cb4=cb4):
                        ex = pid_exprs(e)
                        return e.dma_start(
                            out=y_all[bass.ds(ex["ypub"][cb4], 128), :],
                            in_=xch[:])
                    tk.emit("scalar", f_ypub, reads=("xch",), writes=("y_all",),
                            dma=True)
            publish_flag(0, "y_all")
            poll_partner(0)
            tk.fence()

            def f_ymine(e):
                ex = pid_exprs(e)
                return e.dma_start(out=y_mine[:, :],
                                   in_=y_all[bass.ds(ex["rowoff"], 1024),
                                             bass.ds(ex["coloff"], 2048)])
            tk.emit("scalar", f_ymine, reads=("y_all", "barrier0"), writes=("y_mine",), dma=True)
            tk.fence()

            PHASE[0] = "l0f"
            emit_l0_ffn(nc, tk, b_xT, y_mine, b_wproj, b_bproj, b_w1, b_w2, b_h1,
                        b_h2, b_w3, b_s1, b_s2, b_sh1, b_sh2, b_s3, epsc,
                        x1e_loc[:, 1:TE - 1], dbg=(dbg2 if debug else None),
                        dbg_wpx=(dbg_wpx if debug else None))
            tk.fence()

            if debug:
                tk.dma("sync", dbg_yloc, y_loc, reads=("barrier0",), writes=())
                tk.dma("sync", dbg_ymine, y_mine, reads=("y_mine",), writes=())
            PHASE[0] = "halo"
            with ExitStack() as hctx:
                e_scr = hctx.enter_context(nc.sbuf_tensor(_un("e_scr"), [128, 16], F32))
                ep_scr = hctx.enter_context(nc.sbuf_tensor(_un("ep_scr"), [128, 16], F32))
                hm_sb = hctx.enter_context(nc.sbuf_tensor(_un("hm_sb"), [128, 2], F32))
                hv_sb = hctx.enter_context(nc.sbuf_tensor(_un("hv_sb"), [128, 2], F32))
                tk.dma("sync", hm_sb[:], hmask, reads=(), writes=("hm",))
                for kc in range(8):
                    for (dcol, scol) in ((kc * 2, 1), (kc * 2 + 1, TE - 2)):
                        tk.emit("sync",
                                lambda e, dcol=dcol, scol=scol, kc=kc: e.dma_start(
                                    out=e_scr[:, dcol:dcol + 1],
                                    in_=x1e_loc[kc * 128:(kc + 1) * 128, scol:scol + 1],
                                    allow_slow_non_contiguous=True),
                                reads=(), writes=("e_scr",), dma=True)
                def f_epub(e):
                    ex = pid_exprs(e)
                    return e.dma_start(out=E_all[bass.ds(ex["pid128"], 128), :],
                                       in_=e_scr[:])
                tk.emit("scalar", f_epub, reads=("e_scr",), writes=("E_all",), dma=True)
                publish_flag(1, "E_all")
                poll_partner(1)

                def f_ep(e):
                    ex = pid_exprs(e)
                    return e.dma_start(out=ep_scr[:], in_=E_all[bass.ds(ex["prt128"], 128), :])
                tk.emit("scalar", f_ep, reads=("E_all", "barrier1"), writes=("ep",), dma=True)
                for kc in range(8):
                    tk.tt(hv_sb[:, 0:1], ep_scr[:, kc * 2 + 1: kc * 2 + 2],
                          hm_sb[:, 0:1], ALU.mult, reads=("ep", "hm"), writes=("hv",))
                    tk.tt(hv_sb[:, 1:2], ep_scr[:, kc * 2: kc * 2 + 1],
                          hm_sb[:, 1:2], ALU.mult, reads=("ep", "hm"), writes=("hv",))
                    for (scol, dcol) in ((0, 0), (1, TE - 1)):
                        tk.emit("sync",
                                lambda e, scol=scol, dcol=dcol, kc=kc: e.dma_start(
                                    out=x1e_loc[kc * 128:(kc + 1) * 128, dcol:dcol + 1],
                                    in_=hv_sb[:, scol:scol + 1],
                                    allow_slow_non_contiguous=True),
                                reads=("hv",), writes=("x1eh",), dma=True)
            tk.fence()

            if debug:
                tk.dma("sync", dbg_x1e, x1e_loc, reads=("x1eh",), writes=())
            PHASE[0] = "l1a"
            emit_l1_attn(nc, tk, x1e_loc, c_ck, c_cb, c_rw, c_rb, c_pw, c_pb,
                         c_omg, c_lmt, c_sint, c_cost, c_rotm, epsc,
                         qr_loc, kvp_loc)
            tk.fence()

            PHASE[0] = "xch3"
            with ExitStack() as xctx:
                xch3 = xctx.enter_context(
                    nc.sbuf_tensor(_un("xch3"), [128, 2080], F32))
                tk.reuse("xch3", "kvs")
                tk.dma("sync", xch3[:], kvp_loc, reads=(), writes=("xch3",))

                def f_kpub(e):
                    ex = pid_exprs(e)
                    return e.dma_start(out=kvp_all[bass.ds(ex["pid128"], 128), :],
                                       in_=xch3[:])
                tk.emit("scalar", f_kpub, reads=("xch3",), writes=("kvp_all",), dma=True)
            publish_flag(2, "kvp_all")
            poll_partner(2)
            tk.fence()

            if debug:
                tk.dma("sync", dbg_qr, qr_loc, reads=("barrier2",), writes=())
                tk.dma("sync", dbg_kv, kvp_loc, reads=("barrier2",), writes=())

            def kvload(tk_, kvf):
                def f1(e):
                    ex = pid_exprs(e)
                    return e.dma_start(out=kvf[:], in_=kvp_all[bass.ds(ex["pid128"], 128), :])
                def f2(e):
                    ex = pid_exprs(e)
                    return e.dma_start(out=kvf[:], in_=kvp_all[bass.ds(ex["prt128"], 128), :],
                                       accum_op=ALU.add)
                tk_.emit("gpsimd", f1, reads=("kvp_all", "barrier2"), writes=("kvf",), dma=True)
                tk_.emit("gpsimd", f2, reads=("kvp_all", "barrier2", "kvf"), writes=("kvf",), dma=True)

            PHASE[0] = "l1m"
            emit_l1_moe(nc, tk, x1e_loc[:, 1:TE - 1], qr_loc, None, c_omg, d_outw,
                        d_outb, d_nw, d_gw, d_w1, d_b1, d_w2, d_b2, xo,
                        kvload=kvload)
            tk.flush(block)
    return nc


def stage_fused(inputs):
    dummy_y = {c: np.zeros((512, T), bfloat16) for c in range(NCORES)}
    dummy_x1 = {c: np.zeros((D, 2048), np.float32) for c in range(NCORES)}
    dummy_qr = {c: np.zeros((D, 2048), bfloat16) for c in range(NCORES)}
    dummy_kv = {c: np.zeros((128, 2080), np.float32) for c in range(NCORES)}
    m1 = stage_l0_attn(inputs)
    m2 = stage_l0_ffn(inputs, dummy_y)
    m3 = stage_l1_attn(inputs, dummy_x1)
    m4 = stage_l1_moe(inputs, dummy_x1, dummy_qr, dummy_kv)
    in_maps = []
    for c in range(NCORES):
        half = c % 2
        im = {}
        for k, v in m1[c].items():
            if k != "epsc":
                im["a_" + k] = v
        for k, v in m2[c].items():
            if k not in ("epsc", "yb"):
                im["b_" + k] = v
        for k, v in m3[c].items():
            if k not in ("epsc", "x1e"):
                im["c_" + k] = v
        for k, v in m4[c].items():
            if k not in ("x1T", "qr", "kvp2", "omg"):
                im["d_" + k] = v
        im["epsc"] = np.full((1, 1), 1e-5, np.float32)
        maskL = 1.0 if half == 1 else 0.0
        maskR = 1.0 if half == 0 else 0.0
        im["hmask"] = _f32c(np.tile(np.array([[maskL, maskR]], np.float32), (128, 1)))
        in_maps.append(im)
    nonce = np.uint32(np.random.default_rng().integers(1, 2 ** 31))
    for im in in_maps:
        im["tok"] = np.full((1, 16), nonce, np.uint32)
    return in_maps


def kernel(**inputs):
    nc = build_fused()
    in_maps = stage_fused(inputs)
    res = run_bass_kernel_spmd(nc, in_maps, core_ids=list(range(NCORES)))
    out = np.empty((B, T, D), np.float32)
    for c in range(NCORES):
        b, half = c // 2, c % 2
        out[b, half * 2048:(half + 1) * 2048] = res.results[c]["xo"].T
    return out


def kernel_unfused(**inputs):

    ys = run_l0_attn(inputs)
    y_map = {c: ys[c] for c in range(NCORES)}
    x1s = run_l0_ffn(inputs, y_map)
    x1_map = {c: x1s[c] for c in range(NCORES)}
    l3 = run_l1_attn(inputs, x1_map)
    qr_map = {c: l3[c][0] for c in range(NCORES)}
    kvp_map = {c: l3[c][1] for c in range(NCORES)}
    outs = run_l1_moe(inputs, x1_map, qr_map, kvp_map)
    out = np.empty((B, T, D), np.float32)
    for c in range(NCORES):
        b, half = c // 2, c % 2
        out[b, half * 2048:(half + 1) * 2048] = outs[c].T
    return out
